# version 02
# baseline (speedup 1.0000x reference)
import math
import numpy as np
import ml_dtypes
from contextlib import ExitStack

import concourse.bass as bass
import concourse.mybir as mybir
from concourse.bass_utils import run_bass_kernel_spmd

F32 = mybir.dt.float32
BF16 = mybir.dt.bfloat16
F16 = mybir.dt.float16
ALU = mybir.AluOpType
AF = mybir.ActivationFunctionType

BF = ml_dtypes.bfloat16

T = 2048
D = 1024
DI = 2048
RNE_C = 12582912.0

def _pwl_constants():
    xs = [0.25 * k for k in range(17)]
    ev = [0.5 * x * (1.0 + math.erf(x / math.sqrt(2.0))) - x / 2.0 for x in xs]
    s = [(ev[k + 1] - ev[k]) / 0.25 for k in range(16)]
    knots = []
    for k in range(1, 16):
        knots.append((64.0 * k, s[k] - s[k - 1]))
    knots.append((1024.0, 0.5 - s[15]))
    j256 = 256.0 * (4.0 - (0.5 * 4.0 * (1.0 + math.erf(4.0 / math.sqrt(2.0)))))
    return s[0], knots, j256


PWL_S0, PWL_KNOTS, PWL_J256 = _pwl_constants()


class Ctx:
    ENGINES = ("sync", "vector", "scalar", "tensor", "gpsimd")

    def __init__(self, nc, stack):
        self.nc = nc
        self.stack = stack
        self.ops = {e: [] for e in self.ENGINES}
        self.sems = {}
        self.cnt = {}

    def sem(self, name):
        if name not in self.sems:
            self.sems[name] = self.stack.enter_context(self.nc.semaphore(name))
            self.cnt[name] = 0
        return self.sems[name]

    def fire(self, name, amount):
        h = self.sem(name)
        self.cnt[name] += amount
        return (h, amount)

    def val(self, name):
        return self.cnt.get(name, 0)

    def emit(self, engine, fn, waits=(), fire=None):
        wl = [(self.sem(n), v) for (n, v) in waits if v > 0]

        def runner(eng):
            for h, v in wl:
                eng.wait_ge(h, v)
            ins = fn(eng)
            if fire is not None:
                ins.then_inc(fire[0], fire[1])
        self.ops[engine].append(runner)

    def dma(self, queue, out, in_, waits=(), sem="dma", transpose=False):
        f = self.fire(sem, 16)
        if transpose:
            self.emit(queue, lambda e: e.dma_start_transpose(out=out, in_=in_),
                      waits, f)
        else:
            self.emit(queue, lambda e: e.dma_start(out=out, in_=in_), waits, f)
        return self.cnt[sem]

    def sbuf(self, name, shape, dt):
        return self.stack.enter_context(self.nc.sbuf_tensor(name, shape, dt))

    def psum(self, name, shape, dt=F32):
        return self.stack.enter_context(self.nc.psum_tensor(name, shape, dt))

    def finalize(self, final_waits):
        ops = self.ops
        wl = [(self.sems[n], v) for (n, v) in final_waits]
        with self.nc.Block() as block:
            @block.sync
            def _(eng):
                for f in ops["sync"]:
                    f(eng)
                for h, v in wl:
                    eng.wait_ge(h, v)

            @block.vector
            def _(eng):
                for f in ops["vector"]:
                    f(eng)

            @block.scalar
            def _(eng):
                for f in ops["scalar"]:
                    f(eng)

            @block.tensor
            def _(eng):
                for f in ops["tensor"]:
                    f(eng)

            @block.gpsimd
            def _(eng):
                for f in ops["gpsimd"]:
                    f(eng)


def new_prog():
    nc = bass.Bass("TRN2", target_bir_lowering=False)
    stack = ExitStack()
    ctx = Ctx(nc, stack)
    return nc, stack, ctx


def din(nc, name, shape, dt=F32):
    return nc.dram_tensor(name, list(shape), dt, kind="ExternalInput").ap()


def dout(nc, name, shape, dt=F32):
    return nc.dram_tensor(name, list(shape), dt, kind="ExternalOutput").ap()



def _seq(ctx, engine, fn):
    v = ctx.val("q")
    ctx.emit(engine, fn, waits=(("q", v),), fire=ctx.fire("q", 1))


def _seqdma(ctx, queue, out, in_, transpose=False, slow=False):
    v = ctx.val("q")
    f = ctx.fire("q", 16)
    if transpose:
        ctx.emit(queue, lambda e: e.dma_start_transpose(out=out, in_=in_),
                 waits=(("q", v),), fire=f)
    elif slow:
        ctx.emit(queue, lambda e: e.dma_start(out=out, in_=in_,
                                              allow_slow_non_contiguous=True),
                 waits=(("q", v),), fire=f)
    else:
        ctx.emit(queue, lambda e: e.dma_start(out=out, in_=in_),
                 waits=(("q", v),), fire=f)


class Seq:

    def __init__(self, ctx):
        self.ctx = ctx

    def op(self, engine, fn):
        _seq(self.ctx, engine, fn)

    def dma(self, out, in_, queue="sync", transpose=False, slow=False):
        _seqdma(self.ctx, queue, out, in_, transpose, slow)

    def done(self):
        self.ctx.finalize([("q", self.ctx.val("q"))])


class ConstCols:

    def __init__(self, S, values):
        self.tile = S.ctx.sbuf("cstc", [128, max(1, len(values))], F32)
        self.idx = {}
        for i, v in enumerate(values):
            self.idx[v] = i
            S.op("vector", lambda e, i=i, v=v: e.memset(self.tile[:, i:i + 1], v))

    def ap(self, v):
        i = self.idx[v]
        return self.tile[:, i:i + 1]


CASTS = [
    ("win", "win_bf", D, 4096),
    ("wout", "wout_bf", DI, D),
    ("w1", "w1_bf", D, 4096),
    ("w2", "w2_bf", 4096, D),
    ("wxp", "wxp_bf", DI, 192),
    ("f1t", "f1t_bf", D, 4096),
    ("f2t", "f2t_bf", D, 4096),
    ("h1t", "h1t_bf", 4096, 4096),
    ("h2t", "h2t_bf", 4096, 4096),
    ("w3t", "w3t_bf", 4096, D),
]


def build_prep():
    nc, stack, ctx = new_prog()
    with stack:
        S = Seq(ctx)
        tin = ctx.sbuf("tin", [128, 4096], F32)
        tout = ctx.sbuf("tout", [128, 4096], BF16)
        for (iname, oname, rows, cols) in CASTS:
            src = din(nc, iname, (rows, cols), F32)
            dst = dout(nc, oname, (rows, cols), BF16)
            for i in range(rows // 128):
                S.dma(tin[:, :cols], src[i * 128:(i + 1) * 128, :])
                S.op("vector", lambda e, c=cols: e.tensor_copy(
                    out=tout[:, :c], in_=tin[:, :c]))
                S.dma(dst[i * 128:(i + 1) * 128, :], tout[:, :cols],
                      queue="gpsimd")
        S.done()
    return nc



def build_lxa(layer):
    nc, stack, ctx = new_prog()
    S = Seq(ctx)
    KD = D // 128
    H = T // 2
    NST = 64 if layer == 0 else 16
    MX = 192 if layer == 0 else 96

    x = din(nc, "x", (D, T), F32)
    win = din(nc, "win_bf", (D, 4096), BF16)
    convw = din(nc, "convw", (DI, 4), F32 if layer == 0 else BF16)
    convb = din(nc, "convb", (DI,), F32 if layer == 0 else BF16)
    wxp = din(nc, "wxp_bf", (DI, MX), BF16)
    wdt = din(nc, "wdt", (64, DI), F32 if layer == 0 else BF16)
    bdt = din(nc, "bdt", (DI,), F32 if layer == 0 else BF16)
    if layer == 0:
        lnw = din(nc, "lnw", (D,), F32)
        lnb = din(nc, "lnb", (D,), F32)
        xn_f = dout(nc, "xn_f", (D, T), F32)
    xn_bf = dout(nc, "xn_bf", (D, T), BF16)
    xi_bf = dout(nc, "xi_bf", (DI, T), BF16)
    z_bf = dout(nc, "z_bf", (DI, T), BF16)
    u_f = dout(nc, "u_f", (DI, T), F32)
    u_bf = dout(nc, "u_bf", (DI, T), BF16)
    B0 = dout(nc, "B0", (NST, T), F32)
    C0 = dout(nc, "C0", (NST, T), F32)
    delta = dout(nc, "delta", (DI, T), F32)
    du = dout(nc, "du", (DI, T), F32)

    with stack:
        xt = ctx.sbuf("xt", [128, H], F32)
        sq = ctx.sbuf("sq", [128, H], F32)
        ones_col = ctx.sbuf("ones_col", [128, 1], F32)
        ones_row = ctx.sbuf("ones_row", [1, 128], F32)
        rows = ctx.sbuf("rows", [1, 4, T], F32)
        rep_a = ctx.sbuf("rep_a", [128, T], F32)
        rep_b = ctx.sbuf("rep_b", [128, T], F32)
        xbf = ctx.sbuf("xbf", [128, H], BF16)
        xn2 = ctx.sbuf("xn2", [128, KD, T], BF16)
        wm = ctx.sbuf("wm", [128, KD, 128], BF16)
        ebf = ctx.sbuf("ebf", [128, H], BF16)
        ct = ctx.sbuf("ct", [128, T + 3], BF16)
        cwr = ctx.sbuf("cwr", [128, 16, 4], F32 if layer == 0 else BF16)
        cbr = ctx.sbuf("cbr", [128, 16], F32 if layer == 0 else BF16)
        cw = ctx.sbuf("cw", [128, 16, 4], F32)
        cb = ctx.sbuf("cb", [128, 16], F32)
        ua = ctx.sbuf("ua", [128, T], F32)
        ub = ctx.sbuf("ub", [128, T], BF16)
        wxpt = ctx.sbuf("wxpt", [128, 16, MX], BF16)
        uh = ctx.sbuf("uh", [128, H], BF16)
        dtr = ctx.sbuf("dtr", [64, T], BF16)
        wdtf = ctx.sbuf("wdtf", [64, DI], F32 if layer == 0 else BF16)
        wdtb = ctx.sbuf("wdtb", [64, DI], BF16)
        bdtr = ctx.sbuf("bdtr", [128, 16], F32 if layer == 0 else BF16)
        bdtc = ctx.sbuf("bdtc", [128, 16], F32)
        tb = ctx.sbuf("tb", [128, 4, H], F32)
        lncol = ctx.sbuf("lncol", [128, 2, KD], F32)
        ps_big = ctx.psum("ps_big", [128, 4, 512], F32)
        ps_sm = ctx.psum("ps_sm", [64, 4, 512], F32)

        MU, SQS, RSTD, SCR = range(4)
        psb2 = ps_big[:, 0:2, :].rearrange("p a f -> p (a f)")

        S.op("vector", lambda e: e.memset(ones_col[:], 1.0))
        S.op("vector", lambda e: e.memset(ones_row[:], 1.0))
        cst = ConstCols(S, [1e-5])

        def stats_pass(src_ap, xn_mode):
            for th in range(2):
                for k in range(KD):
                    S.dma(xt[:], src_ap[k * 128:(k + 1) * 128, th * H:(th + 1) * H])
                    if xn_mode:
                        S.op("vector", lambda e, th=th: e.tensor_sub(
                            out=xt[:], in0=xt[:], in1=rep_a[:, th * H:(th + 1) * H]))
                        S.op("vector", lambda e, th=th: e.tensor_mul(
                            out=xt[:], in0=xt[:], in1=rep_b[:, th * H:(th + 1) * H]))
                        if layer == 0:
                            S.dma(xn_f[k * 128:(k + 1) * 128, th * H:(th + 1) * H],
                                  xt[:], queue="gpsimd")
                        S.op("vector", lambda e: e.tensor_copy(out=xbf[:], in_=xt[:]))
                        S.dma(xn_bf[k * 128:(k + 1) * 128, th * H:(th + 1) * H],
                              xbf[:], queue="gpsimd")
                    if (not xn_mode) or layer == 0:
                        S.op("scalar", lambda e: e.activation(
                            out=sq[:], in_=xt[:], func=AF.Square))
                        for which in range(2):
                            st = xt if which == 0 else sq
                            for fc in range(2):
                                S.op("tensor", lambda e, w=which, fc=fc, k=k, st=st:
                                     e.matmul(out=ps_sm[0:1, w * 2 + fc, :],
                                              lhsT=ones_col[:],
                                              rhs=st[:, fc * 512:(fc + 1) * 512],
                                              start=(k == 0), stop=(k == KD - 1)))
                if (not xn_mode) or layer == 0:
                    for which, slot in ((0, MU), (1, SQS)):
                        for fc in range(2):
                            S.op("scalar", lambda e, w=which, s=slot, fc=fc, th=th:
                                 e.activation(
                                     out=rows[:, s, th * H + fc * 512:
                                              th * H + (fc + 1) * 512],
                                     in_=ps_sm[0:1, w * 2 + fc, :],
                                     func=AF.Identity, scale=1.0 / D))

        def row_math():
            S.op("vector", lambda e: e.tensor_mul(
                out=rows[:, SCR, :], in0=rows[:, MU, :], in1=rows[:, MU, :]))
            S.op("vector", lambda e: e.tensor_sub(
                out=rows[:, SCR, :], in0=rows[:, SQS, :], in1=rows[:, SCR, :]))
            S.op("scalar", lambda e: e.activation(
                out=rows[:, SCR, :], in_=rows[:, SCR, :], func=AF.Sqrt,
                bias=cst.ap(1e-5)[0:1, :]))
            S.op("vector", lambda e: e.reciprocal(
                out=rows[:, RSTD, :], in_=rows[:, SCR, :]))

        def rep_pass():
            for slot, dst in ((MU, rep_a), (RSTD, rep_b)):
                for fc in range(4):
                    S.op("tensor", lambda e, s=slot, fc=fc: e.matmul(
                        out=ps_big[:, fc, :], lhsT=ones_row[:],
                        rhs=rows[:, s, fc * 512:(fc + 1) * 512],
                        start=True, stop=True))
                S.op("vector", lambda e, d=dst: e.tensor_copy(
                    out=d[:], in_=ps_big[:].rearrange("p a f -> p (a f)")))

        stats_pass(x, False)
        row_math()
        rep_pass()
        stats_pass(x, True)
        if layer == 0:
            S.dma(lncol[:, 0, :], lnw.rearrange("(k p) -> p k", p=128), slow=True)
            S.dma(lncol[:, 1, :], lnb.rearrange("(k p) -> p k", p=128), slow=True)
            row_math()
            rep_pass()
            for th in range(2):
                for k in range(KD):
                    S.dma(xt[:], xn_f[k * 128:(k + 1) * 128, th * H:(th + 1) * H])
                    S.op("vector", lambda e, th=th: e.tensor_sub(
                        out=xt[:], in0=xt[:], in1=rep_a[:, th * H:(th + 1) * H]))
                    S.op("vector", lambda e, th=th: e.tensor_mul(
                        out=xt[:], in0=xt[:], in1=rep_b[:, th * H:(th + 1) * H]))
                    S.op("scalar", lambda e, k=k, th=th: e.activation(
                        out=xn2[:, k, th * H:(th + 1) * H], in_=xt[:],
                        func=AF.Identity, scale=lncol[:, 0, k:k + 1],
                        bias=lncol[:, 1, k:k + 1]))
        else:
            for k in range(KD):
                S.dma(xn2[:, k, :], xn_bf[k * 128:(k + 1) * 128, :])

        for m in range(32):
            S.dma(wm[:], win.rearrange("(k p) m -> p k m", p=128)[:, :, m * 128:(m + 1) * 128])
            for th in range(2):
                for k in range(KD):
                    for j in range(2):
                        S.op("tensor", lambda e, k=k, th=th, j=j: e.matmul(
                            out=ps_big[:, j, :], lhsT=wm[:, k, :],
                            rhs=xn2[:, k, th * H + j * 512:th * H + (j + 1) * 512],
                            start=(k == 0), stop=(k == KD - 1)))
                S.op("vector", lambda e: e.tensor_copy(out=ebf[:], in_=psb2))
                dst = xi_bf if m < 16 else z_bf
                mm2 = m if m < 16 else m - 16
                S.dma(dst[mm2 * 128:(mm2 + 1) * 128, th * H:(th + 1) * H],
                      ebf[:], queue="gpsimd")

        S.dma(cwr[:], convw.rearrange("(g p) k -> p g k", p=128), slow=True)
        S.dma(cbr[:], convb.rearrange("(g p) -> p g", p=128), slow=True)
        S.op("vector", lambda e: e.tensor_copy(out=cw[:], in_=cwr[:]))
        S.op("vector", lambda e: e.tensor_copy(out=cb[:], in_=cbr[:]))
        S.op("vector", lambda e: e.memset(ct[:, 0:3], 0.0))
        for g in range(16):
            S.dma(ct[:, 3:T + 3], xi_bf[g * 128:(g + 1) * 128, :])
            S.op("vector", lambda e, g=g: e.tensor_scalar(
                out=ua[:], in0=ct[:, 0:T], scalar1=cw[:, g, 0:1],
                scalar2=None, op0=ALU.mult))
            for kk in range(1, 4):
                S.op("vector", lambda e, g=g, kk=kk: e.scalar_tensor_tensor(
                    out=ua[:], in0=ct[:, kk:T + kk], scalar=cw[:, g, kk:kk + 1],
                    in1=ua[:], op0=ALU.mult, op1=ALU.add))
            S.op("scalar", lambda e, g=g: e.activation(
                out=ua[:], in_=ua[:], func=AF.Silu, bias=cb[:, g:g + 1]))
            S.op("vector", lambda e: e.tensor_copy(out=ub[:], in_=ua[:]))
            if layer == 1:
                S.op("vector", lambda e: e.tensor_copy(out=ua[:], in_=ub[:]))
            S.dma(u_f[g * 128:(g + 1) * 128, :], ua[:], queue="gpsimd")
            S.dma(u_bf[g * 128:(g + 1) * 128, :], ub[:], queue="gpsimd")

        S.dma(wxpt[:], wxp.rearrange("(k p) m -> p k m", p=128))
        psc2 = ps_sm[:, 0:2, :].rearrange("p a f -> p (a f)")
        psc3 = ps_sm[:, 2:4, :].rearrange("p a f -> p (a f)")
        for th in range(2):
            for k in range(16):
                S.dma(uh[:], u_bf[k * 128:(k + 1) * 128, th * H:(th + 1) * H])
                for j in range(2):
                    uhj = uh[:, j * 512:(j + 1) * 512]
                    if MX == 192:
                        S.op("tensor", lambda e, k=k, j=j, u2=uhj: e.matmul(
                            out=ps_big[:, j, :], lhsT=wxpt[:, k, 0:128],
                            rhs=u2, start=(k == 0), stop=(k == 15)))
                        S.op("tensor", lambda e, k=k, j=j, u2=uhj: e.matmul(
                            out=ps_sm[:, j, :], lhsT=wxpt[:, k, 128:192], rhs=u2,
                            start=(k == 0), stop=(k == 15)))
                    else:
                        S.op("tensor", lambda e, k=k, j=j, u2=uhj: e.matmul(
                            out=ps_big[0:64, j, :], lhsT=wxpt[:, k, 0:64],
                            rhs=u2, start=(k == 0), stop=(k == 15)))
                        S.op("tensor", lambda e, k=k, j=j, u2=uhj: e.matmul(
                            out=ps_sm[0:16, j, :], lhsT=wxpt[:, k, 64:80], rhs=u2,
                            start=(k == 0), stop=(k == 15)))
                        S.op("tensor", lambda e, k=k, j=j, u2=uhj: e.matmul(
                            out=ps_sm[0:16, 2 + j, :], lhsT=wxpt[:, k, 80:96], rhs=u2,
                            start=(k == 0), stop=(k == 15)))
            S.op("vector", lambda e, th=th: e.tensor_copy(
                out=dtr[:, th * H:(th + 1) * H],
                in_=ps_big[0:64, 0:2, :].rearrange("p a f -> p (a f)")))
            if MX == 192:
                S.op("vector", lambda e: e.tensor_copy(
                    out=tb[64:128, 0, :], in_=psb2[64:128, :]))
                S.op("vector", lambda e: e.tensor_copy(
                    out=tb[0:64, 1, :], in_=psc2[0:64, :]))
                S.dma(B0[:, th * H:(th + 1) * H], tb[64:128, 0, :], queue="gpsimd")
                S.dma(C0[:, th * H:(th + 1) * H], tb[0:64, 1, :], queue="gpsimd")
            else:
                S.op("vector", lambda e: e.tensor_copy(
                    out=tb[0:16, 0, :], in_=psc2[0:16, :]))
                S.op("vector", lambda e: e.tensor_copy(
                    out=tb[0:16, 1, :], in_=psc3[0:16, :]))
                S.dma(B0[:, th * H:(th + 1) * H], tb[0:16, 0, :], queue="gpsimd")
                S.dma(C0[:, th * H:(th + 1) * H], tb[0:16, 1, :], queue="gpsimd")

        S.dma(wdtf[:], wdt)
        S.op("vector", lambda e: e.tensor_copy(out=wdtb[:], in_=wdtf[:]))
        S.dma(bdtr[:], bdt.rearrange("(g p) -> p g", p=128), slow=True)
        S.op("vector", lambda e: e.tensor_copy(out=bdtc[:], in_=bdtr[:]))
        for m in range(16):
            for th in range(2):
                for j in range(2):
                    S.op("tensor", lambda e, m=m, th=th, j=j: e.matmul(
                        out=ps_big[:, j, :], lhsT=wdtb[:, m * 128:(m + 1) * 128],
                        rhs=dtr[:, th * H + j * 512:th * H + (j + 1) * 512],
                        start=True, stop=True))
                S.op("vector", lambda e, m=m: e.tensor_scalar(
                    out=tb[:, 0, :], in0=psb2, scalar1=bdtc[:, m:m + 1],
                    scalar2=0.0, op0=ALU.add, op1=ALU.max))
                S.op("scalar", lambda e, m=m: e.activation(
                    out=tb[:, 1, :], in_=psb2, func=AF.Abs, bias=bdtc[:, m:m + 1]))
                S.op("scalar", lambda e: e.activation(
                    out=tb[:, 2, :], in_=tb[:, 1, :], func=AF.Exp, scale=-1.0))
                S.op("scalar", lambda e: e.activation(
                    out=tb[:, 1, :], in_=tb[:, 2, :], func=AF.Ln, bias=1.0))
                S.op("vector", lambda e: e.tensor_add(
                    out=tb[:, 2, :], in0=tb[:, 0, :], in1=tb[:, 1, :]))
                S.dma(tb[:, 3, :], u_f[m * 128:(m + 1) * 128, th * H:(th + 1) * H])
                S.op("vector", lambda e: e.tensor_mul(
                    out=tb[:, 0, :], in0=tb[:, 2, :], in1=tb[:, 3, :]))
                S.dma(delta[m * 128:(m + 1) * 128, th * H:(th + 1) * H],
                      tb[:, 2, :], queue="gpsimd")
                S.dma(du[m * 128:(m + 1) * 128, th * H:(th + 1) * H],
                      tb[:, 0, :], queue="gpsimd")
        S.done()
    return nc



def build_scan(layer, dg0=0, dg1=16):
    nc, stack, ctx = new_prog()
    S = Seq(ctx)
    NST = 64 if layer == 0 else 16
    a_dt = F32 if layer == 0 else BF16

    delta = din(nc, "delta", (DI, T), F32)
    du = din(nc, "du", (DI, T), F32)
    Bm = din(nc, "Bm", (NST, T), F32)
    Cm = din(nc, "Cm", (NST, T), F32)
    alog = din(nc, "alog", (DI, NST), a_dt)
    y = dout(nc, "y", (DI, T), F32)

    with stack:
        dl = ctx.sbuf("dl", [128, T], F32)
        dul = ctx.sbuf("dul", [128, T], F32)
        yac = ctx.sbuf("yac", [128, T], F32)
        dat = ctx.sbuf("dat", [128, T], F32)
        dbut = ctx.sbuf("dbut", [128, T], F32)
        ht = ctx.sbuf("ht", [128, T], F32)
        hct = ctx.sbuf("hct", [128, T], F32)
        brow = ctx.sbuf("brow", [1, T], F32)
        crow = ctx.sbuf("crow", [1, T], F32)
        alr = ctx.sbuf("alr", [128, NST], a_dt)
        asb = ctx.sbuf("asb", [128, NST], F32)
        ones_row = ctx.sbuf("ones_row", [1, 128], F32)
        ps_b = ctx.psum("ps_b", [128, 4, 512], F32)
        ps_c = ctx.psum("ps_c", [128, 4, 512], F32)
        psbv = ps_b[:].rearrange("p a f -> p (a f)")
        pscv = ps_c[:].rearrange("p a f -> p (a f)")

        S.op("vector", lambda e: e.memset(ones_row[:], 1.0))
        for dg in range(dg0, dg1):
            S.dma(dl[:], delta[dg * 128:(dg + 1) * 128, :])
            S.dma(dul[:], du[dg * 128:(dg + 1) * 128, :])
            S.dma(alr[:], alog[dg * 128:(dg + 1) * 128, :])
            S.op("scalar", lambda e: e.activation(out=asb[:], in_=alr[:], func=AF.Exp))
            S.op("vector", lambda e: e.tensor_single_scalar(
                out=asb[:], in_=asb[:], scalar=-1.0, op=ALU.mult))
            S.op("vector", lambda e: e.memset(yac[:], 0.0))
            for n in range(NST):
                S.dma(brow[:], Bm[n:n + 1, :])
                S.dma(crow[:], Cm[n:n + 1, :])
                for fc in range(4):
                    S.op("tensor", lambda e, fc=fc: e.matmul(
                        out=ps_b[:, fc, :], lhsT=ones_row[:],
                        rhs=brow[:, fc * 512:(fc + 1) * 512], start=True, stop=True))
                for fc in range(4):
                    S.op("tensor", lambda e, fc=fc: e.matmul(
                        out=ps_c[:, fc, :], lhsT=ones_row[:],
                        rhs=crow[:, fc * 512:(fc + 1) * 512], start=True, stop=True))
                S.op("scalar", lambda e, n=n: e.activation(
                    out=dat[:], in_=dl[:], func=AF.Exp, scale=asb[:, n:n + 1]))
                S.op("vector", lambda e: e.tensor_mul(out=dbut[:], in0=dul[:], in1=psbv))
                S.op("vector", lambda e: e.tensor_tensor_scan(
                    out=ht[:], data0=dat[:], data1=dbut[:], initial=0.0,
                    op0=ALU.mult, op1=ALU.add))
                S.op("vector", lambda e: e.tensor_mul(out=hct[:], in0=ht[:], in1=pscv))
                S.op("vector", lambda e: e.tensor_add(out=yac[:], in0=yac[:], in1=hct[:]))
            S.dma(y[dg * 128:(dg + 1) * 128, :], yac[:], queue="gpsimd")
        S.done()
    return nc



def build_l0out():
    nc, stack, ctx = new_prog()
    S = Seq(ctx)
    H = T // 2

    y = din(nc, "y", (DI, T), F32)
    u_f = din(nc, "u_f", (DI, T), F32)
    z_bf = din(nc, "z_bf", (DI, T), BF16)
    d0 = din(nc, "d0", (DI,), F32)
    wout = din(nc, "wout_bf", (DI, D), BF16)
    xn_f = din(nc, "xn_f", (D, T), F32)
    xn_bf = din(nc, "xn_bf", (D, T), BF16)
    w1 = din(nc, "w1_bf", (D, 4096), BF16)
    b1 = din(nc, "b1", (4096,), F32)
    w2 = din(nc, "w2_bf", (4096, D), BF16)
    b2 = din(nc, "b2", (D,), F32)
    x = din(nc, "x", (D, T), F32)
    attn0 = dout(nc, "attn0", (D, T), F32)
    h_bf = dout(nc, "h_bf", (4096, T), BF16)
    x1 = dout(nc, "x1", (D, T), F32)

    KD = D // 128
    with stack:
        gbuf = ctx.sbuf("gbuf", [128, 16, T], BF16)
        xnb = ctx.sbuf("xnb", [128, KD, T], BF16)
        t0 = ctx.sbuf("t0", [128, T], F32)
        t1 = ctx.sbuf("t1", [128, T], F32)
        dcol = ctx.sbuf("dcol", [128, 16], F32)
        zt = ctx.sbuf("zt", [128, T], BF16)
        wom = ctx.sbuf("wom", [128, 16, 128], BF16)
        w1m = ctx.sbuf("w1m", [128, KD, 128], BF16)
        b1c = ctx.sbuf("b1c", [128, 32], F32)
        b2c = ctx.sbuf("b2c", [128, 8], F32)
        tb = ctx.sbuf("tb", [128, 4, H], F32)
        hb = ctx.sbuf("hb", [128, H], BF16)
        hq = ctx.sbuf("hq", [128, 32, 512], BF16)
        w2m = ctx.sbuf("w2m", [128, 32, 128], BF16)
        knot = ctx.sbuf("knot", [128, 18], F32)
        ps_big = ctx.psum("ps_big", [128, 4, 512], F32)
        psb2 = ps_big[:, 0:2, :].rearrange("p a f -> p (a f)")
        psq = ps_big[:, 0:1, :].rearrange("p a f -> p (a f)")

        sgn = []
        for i, (vk, ck) in enumerate(PWL_KNOTS):
            S.op("vector", lambda e, i=i, vk=vk, ck=ck: e.memset(
                knot[:, i:i + 1], -abs(ck) * vk))
            sgn.append(1.0 if ck >= 0 else -1.0)
        S.op("vector", lambda e: e.memset(knot[:, 16:17], -1024.0))
        S.dma(dcol[:], d0.rearrange("(g p) -> p g", p=128), slow=True)
        S.dma(b1c[:], b1.rearrange("(g p) -> p g", p=128), slow=True)
        S.dma(b2c[:], b2.rearrange("(g p) -> p g", p=128), slow=True)
        S.op("vector", lambda e: e.tensor_single_scalar(
            out=b2c[:], in_=b2c[:], scalar=256.0, op=ALU.mult))

        for g in range(16):
            S.dma(t0[:], y[g * 128:(g + 1) * 128, :])
            S.dma(t1[:], u_f[g * 128:(g + 1) * 128, :])
            S.dma(zt[:], z_bf[g * 128:(g + 1) * 128, :])
            S.op("vector", lambda e, g=g: e.scalar_tensor_tensor(
                out=t0[:], in0=t1[:], scalar=dcol[:, g:g + 1], in1=t0[:],
                op0=ALU.mult, op1=ALU.add))
            S.op("scalar", lambda e: e.activation(out=t1[:], in_=zt[:], func=AF.Silu))
            S.op("vector", lambda e, g=g: e.tensor_mul(
                out=gbuf[:, g, :], in0=t0[:], in1=t1[:]))

        for m in range(KD):
            S.dma(wom[:], wout.rearrange("(k p) m -> p k m", p=128)[:, :, m * 128:(m + 1) * 128])
            for th in range(2):
                for k in range(16):
                    for j in range(2):
                        S.op("tensor", lambda e, k=k, th=th, j=j: e.matmul(
                            out=ps_big[:, j, :], lhsT=wom[:, k, :],
                            rhs=gbuf[:, k, th * H + j * 512:th * H + (j + 1) * 512],
                            start=(k == 0), stop=(k == 15)))
                S.dma(t0[:, 0:H], xn_f[m * 128:(m + 1) * 128, th * H:(th + 1) * H])
                S.op("vector", lambda e: e.tensor_add(
                    out=t1[:, 0:H], in0=t0[:, 0:H], in1=psb2))
                S.dma(attn0[m * 128:(m + 1) * 128, th * H:(th + 1) * H],
                      t1[:, 0:H], queue="gpsimd")

        for k in range(KD):
            S.dma(xnb[:, k, :], xn_bf[k * 128:(k + 1) * 128, :])
        for m in range(32):
            S.dma(w1m[:], w1.rearrange("(k p) m -> p k m", p=128)[:, :, m * 128:(m + 1) * 128])
            for th in range(2):
                for k in range(KD):
                    for j in range(2):
                        S.op("tensor", lambda e, k=k, th=th, j=j: e.matmul(
                            out=ps_big[:, j, :], lhsT=w1m[:, k, :],
                            rhs=xnb[:, k, th * H + j * 512:th * H + (j + 1) * 512],
                            start=(k == 0), stop=(k == KD - 1)))
                S.op("vector", lambda e, m=m: e.tensor_scalar(
                    out=tb[:, 0, :], in0=psb2, scalar1=b1c[:, m:m + 1],
                    scalar2=256.0, op0=ALU.add, op1=ALU.mult))
                S.op("vector", lambda e: e.tensor_scalar(
                    out=tb[:, 1, :], in0=tb[:, 0, :], scalar1=RNE_C,
                    scalar2=RNE_C, op0=ALU.add, op1=ALU.subtract))
                S.op("scalar", lambda e: e.activation(
                    out=tb[:, 2, :], in_=tb[:, 1, :], func=AF.Abs))
                S.op("vector", lambda e: e.tensor_single_scalar(
                    out=tb[:, 3, :], in_=tb[:, 2, :], scalar=PWL_S0, op=ALU.mult))
                for i, (vk, ck) in enumerate(PWL_KNOTS):
                    S.op("scalar", lambda e, i=i, ck=ck: e.activation(
                        out=tb[:, 0, :], in_=tb[:, 2, :], func=AF.Relu,
                        scale=abs(ck), bias=knot[:, i:i + 1]))
                    S.op("vector", lambda e, i=i: e.scalar_tensor_tensor(
                        out=tb[:, 3, :], in0=tb[:, 0, :], scalar=sgn[i],
                        in1=tb[:, 3, :], op0=ALU.mult, op1=ALU.add))
                S.op("scalar", lambda e: e.activation(
                    out=tb[:, 0, :], in_=tb[:, 2, :], func=AF.Relu,
                    bias=knot[:, 16:17]))
                S.op("vector", lambda e: e.tensor_scalar(
                    out=tb[:, 0, :], in0=tb[:, 0, :], scalar1=1.0,
                    scalar2=PWL_J256, op0=ALU.min, op1=ALU.mult))
                S.op("vector", lambda e: e.tensor_add(
                    out=tb[:, 3, :], in0=tb[:, 3, :], in1=tb[:, 0, :]))
                S.op("vector", lambda e: e.scalar_tensor_tensor(
                    out=tb[:, 0, :], in0=tb[:, 1, :], scalar=0.5,
                    in1=tb[:, 3, :], op0=ALU.mult, op1=ALU.add))
                S.op("vector", lambda e: e.tensor_scalar(
                    out=tb[:, 1, :], in0=tb[:, 0, :], scalar1=RNE_C,
                    scalar2=RNE_C, op0=ALU.add, op1=ALU.subtract))
                S.op("vector", lambda e: e.tensor_copy(out=hb[:], in_=tb[:, 1, :]))
                S.dma(h_bf[m * 128:(m + 1) * 128, th * H:(th + 1) * H],
                      hb[:], queue="gpsimd")

        for f4 in range(4):
            for k in range(32):
                S.dma(hq[:, k, :], h_bf[k * 128:(k + 1) * 128, f4 * 512:(f4 + 1) * 512])
            for m in range(KD):
                S.dma(w2m[:], w2.rearrange("(k p) m -> p k m", p=128)[:, :, m * 128:(m + 1) * 128])
                for k in range(32):
                    S.op("tensor", lambda e, k=k: e.matmul(
                        out=psq, lhsT=w2m[:, k, :], rhs=hq[:, k, :],
                        start=(k == 0), stop=(k == 31)))
                S.op("vector", lambda e, m=m: e.tensor_scalar(
                    out=tb[:, 0, 0:512], in0=psq, scalar1=b2c[:, m:m + 1],
                    scalar2=RNE_C, op0=ALU.add, op1=ALU.add))
                S.op("vector", lambda e: e.tensor_scalar(
                    out=tb[:, 1, 0:512], in0=tb[:, 0, 0:512], scalar1=RNE_C,
                    scalar2=1.0 / 256.0, op0=ALU.subtract, op1=ALU.mult))
                S.dma(tb[:, 2, 0:512], x[m * 128:(m + 1) * 128, f4 * 512:(f4 + 1) * 512])
                S.dma(tb[:, 3, 0:512], attn0[m * 128:(m + 1) * 128, f4 * 512:(f4 + 1) * 512])
                S.op("vector", lambda e: e.tensor_add(
                    out=tb[:, 0, 0:512], in0=tb[:, 2, 0:512], in1=tb[:, 3, 0:512]))
                S.op("vector", lambda e: e.tensor_add(
                    out=tb[:, 1, 0:512], in0=tb[:, 0, 0:512], in1=tb[:, 1, 0:512]))
                S.dma(x1[m * 128:(m + 1) * 128, f4 * 512:(f4 + 1) * 512],
                      tb[:, 1, 0:512], queue="gpsimd")
        S.done()
    return nc



def build_l1out():
    nc, stack, ctx = new_prog()
    S = Seq(ctx)
    H = T // 2
    KD = D // 128

    y1 = din(nc, "y", (DI, T), F32)
    u1 = din(nc, "u_f", (DI, T), F32)
    z1 = din(nc, "z_bf", (DI, T), BF16)
    dskip = din(nc, "dskip", (DI,), BF16)
    opt = din(nc, "opT", (DI, D), BF16)
    xn1 = din(nc, "xn_bf", (D, T), BF16)
    f1t = din(nc, "f1t_bf", (D, 4096), BF16)
    w1s = din(nc, "w1s", (4096,), F32)
    f2t = din(nc, "f2t_bf", (D, 4096), BF16)
    w2s = din(nc, "w2s", (4096,), F32)
    h1t = din(nc, "h1t_bf", (4096, 4096), BF16)
    h1s = din(nc, "h1s", (4096,), F32)
    h2t = din(nc, "h2t_bf", (4096, 4096), BF16)
    h2s = din(nc, "h2s", (4096,), F32)
    w3t = din(nc, "w3t_bf", (4096, D), BF16)
    w3s = din(nc, "w3s", (D,), F32)
    x1 = din(nc, "x1", (D, T), F32)
    attn1 = dout(nc, "attn1", (D, T), BF16)
    gf = dout(nc, "gf", (4096, T), BF16)
    a1 = dout(nc, "a1", (4096, T), BF16)
    a2 = dout(nc, "a2", (4096, T), BF16)
    out = dout(nc, "out", (D, T), F32)

    with stack:
        gbuf = ctx.sbuf("gbuf", [128, 16, T], BF16)
        t0 = ctx.sbuf("t0", [128, T], F32)
        t1 = ctx.sbuf("t1", [128, T], F32)
        zt = ctx.sbuf("zt", [128, T], BF16)
        dcr = ctx.sbuf("dcr", [128, 16], BF16)
        dcol = ctx.sbuf("dcol", [128, 16], F32)
        wm = ctx.sbuf("wm", [128, 16, 128], BF16)
        wm2 = ctx.sbuf("wm2", [128, 16, 128], BF16)
        ab = ctx.sbuf("ab", [128, H], BF16)
        s1c = ctx.sbuf("s1c", [128, 32], F32)
        s2c = ctx.sbuf("s2c", [128, 32], F32)
        sh1c = ctx.sbuf("sh1c", [128, 32], F32)
        sh2c = ctx.sbuf("sh2c", [128, 32], F32)
        s3c = ctx.sbuf("s3c", [128, 8], F32)
        rq = ctx.sbuf("rq", [128, 32, 512], BF16)
        w32 = ctx.sbuf("w32", [128, 32, 128], BF16)
        ps_big = ctx.psum("ps_big", [128, 4, 512], F32)
        psb2 = ps_big[:, 0:2, :].rearrange("p a f -> p (a f)")
        psc2 = ps_big[:, 2:4, :].rearrange("p a f -> p (a f)")
        psq = ps_big[:, 0:1, :].rearrange("p a f -> p (a f)")

        S.dma(dcr[:], dskip.rearrange("(g p) -> p g", p=128), slow=True)
        S.op("vector", lambda e: e.tensor_copy(out=dcol[:], in_=dcr[:]))
        S.dma(s1c[:], w1s.rearrange("(g p) -> p g", p=128), slow=True)
        S.dma(s2c[:], w2s.rearrange("(g p) -> p g", p=128), slow=True)
        S.dma(sh1c[:], h1s.rearrange("(g p) -> p g", p=128), slow=True)
        S.dma(sh2c[:], h2s.rearrange("(g p) -> p g", p=128), slow=True)
        S.dma(s3c[:], w3s.rearrange("(g p) -> p g", p=128), slow=True)

        for g in range(16):
            S.dma(t0[:], y1[g * 128:(g + 1) * 128, :])
            S.dma(t1[:], u1[g * 128:(g + 1) * 128, :])
            S.dma(zt[:], z1[g * 128:(g + 1) * 128, :])
            S.op("vector", lambda e, g=g: e.scalar_tensor_tensor(
                out=t0[:], in0=t1[:], scalar=dcol[:, g:g + 1], in1=t0[:],
                op0=ALU.mult, op1=ALU.add))
            S.op("scalar", lambda e: e.activation(out=t1[:], in_=zt[:], func=AF.Silu))
            S.op("vector", lambda e, g=g: e.tensor_mul(
                out=gbuf[:, g, :], in0=t0[:], in1=t1[:]))

        for m in range(KD):
            S.dma(wm[:], opt.rearrange("(k p) m -> p k m", p=128)[:, :, m * 128:(m + 1) * 128])
            for th in range(2):
                for k in range(16):
                    for j in range(2):
                        S.op("tensor", lambda e, k=k, th=th, j=j: e.matmul(
                            out=ps_big[:, j, :], lhsT=wm[:, k, :],
                            rhs=gbuf[:, k, th * H + j * 512:th * H + (j + 1) * 512],
                            start=(k == 0), stop=(k == 15)))
                S.op("vector", lambda e: e.tensor_copy(out=ab[:], in_=psb2))
                S.dma(attn1[m * 128:(m + 1) * 128, th * H:(th + 1) * H],
                      ab[:], queue="gpsimd")

        for k in range(KD):
            S.dma(gbuf[:, k, :], xn1[k * 128:(k + 1) * 128, :])
        for m in range(32):
            S.dma(wm[:, 0:KD, :], f1t.rearrange("(k p) m -> p k m", p=128)[:, :, m * 128:(m + 1) * 128])
            S.dma(wm2[:, 0:KD, :], f2t.rearrange("(k p) m -> p k m", p=128)[:, :, m * 128:(m + 1) * 128])
            for th in range(2):
                for k in range(KD):
                    for j in range(2):
                        S.op("tensor", lambda e, k=k, th=th, j=j: e.matmul(
                            out=ps_big[:, j, :], lhsT=wm[:, k, :],
                            rhs=gbuf[:, k, th * H + j * 512:th * H + (j + 1) * 512],
                            start=(k == 0), stop=(k == KD - 1)))
                for k in range(KD):
                    for j in range(2):
                        S.op("tensor", lambda e, k=k, th=th, j=j: e.matmul(
                            out=ps_big[:, 2 + j, :], lhsT=wm2[:, k, :],
                            rhs=gbuf[:, k, th * H + j * 512:th * H + (j + 1) * 512],
                            start=(k == 0), stop=(k == KD - 1)))
                S.op("scalar", lambda e, m=m: e.activation(
                    out=t0[:, 0:H], in_=psb2, func=AF.Silu, scale=s1c[:, m:m + 1]))
                S.op("scalar", lambda e, m=m: e.activation(
                    out=t1[:, 0:H], in_=psc2, func=AF.Identity, scale=s2c[:, m:m + 1]))
                S.op("vector", lambda e: e.tensor_mul(
                    out=ab[:], in0=t0[:, 0:H], in1=t1[:, 0:H]))
                S.dma(gf[m * 128:(m + 1) * 128, th * H:(th + 1) * H],
                      ab[:], queue="gpsimd")

        for (src, dst, wmat, scol) in ((gf, a1, h1t, sh1c), (a1, a2, h2t, sh2c)):
            for f4 in range(4):
                for k in range(32):
                    S.dma(rq[:, k, :], src[k * 128:(k + 1) * 128,
                                           f4 * 512:(f4 + 1) * 512])
                for m in range(32):
                    S.dma(w32[:], wmat.rearrange("(k p) m -> p k m", p=128)[:, :, m * 128:(m + 1) * 128])
                    for k in range(32):
                        S.op("tensor", lambda e, k=k: e.matmul(
                            out=psq, lhsT=w32[:, k, :], rhs=rq[:, k, :],
                            start=(k == 0), stop=(k == 31)))
                    S.op("scalar", lambda e, m=m, scol=scol: e.activation(
                        out=ab[:, 0:512], in_=psq, func=AF.Gelu,
                        scale=scol[:, m:m + 1]))
                    S.dma(dst[m * 128:(m + 1) * 128, f4 * 512:(f4 + 1) * 512],
                          ab[:, 0:512], queue="gpsimd")

        for f4 in range(4):
            for k in range(32):
                S.dma(rq[:, k, :], a2[k * 128:(k + 1) * 128, f4 * 512:(f4 + 1) * 512])
            for m in range(KD):
                S.dma(w32[:], w3t.rearrange("(k p) m -> p k m", p=128)[:, :, m * 128:(m + 1) * 128])
                for k in range(32):
                    S.op("tensor", lambda e, k=k: e.matmul(
                        out=psq, lhsT=w32[:, k, :], rhs=rq[:, k, :],
                        start=(k == 0), stop=(k == 31)))
                S.op("scalar", lambda e, m=m: e.activation(
                    out=t0[:, 0:512], in_=psq, func=AF.Identity,
                    scale=s3c[:, m:m + 1]))
                S.dma(t1[:, 0:512], x1[m * 128:(m + 1) * 128, f4 * 512:(f4 + 1) * 512])
                S.dma(ab[:, 0:512], attn1[m * 128:(m + 1) * 128, f4 * 512:(f4 + 1) * 512])
                S.op("vector", lambda e: e.tensor_add(
                    out=t1[:, 0:512], in0=t1[:, 0:512], in1=ab[:, 0:512]))
                S.op("vector", lambda e: e.tensor_add(
                    out=t0[:, 0:512], in0=t0[:, 0:512], in1=t1[:, 0:512]))
                S.dma(out[m * 128:(m + 1) * 128, f4 * 512:(f4 + 1) * 512],
                      t0[:, 0:512], queue="gpsimd")
        S.done()
    return nc


_PROGS = {}


def _prog(name, builder):
    if name not in _PROGS:
        _PROGS[name] = builder()
    return _PROGS[name]


def _run(name, builder, maps):
    res = run_bass_kernel_spmd(_prog(name, builder), maps, core_ids=list(range(8)))
    return res.results


def kernel(**inp):
    asc = np.ascontiguousarray
    g = {k: np.asarray(v) for k, v in inp.items()}
    x = g["L0_attn_x"]
    xfm = [asc(x[c].T) for c in range(8)]

    f1T = asc(g["L1_ffn_w1_w"].T)
    f2T = asc(g["L1_ffn_w2_w"].T)
    h1T = asc(g["L1_ffn_h1_w"].T)
    h2T = asc(g["L1_ffn_h2_w"].T)
    w3T = asc(g["L1_ffn_w3_w"].T)
    ipT = asc(g["L1_attn_in_proj_w"].T)
    opT = asc(g["L1_attn_out_proj_w"].T)
    xpT = asc(g["L1_attn_x_proj_w"].T)
    dpT = asc(g["L1_attn_dt_proj_w"].T)
    cw0 = asc(g["L0_attn_conv_w"].reshape(DI, 4))
    cw1 = asc(g["L1_attn_conv_w"].reshape(DI, 4))

    pm = {"win": g["L0_attn_W_in"], "wout": g["L0_attn_W_out"],
          "w1": g["L0_ffn_w1"], "w2": g["L0_ffn_w2"],
          "wxp": g["L0_attn_W_xproj"],
          "f1t": f1T, "f2t": f2T, "h1t": h1T, "h2t": h2T, "w3t": w3T}
    W = _run("prep", build_prep, [pm] * 8)[0]

    mA = [{"x": xfm[c], "lnw": g["L0_attn_ln_w"], "lnb": g["L0_attn_ln_b"],
           "win_bf": W["win_bf"], "convw": cw0, "convb": g["L0_attn_conv_b"],
           "wxp_bf": W["wxp_bf"], "wdt": g["L0_attn_W_dt"],
           "bdt": g["L0_attn_b_dt"]} for c in range(8)]
    rA = _run("l0a", lambda: build_lxa(0), mA)

    mS = [{"delta": rA[c]["delta"], "du": rA[c]["du"], "Bm": rA[c]["B0"],
           "Cm": rA[c]["C0"], "alog": g["L0_attn_A_log"]} for c in range(8)]
    rS = _run("scan0", lambda: build_scan(0), mS)

    mO = [{"y": rS[c]["y"], "u_f": rA[c]["u_f"], "z_bf": rA[c]["z_bf"],
           "d0": g["L0_attn_D"], "wout_bf": W["wout_bf"], "xn_f": rA[c]["xn_f"],
           "xn_bf": rA[c]["xn_bf"], "w1_bf": W["w1_bf"], "b1": g["L0_ffn_b1"],
           "w2_bf": W["w2_bf"], "b2": g["L0_ffn_b2"], "x": xfm[c]}
          for c in range(8)]
    rO = _run("l0out", build_l0out, mO)

    mA1 = [{"x": rO[c]["x1"], "win_bf": ipT, "convw": cw1,
            "convb": g["L1_attn_conv_b"], "wxp_bf": xpT, "wdt": dpT,
            "bdt": g["L1_attn_dt_proj_b"]} for c in range(8)]
    rA1 = _run("l1a", lambda: build_lxa(1), mA1)

    mS1 = [{"delta": rA1[c]["delta"], "du": rA1[c]["du"], "Bm": rA1[c]["B0"],
            "Cm": rA1[c]["C0"], "alog": g["L1_attn_A_log"]} for c in range(8)]
    rS1 = _run("scan1", lambda: build_scan(1), mS1)

    mO1 = [{"y": rS1[c]["y"], "u_f": rA1[c]["u_f"], "z_bf": rA1[c]["z_bf"],
            "dskip": g["L1_attn_D_skip"], "opT": opT, "xn_bf": rA1[c]["xn_bf"],
            "f1t_bf": W["f1t_bf"], "w1s": g["L1_ffn_w1_s"],
            "f2t_bf": W["f2t_bf"], "w2s": g["L1_ffn_w2_s"],
            "h1t_bf": W["h1t_bf"], "h1s": g["L1_ffn_h1_s"],
            "h2t_bf": W["h2t_bf"], "h2s": g["L1_ffn_h2_s"],
            "w3t_bf": W["w3t_bf"], "w3s": g["L1_ffn_w3_s"],
            "x1": rO[c]["x1"]} for c in range(8)]
    rO1 = _run("l1out", build_l1out, mO1)

    return np.stack([asc(rO1[c]["out"].T) for c in range(8)], axis=0)


# version 03
# speedup vs baseline: 1.0026x; 1.0026x over previous
import math
import numpy as np
import ml_dtypes
from contextlib import ExitStack

import concourse.bass as bass
import concourse.mybir as mybir
from concourse.bass_utils import run_bass_kernel_spmd

F32 = mybir.dt.float32
BF16 = mybir.dt.bfloat16
F16 = mybir.dt.float16
ALU = mybir.AluOpType
AF = mybir.ActivationFunctionType

BF = ml_dtypes.bfloat16

T = 2048
D = 1024
DI = 2048
RNE_C = 12582912.0

def _pwl_constants():
    xs = [0.25 * k for k in range(17)]
    ev = [0.5 * x * (1.0 + math.erf(x / math.sqrt(2.0))) - x / 2.0 for x in xs]
    s = [(ev[k + 1] - ev[k]) / 0.25 for k in range(16)]
    knots = []
    for k in range(1, 16):
        knots.append((64.0 * k, s[k] - s[k - 1]))
    knots.append((1024.0, 0.5 - s[15]))
    j256 = 256.0 * (4.0 - (0.5 * 4.0 * (1.0 + math.erf(4.0 / math.sqrt(2.0)))))
    return s[0], knots, j256


PWL_S0, PWL_KNOTS, PWL_J256 = _pwl_constants()


class Ctx:
    ENGINES = ("sync", "vector", "scalar", "tensor", "gpsimd")

    def __init__(self, nc, stack):
        self.nc = nc
        self.stack = stack
        self.ops = {e: [] for e in self.ENGINES}
        self.sems = {}
        self.cnt = {}
        self.bump = 16512
        self.phase_tag = "g"
        self.psA = None
        self.psB = None

    def sem(self, name):
        if name not in self.sems:
            self.sems[name] = self.stack.enter_context(self.nc.semaphore(name))
            self.cnt[name] = 0
        return self.sems[name]

    def fire(self, name, amount):
        h = self.sem(name)
        self.cnt[name] += amount
        return (h, amount)

    def val(self, name):
        return self.cnt.get(name, 0)

    def emit(self, engine, fn, waits=(), fire=None):
        wl = [(self.sem(n), v) for (n, v) in waits if v > 0]

        def runner(eng):
            for h, v in wl:
                eng.wait_ge(h, v)
            ins = fn(eng)
            if fire is not None:
                ins.then_inc(fire[0], fire[1])
        self.ops[engine].append(runner)

    def dma(self, queue, out, in_, waits=(), sem="dma", transpose=False):
        f = self.fire(sem, 16)
        if transpose:
            self.emit(queue, lambda e: e.dma_start_transpose(out=out, in_=in_),
                      waits, f)
        else:
            self.emit(queue, lambda e: e.dma_start(out=out, in_=in_), waits, f)
        return self.cnt[sem]

    SBUF_BASE = 16512
    SBUF_LIMIT = 229344

    def phase(self, tag):
        self.phase_tag = tag
        self.bump = self.SBUF_BASE

    def sbuf(self, name, shape, dt):
        per = 1
        for s in shape[1:]:
            per *= s
        per *= {F32: 4, BF16: 2, F16: 2}[dt]
        off = (self.bump + 31) & ~31
        self.bump = off + per
        assert self.bump <= self.SBUF_LIMIT, (
            f"sbuf overflow in phase {self.phase_tag}: {self.bump}")
        h = self.nc.alloc_sbuf_tensor_at(
            f"{self.phase_tag}_{name}", list(shape), dt, offset=off)
        return h

    def psum(self, name, shape, dt=F32):
        return self.stack.enter_context(self.nc.psum_tensor(name, shape, dt))

    def finalize(self, final_waits):
        ops = self.ops
        wl = [(self.sems[n], v) for (n, v) in final_waits]
        with self.nc.Block() as block:
            @block.sync
            def _(eng):
                for f in ops["sync"]:
                    f(eng)
                for h, v in wl:
                    eng.wait_ge(h, v)

            @block.vector
            def _(eng):
                for f in ops["vector"]:
                    f(eng)

            @block.scalar
            def _(eng):
                for f in ops["scalar"]:
                    f(eng)

            @block.tensor
            def _(eng):
                for f in ops["tensor"]:
                    f(eng)

            @block.gpsimd
            def _(eng):
                for f in ops["gpsimd"]:
                    f(eng)


def new_prog():
    nc = bass.Bass("TRN2", target_bir_lowering=False)
    stack = ExitStack()
    ctx = Ctx(nc, stack)
    return nc, stack, ctx


def din(nc, name, shape, dt=F32):
    return nc.dram_tensor(name, list(shape), dt, kind="ExternalInput").ap()


def dout(nc, name, shape, dt=F32):
    return nc.dram_tensor(name, list(shape), dt, kind="ExternalOutput").ap()



def _seq(ctx, engine, fn):
    v = ctx.val("q")
    ctx.emit(engine, fn, waits=(("q", v),), fire=ctx.fire("q", 1))


def _seqdma(ctx, queue, out, in_, transpose=False, slow=False):
    v = ctx.val("q")
    f = ctx.fire("q", 16)
    if transpose:
        ctx.emit(queue, lambda e: e.dma_start_transpose(out=out, in_=in_),
                 waits=(("q", v),), fire=f)
    elif slow:
        ctx.emit(queue, lambda e: e.dma_start(out=out, in_=in_,
                                              allow_slow_non_contiguous=True),
                 waits=(("q", v),), fire=f)
    else:
        ctx.emit(queue, lambda e: e.dma_start(out=out, in_=in_),
                 waits=(("q", v),), fire=f)


class Seq:

    def __init__(self, ctx):
        self.ctx = ctx

    def op(self, engine, fn):
        _seq(self.ctx, engine, fn)

    def dma(self, out, in_, queue="sync", transpose=False, slow=False):
        _seqdma(self.ctx, queue, out, in_, transpose, slow)

    def done(self):
        self.ctx.finalize([("q", self.ctx.val("q"))])


class ConstCols:

    def __init__(self, S, values):
        self.tile = S.ctx.sbuf("cstc", [128, max(1, len(values))], F32)
        self.idx = {}
        for i, v in enumerate(values):
            self.idx[v] = i
            S.op("vector", lambda e, i=i, v=v: e.memset(self.tile[:, i:i + 1], v))

    def ap(self, v):
        i = self.idx[v]
        return self.tile[:, i:i + 1]


CASTS = [
    ("win", "win_bf", D, 4096),
    ("wout", "wout_bf", DI, D),
    ("w1", "w1_bf", D, 4096),
    ("w2", "w2_bf", 4096, D),
    ("wxp", "wxp_bf", DI, 192),
    ("f1t", "f1t_bf", D, 4096),
    ("f2t", "f2t_bf", D, 4096),
    ("h1t", "h1t_bf", 4096, 4096),
    ("h2t", "h2t_bf", 4096, 4096),
    ("w3t", "w3t_bf", 4096, D),
]


def prep_io(nc, mk):
    io = {}
    for (iname, oname, rows, cols) in CASTS:
        io[iname] = mk(iname, (rows, cols), F32, 0)
        io[oname] = mk(oname, (rows, cols), BF16, 1)
    return io


def emit_prep(nc, ctx, S, io):
    tin = ctx.sbuf("tin", [128, 4096], F32)
    tout = ctx.sbuf("tout", [128, 4096], BF16)
    for (iname, oname, rows, cols) in CASTS:
        src = io[iname]
        dst = io[oname]
        for i in range(rows // 128):
            S.dma(tin[:, :cols], src[i * 128:(i + 1) * 128, :])
            S.op("vector", lambda e, c=cols: e.tensor_copy(
                out=tout[:, :c], in_=tin[:, :c]))
            S.dma(dst[i * 128:(i + 1) * 128, :], tout[:, :cols],
                  queue="gpsimd")


def build_prep():
    nc, stack, ctx = new_prog()
    with stack:
        S = Seq(ctx)
        io = prep_io(nc, lambda n, s, d, o: (dout if o else din)(nc, n, s, d))
        ctx.phase("prep")
        emit_prep(nc, ctx, S, io)
        S.done()
    return nc



LXA_IO = ["x", "win_bf", "convw", "convb", "wxp_bf", "wdt", "bdt",
          "xn_bf", "xi_bf", "z_bf", "u_f", "u_bf", "B0", "C0", "delta", "du"]


def lxa_io(nc, layer, mk):
    NST = 64 if layer == 0 else 16
    MX = 192 if layer == 0 else 96
    fl = F32 if layer == 0 else BF16
    io = {}
    io["x"] = mk("x", (D, T), F32, 0)
    io["win_bf"] = mk("win_bf", (D, 4096), BF16, 0)
    io["convw"] = mk("convw", (DI, 4), fl, 0)
    io["convb"] = mk("convb", (DI,), fl, 0)
    io["wxp_bf"] = mk("wxp_bf", (DI, MX), BF16, 0)
    io["wdt"] = mk("wdt", (64, DI), fl, 0)
    io["bdt"] = mk("bdt", (DI,), fl, 0)
    if layer == 0:
        io["lnw"] = mk("lnw", (D,), F32, 0)
        io["lnb"] = mk("lnb", (D,), F32, 0)
        io["xn_f"] = mk("xn_f", (D, T), F32, 1)
    io["xn_bf"] = mk("xn_bf", (D, T), BF16, 1)
    io["xi_bf"] = mk("xi_bf", (DI, T), BF16, 1)
    io["z_bf"] = mk("z_bf", (DI, T), BF16, 1)
    io["u_f"] = mk("u_f", (DI, T), F32, 1)
    io["u_bf"] = mk("u_bf", (DI, T), BF16, 1)
    io["B0"] = mk("B0", (NST, T), F32, 1)
    io["C0"] = mk("C0", (NST, T), F32, 1)
    io["delta"] = mk("delta", (DI, T), F32, 1)
    io["du"] = mk("du", (DI, T), F32, 1)
    return io


def emit_lxa(nc, ctx, S, io, layer):
    KD = D // 128
    H = T // 2
    NST = 64 if layer == 0 else 16
    MX = 192 if layer == 0 else 96

    x = io["x"]
    win = io["win_bf"]
    convw = io["convw"]
    convb = io["convb"]
    wxp = io["wxp_bf"]
    wdt = io["wdt"]
    bdt = io["bdt"]
    if layer == 0:
        lnw = io["lnw"]
        lnb = io["lnb"]
        xn_f = io["xn_f"]
    xn_bf = io["xn_bf"]
    xi_bf = io["xi_bf"]
    z_bf = io["z_bf"]
    u_f = io["u_f"]
    u_bf = io["u_bf"]
    B0 = io["B0"]
    C0 = io["C0"]
    delta = io["delta"]
    du = io["du"]

    if True:
        xt = ctx.sbuf("xt", [128, H], F32)
        sq = ctx.sbuf("sq", [128, H], F32)
        ones_col = ctx.sbuf("ones_col", [128, 1], F32)
        ones_row = ctx.sbuf("ones_row", [1, 128], F32)
        rows = ctx.sbuf("rows", [1, 4, T], F32)
        rep_a = ctx.sbuf("rep_a", [128, T], F32)
        rep_b = ctx.sbuf("rep_b", [128, T], F32)
        xbf = ctx.sbuf("xbf", [128, H], BF16)
        xn2 = ctx.sbuf("xn2", [128, KD, T], BF16)
        wm = ctx.sbuf("wm", [128, KD, 128], BF16)
        ebf = ctx.sbuf("ebf", [128, H], BF16)
        ct = ctx.sbuf("ct", [128, T + 3], BF16)
        cwr = ctx.sbuf("cwr", [128, 16, 4], F32 if layer == 0 else BF16)
        cbr = ctx.sbuf("cbr", [128, 16], F32 if layer == 0 else BF16)
        cw = ctx.sbuf("cw", [128, 16, 4], F32)
        cb = ctx.sbuf("cb", [128, 16], F32)
        ua = ctx.sbuf("ua", [128, T], F32)
        ub = ctx.sbuf("ub", [128, T], BF16)
        wxpt = ctx.sbuf("wxpt", [128, 16, MX], BF16)
        uh = ctx.sbuf("uh", [128, H], BF16)
        dtr = ctx.sbuf("dtr", [64, T], BF16)
        wdtf = ctx.sbuf("wdtf", [64, DI], F32 if layer == 0 else BF16)
        wdtb = ctx.sbuf("wdtb", [64, DI], BF16)
        bdtr = ctx.sbuf("bdtr", [128, 16], F32 if layer == 0 else BF16)
        bdtc = ctx.sbuf("bdtc", [128, 16], F32)
        tb = ctx.sbuf("tb", [128, 4, H], F32)
        lncol = ctx.sbuf("lncol", [128, 2, KD], F32)
        ps_big = ctx.psA
        ps_sm = ctx.psB[0:64]

        MU, SQS, RSTD, SCR = range(4)
        psb2 = ps_big[:, 0:2, :].rearrange("p a f -> p (a f)")

        S.op("vector", lambda e: e.memset(ones_col[:], 1.0))
        S.op("vector", lambda e: e.memset(ones_row[:], 1.0))
        cst = ConstCols(S, [1e-5])

        def stats_pass(src_ap, xn_mode):
            for th in range(2):
                for k in range(KD):
                    S.dma(xt[:], src_ap[k * 128:(k + 1) * 128, th * H:(th + 1) * H])
                    if xn_mode:
                        S.op("vector", lambda e, th=th: e.tensor_sub(
                            out=xt[:], in0=xt[:], in1=rep_a[:, th * H:(th + 1) * H]))
                        S.op("vector", lambda e, th=th: e.tensor_mul(
                            out=xt[:], in0=xt[:], in1=rep_b[:, th * H:(th + 1) * H]))
                        if layer == 0:
                            S.dma(xn_f[k * 128:(k + 1) * 128, th * H:(th + 1) * H],
                                  xt[:], queue="gpsimd")
                        S.op("vector", lambda e: e.tensor_copy(out=xbf[:], in_=xt[:]))
                        S.dma(xn_bf[k * 128:(k + 1) * 128, th * H:(th + 1) * H],
                              xbf[:], queue="gpsimd")
                    if (not xn_mode) or layer == 0:
                        S.op("scalar", lambda e: e.activation(
                            out=sq[:], in_=xt[:], func=AF.Square))
                        for which in range(2):
                            st = xt if which == 0 else sq
                            for fc in range(2):
                                S.op("tensor", lambda e, w=which, fc=fc, k=k, st=st:
                                     e.matmul(out=ps_sm[0:1, w * 2 + fc, :],
                                              lhsT=ones_col[:],
                                              rhs=st[:, fc * 512:(fc + 1) * 512],
                                              start=(k == 0), stop=(k == KD - 1)))
                if (not xn_mode) or layer == 0:
                    for which, slot in ((0, MU), (1, SQS)):
                        for fc in range(2):
                            S.op("scalar", lambda e, w=which, s=slot, fc=fc, th=th:
                                 e.activation(
                                     out=rows[:, s, th * H + fc * 512:
                                              th * H + (fc + 1) * 512],
                                     in_=ps_sm[0:1, w * 2 + fc, :],
                                     func=AF.Identity, scale=1.0 / D))

        def row_math():
            S.op("vector", lambda e: e.tensor_mul(
                out=rows[:, SCR, :], in0=rows[:, MU, :], in1=rows[:, MU, :]))
            S.op("vector", lambda e: e.tensor_sub(
                out=rows[:, SCR, :], in0=rows[:, SQS, :], in1=rows[:, SCR, :]))
            S.op("scalar", lambda e: e.activation(
                out=rows[:, SCR, :], in_=rows[:, SCR, :], func=AF.Sqrt,
                bias=cst.ap(1e-5)[0:1, :]))
            S.op("vector", lambda e: e.reciprocal(
                out=rows[:, RSTD, :], in_=rows[:, SCR, :]))

        def rep_pass():
            for slot, dst in ((MU, rep_a), (RSTD, rep_b)):
                for fc in range(4):
                    S.op("tensor", lambda e, s=slot, fc=fc: e.matmul(
                        out=ps_big[:, fc, :], lhsT=ones_row[:],
                        rhs=rows[:, s, fc * 512:(fc + 1) * 512],
                        start=True, stop=True))
                S.op("vector", lambda e, d=dst: e.tensor_copy(
                    out=d[:], in_=ps_big[:].rearrange("p a f -> p (a f)")))

        stats_pass(x, False)
        row_math()
        rep_pass()
        stats_pass(x, True)
        if layer == 0:
            S.dma(lncol[:, 0, :], lnw.rearrange("(k p) -> p k", p=128), slow=True)
            S.dma(lncol[:, 1, :], lnb.rearrange("(k p) -> p k", p=128), slow=True)
            row_math()
            rep_pass()
            for th in range(2):
                for k in range(KD):
                    S.dma(xt[:], xn_f[k * 128:(k + 1) * 128, th * H:(th + 1) * H])
                    S.op("vector", lambda e, th=th: e.tensor_sub(
                        out=xt[:], in0=xt[:], in1=rep_a[:, th * H:(th + 1) * H]))
                    S.op("vector", lambda e, th=th: e.tensor_mul(
                        out=xt[:], in0=xt[:], in1=rep_b[:, th * H:(th + 1) * H]))
                    S.op("scalar", lambda e, k=k, th=th: e.activation(
                        out=xn2[:, k, th * H:(th + 1) * H], in_=xt[:],
                        func=AF.Identity, scale=lncol[:, 0, k:k + 1],
                        bias=lncol[:, 1, k:k + 1]))
        else:
            for k in range(KD):
                S.dma(xn2[:, k, :], xn_bf[k * 128:(k + 1) * 128, :])

        for m in range(32):
            S.dma(wm[:], win.rearrange("(k p) m -> p k m", p=128)[:, :, m * 128:(m + 1) * 128])
            for th in range(2):
                for k in range(KD):
                    for j in range(2):
                        S.op("tensor", lambda e, k=k, th=th, j=j: e.matmul(
                            out=ps_big[:, j, :], lhsT=wm[:, k, :],
                            rhs=xn2[:, k, th * H + j * 512:th * H + (j + 1) * 512],
                            start=(k == 0), stop=(k == KD - 1)))
                S.op("vector", lambda e: e.tensor_copy(out=ebf[:], in_=psb2))
                dst = xi_bf if m < 16 else z_bf
                mm2 = m if m < 16 else m - 16
                S.dma(dst[mm2 * 128:(mm2 + 1) * 128, th * H:(th + 1) * H],
                      ebf[:], queue="gpsimd")

        S.dma(cwr[:], convw.rearrange("(g p) k -> p g k", p=128), slow=True)
        S.dma(cbr[:], convb.rearrange("(g p) -> p g", p=128), slow=True)
        S.op("vector", lambda e: e.tensor_copy(out=cw[:], in_=cwr[:]))
        S.op("vector", lambda e: e.tensor_copy(out=cb[:], in_=cbr[:]))
        S.op("vector", lambda e: e.memset(ct[:, 0:3], 0.0))
        for g in range(16):
            S.dma(ct[:, 3:T + 3], xi_bf[g * 128:(g + 1) * 128, :])
            S.op("vector", lambda e, g=g: e.tensor_scalar(
                out=ua[:], in0=ct[:, 0:T], scalar1=cw[:, g, 0:1],
                scalar2=None, op0=ALU.mult))
            for kk in range(1, 4):
                S.op("vector", lambda e, g=g, kk=kk: e.scalar_tensor_tensor(
                    out=ua[:], in0=ct[:, kk:T + kk], scalar=cw[:, g, kk:kk + 1],
                    in1=ua[:], op0=ALU.mult, op1=ALU.add))
            S.op("scalar", lambda e, g=g: e.activation(
                out=ua[:], in_=ua[:], func=AF.Silu, bias=cb[:, g:g + 1]))
            S.op("vector", lambda e: e.tensor_copy(out=ub[:], in_=ua[:]))
            if layer == 1:
                S.op("vector", lambda e: e.tensor_copy(out=ua[:], in_=ub[:]))
            S.dma(u_f[g * 128:(g + 1) * 128, :], ua[:], queue="gpsimd")
            S.dma(u_bf[g * 128:(g + 1) * 128, :], ub[:], queue="gpsimd")

        S.dma(wxpt[:], wxp.rearrange("(k p) m -> p k m", p=128))
        psc2 = ps_sm[:, 0:2, :].rearrange("p a f -> p (a f)")
        psc3 = ps_sm[:, 2:4, :].rearrange("p a f -> p (a f)")
        for th in range(2):
            for k in range(16):
                S.dma(uh[:], u_bf[k * 128:(k + 1) * 128, th * H:(th + 1) * H])
                for j in range(2):
                    uhj = uh[:, j * 512:(j + 1) * 512]
                    if MX == 192:
                        S.op("tensor", lambda e, k=k, j=j, u2=uhj: e.matmul(
                            out=ps_big[:, j, :], lhsT=wxpt[:, k, 0:128],
                            rhs=u2, start=(k == 0), stop=(k == 15)))
                        S.op("tensor", lambda e, k=k, j=j, u2=uhj: e.matmul(
                            out=ps_sm[:, j, :], lhsT=wxpt[:, k, 128:192], rhs=u2,
                            start=(k == 0), stop=(k == 15)))
                    else:
                        S.op("tensor", lambda e, k=k, j=j, u2=uhj: e.matmul(
                            out=ps_big[0:64, j, :], lhsT=wxpt[:, k, 0:64],
                            rhs=u2, start=(k == 0), stop=(k == 15)))
                        S.op("tensor", lambda e, k=k, j=j, u2=uhj: e.matmul(
                            out=ps_sm[0:16, j, :], lhsT=wxpt[:, k, 64:80], rhs=u2,
                            start=(k == 0), stop=(k == 15)))
                        S.op("tensor", lambda e, k=k, j=j, u2=uhj: e.matmul(
                            out=ps_sm[0:16, 2 + j, :], lhsT=wxpt[:, k, 80:96], rhs=u2,
                            start=(k == 0), stop=(k == 15)))
            S.op("vector", lambda e, th=th: e.tensor_copy(
                out=dtr[:, th * H:(th + 1) * H],
                in_=ps_big[0:64, 0:2, :].rearrange("p a f -> p (a f)")))
            if MX == 192:
                S.op("vector", lambda e: e.tensor_copy(
                    out=tb[64:128, 0, :], in_=psb2[64:128, :]))
                S.op("vector", lambda e: e.tensor_copy(
                    out=tb[0:64, 1, :], in_=psc2[0:64, :]))
                S.dma(B0[:, th * H:(th + 1) * H], tb[64:128, 0, :], queue="gpsimd")
                S.dma(C0[:, th * H:(th + 1) * H], tb[0:64, 1, :], queue="gpsimd")
            else:
                S.op("vector", lambda e: e.tensor_copy(
                    out=tb[0:16, 0, :], in_=psc2[0:16, :]))
                S.op("vector", lambda e: e.tensor_copy(
                    out=tb[0:16, 1, :], in_=psc3[0:16, :]))
                S.dma(B0[:, th * H:(th + 1) * H], tb[0:16, 0, :], queue="gpsimd")
                S.dma(C0[:, th * H:(th + 1) * H], tb[0:16, 1, :], queue="gpsimd")

        S.dma(wdtf[:], wdt)
        S.op("vector", lambda e: e.tensor_copy(out=wdtb[:], in_=wdtf[:]))
        S.dma(bdtr[:], bdt.rearrange("(g p) -> p g", p=128), slow=True)
        S.op("vector", lambda e: e.tensor_copy(out=bdtc[:], in_=bdtr[:]))
        for m in range(16):
            for th in range(2):
                for j in range(2):
                    S.op("tensor", lambda e, m=m, th=th, j=j: e.matmul(
                        out=ps_big[:, j, :], lhsT=wdtb[:, m * 128:(m + 1) * 128],
                        rhs=dtr[:, th * H + j * 512:th * H + (j + 1) * 512],
                        start=True, stop=True))
                S.op("vector", lambda e, m=m: e.tensor_scalar(
                    out=tb[:, 0, :], in0=psb2, scalar1=bdtc[:, m:m + 1],
                    scalar2=0.0, op0=ALU.add, op1=ALU.max))
                S.op("scalar", lambda e, m=m: e.activation(
                    out=tb[:, 1, :], in_=psb2, func=AF.Abs, bias=bdtc[:, m:m + 1]))
                S.op("scalar", lambda e: e.activation(
                    out=tb[:, 2, :], in_=tb[:, 1, :], func=AF.Exp, scale=-1.0))
                S.op("scalar", lambda e: e.activation(
                    out=tb[:, 1, :], in_=tb[:, 2, :], func=AF.Ln, bias=1.0))
                S.op("vector", lambda e: e.tensor_add(
                    out=tb[:, 2, :], in0=tb[:, 0, :], in1=tb[:, 1, :]))
                S.dma(tb[:, 3, :], u_f[m * 128:(m + 1) * 128, th * H:(th + 1) * H])
                S.op("vector", lambda e: e.tensor_mul(
                    out=tb[:, 0, :], in0=tb[:, 2, :], in1=tb[:, 3, :]))
                S.dma(delta[m * 128:(m + 1) * 128, th * H:(th + 1) * H],
                      tb[:, 2, :], queue="gpsimd")
                S.dma(du[m * 128:(m + 1) * 128, th * H:(th + 1) * H],
                      tb[:, 0, :], queue="gpsimd")


def build_lxa(layer):
    nc, stack, ctx = new_prog()
    S = Seq(ctx)
    with stack:
        io = lxa_io(nc, layer, lambda n, s, d, o: (dout if o else din)(nc, n, s, d))
        ctx.psA = ctx.psum("psA", [128, 4, 512], F32)
        ctx.psB = ctx.psum("psB", [128, 4, 512], F32)
        ctx.phase("lxa")
        emit_lxa(nc, ctx, S, io, layer)
        S.done()
    return nc



def scan_io(nc, layer, mk):
    NST = 64 if layer == 0 else 16
    a_dt = F32 if layer == 0 else BF16
    io = {}
    io["delta"] = mk("delta", (DI, T), F32, 0)
    io["du"] = mk("du", (DI, T), F32, 0)
    io["Bm"] = mk("Bm", (NST, T), F32, 0)
    io["Cm"] = mk("Cm", (NST, T), F32, 0)
    io["alog"] = mk("alog", (DI, NST), a_dt, 0)
    io["y"] = mk("y", (DI, T), F32, 1)
    return io


def emit_scan(nc, ctx, S, io, layer, dg0=0, dg1=16):
    NST = 64 if layer == 0 else 16
    a_dt = F32 if layer == 0 else BF16

    delta = io["delta"]
    du = io["du"]
    Bm = io["Bm"]
    Cm = io["Cm"]
    alog = io["alog"]
    y = io["y"]

    if True:
        dl = ctx.sbuf("dl", [128, T], F32)
        dul = ctx.sbuf("dul", [128, T], F32)
        yac = ctx.sbuf("yac", [128, T], F32)
        dat = ctx.sbuf("dat", [128, T], F32)
        dbut = ctx.sbuf("dbut", [128, T], F32)
        ht = ctx.sbuf("ht", [128, T], F32)
        hct = ctx.sbuf("hct", [128, T], F32)
        brow = ctx.sbuf("brow", [1, T], F32)
        crow = ctx.sbuf("crow", [1, T], F32)
        alr = ctx.sbuf("alr", [128, NST], a_dt)
        asb = ctx.sbuf("asb", [128, NST], F32)
        ones_row = ctx.sbuf("ones_row", [1, 128], F32)
        ps_b = ctx.psA
        ps_c = ctx.psB
        psbv = ps_b[:].rearrange("p a f -> p (a f)")
        pscv = ps_c[:].rearrange("p a f -> p (a f)")

        S.op("vector", lambda e: e.memset(ones_row[:], 1.0))
        for dg in range(dg0, dg1):
            S.dma(dl[:], delta[dg * 128:(dg + 1) * 128, :])
            S.dma(dul[:], du[dg * 128:(dg + 1) * 128, :])
            S.dma(alr[:], alog[dg * 128:(dg + 1) * 128, :])
            S.op("scalar", lambda e: e.activation(out=asb[:], in_=alr[:], func=AF.Exp))
            S.op("vector", lambda e: e.tensor_single_scalar(
                out=asb[:], in_=asb[:], scalar=-1.0, op=ALU.mult))
            S.op("vector", lambda e: e.memset(yac[:], 0.0))
            for n in range(NST):
                S.dma(brow[:], Bm[n:n + 1, :])
                S.dma(crow[:], Cm[n:n + 1, :])
                for fc in range(4):
                    S.op("tensor", lambda e, fc=fc: e.matmul(
                        out=ps_b[:, fc, :], lhsT=ones_row[:],
                        rhs=brow[:, fc * 512:(fc + 1) * 512], start=True, stop=True))
                for fc in range(4):
                    S.op("tensor", lambda e, fc=fc: e.matmul(
                        out=ps_c[:, fc, :], lhsT=ones_row[:],
                        rhs=crow[:, fc * 512:(fc + 1) * 512], start=True, stop=True))
                S.op("scalar", lambda e, n=n: e.activation(
                    out=dat[:], in_=dl[:], func=AF.Exp, scale=asb[:, n:n + 1]))
                S.op("vector", lambda e: e.tensor_mul(out=dbut[:], in0=dul[:], in1=psbv))
                S.op("vector", lambda e: e.tensor_tensor_scan(
                    out=ht[:], data0=dat[:], data1=dbut[:], initial=0.0,
                    op0=ALU.mult, op1=ALU.add))
                S.op("vector", lambda e: e.tensor_mul(out=hct[:], in0=ht[:], in1=pscv))
                S.op("vector", lambda e: e.tensor_add(out=yac[:], in0=yac[:], in1=hct[:]))
            S.dma(y[dg * 128:(dg + 1) * 128, :], yac[:], queue="gpsimd")


def build_scan(layer, dg0=0, dg1=16):
    nc, stack, ctx = new_prog()
    S = Seq(ctx)
    with stack:
        io = scan_io(nc, layer, lambda n, s, d, o: (dout if o else din)(nc, n, s, d))
        ctx.psA = ctx.psum("psA", [128, 4, 512], F32)
        ctx.psB = ctx.psum("psB", [128, 4, 512], F32)
        ctx.phase("scan")
        emit_scan(nc, ctx, S, io, layer, dg0, dg1)
        S.done()
    return nc



def l0out_io(nc, mk):
    io = {}
    io["y"] = mk("y", (DI, T), F32, 0)
    io["u_f"] = mk("u_f", (DI, T), F32, 0)
    io["z_bf"] = mk("z_bf", (DI, T), BF16, 0)
    io["d0"] = mk("d0", (DI,), F32, 0)
    io["wout_bf"] = mk("wout_bf", (DI, D), BF16, 0)
    io["xn_f"] = mk("xn_f", (D, T), F32, 0)
    io["xn_bf"] = mk("xn_bf", (D, T), BF16, 0)
    io["w1_bf"] = mk("w1_bf", (D, 4096), BF16, 0)
    io["b1"] = mk("b1", (4096,), F32, 0)
    io["w2_bf"] = mk("w2_bf", (4096, D), BF16, 0)
    io["b2"] = mk("b2", (D,), F32, 0)
    io["x"] = mk("x", (D, T), F32, 0)
    io["attn0"] = mk("attn0", (D, T), F32, 1)
    io["h_bf"] = mk("h_bf", (4096, T), BF16, 1)
    io["x1"] = mk("x1", (D, T), F32, 1)
    return io


def emit_l0out(nc, ctx, S, io):
    H = T // 2

    y = io["y"]
    u_f = io["u_f"]
    z_bf = io["z_bf"]
    d0 = io["d0"]
    wout = io["wout_bf"]
    xn_f = io["xn_f"]
    xn_bf = io["xn_bf"]
    w1 = io["w1_bf"]
    b1 = io["b1"]
    w2 = io["w2_bf"]
    b2 = io["b2"]
    x = io["x"]
    attn0 = io["attn0"]
    h_bf = io["h_bf"]
    x1 = io["x1"]

    KD = D // 128
    if True:
        gbuf = ctx.sbuf("gbuf", [128, 16, T], BF16)
        xnb = ctx.sbuf("xnb", [128, KD, T], BF16)
        t0 = ctx.sbuf("t0", [128, T], F32)
        t1 = ctx.sbuf("t1", [128, T], F32)
        dcol = ctx.sbuf("dcol", [128, 16], F32)
        zt = ctx.sbuf("zt", [128, T], BF16)
        wom = ctx.sbuf("wom", [128, 16, 128], BF16)
        w1m = ctx.sbuf("w1m", [128, KD, 128], BF16)
        b1c = ctx.sbuf("b1c", [128, 32], F32)
        b2c = ctx.sbuf("b2c", [128, 8], F32)
        tb = ctx.sbuf("tb", [128, 4, H], F32)
        hb = ctx.sbuf("hb", [128, H], BF16)
        hq = ctx.sbuf("hq", [128, 32, 512], BF16)
        w2m = ctx.sbuf("w2m", [128, 32, 128], BF16)
        knot = ctx.sbuf("knot", [128, 18], F32)
        ps_big = ctx.psA
        psb2 = ps_big[:, 0:2, :].rearrange("p a f -> p (a f)")
        psq = ps_big[:, 0:1, :].rearrange("p a f -> p (a f)")

        sgn = []
        for i, (vk, ck) in enumerate(PWL_KNOTS):
            S.op("vector", lambda e, i=i, vk=vk, ck=ck: e.memset(
                knot[:, i:i + 1], -abs(ck) * vk))
            sgn.append(1.0 if ck >= 0 else -1.0)
        S.op("vector", lambda e: e.memset(knot[:, 16:17], -1024.0))
        S.dma(dcol[:], d0.rearrange("(g p) -> p g", p=128), slow=True)
        S.dma(b1c[:], b1.rearrange("(g p) -> p g", p=128), slow=True)
        S.dma(b2c[:], b2.rearrange("(g p) -> p g", p=128), slow=True)
        S.op("vector", lambda e: e.tensor_single_scalar(
            out=b2c[:], in_=b2c[:], scalar=256.0, op=ALU.mult))

        for g in range(16):
            S.dma(t0[:], y[g * 128:(g + 1) * 128, :])
            S.dma(t1[:], u_f[g * 128:(g + 1) * 128, :])
            S.dma(zt[:], z_bf[g * 128:(g + 1) * 128, :])
            S.op("vector", lambda e, g=g: e.scalar_tensor_tensor(
                out=t0[:], in0=t1[:], scalar=dcol[:, g:g + 1], in1=t0[:],
                op0=ALU.mult, op1=ALU.add))
            S.op("scalar", lambda e: e.activation(out=t1[:], in_=zt[:], func=AF.Silu))
            S.op("vector", lambda e, g=g: e.tensor_mul(
                out=gbuf[:, g, :], in0=t0[:], in1=t1[:]))

        for m in range(KD):
            S.dma(wom[:], wout.rearrange("(k p) m -> p k m", p=128)[:, :, m * 128:(m + 1) * 128])
            for th in range(2):
                for k in range(16):
                    for j in range(2):
                        S.op("tensor", lambda e, k=k, th=th, j=j: e.matmul(
                            out=ps_big[:, j, :], lhsT=wom[:, k, :],
                            rhs=gbuf[:, k, th * H + j * 512:th * H + (j + 1) * 512],
                            start=(k == 0), stop=(k == 15)))
                S.dma(t0[:, 0:H], xn_f[m * 128:(m + 1) * 128, th * H:(th + 1) * H])
                S.op("vector", lambda e: e.tensor_add(
                    out=t1[:, 0:H], in0=t0[:, 0:H], in1=psb2))
                S.dma(attn0[m * 128:(m + 1) * 128, th * H:(th + 1) * H],
                      t1[:, 0:H], queue="gpsimd")

        for k in range(KD):
            S.dma(xnb[:, k, :], xn_bf[k * 128:(k + 1) * 128, :])
        for m in range(32):
            S.dma(w1m[:], w1.rearrange("(k p) m -> p k m", p=128)[:, :, m * 128:(m + 1) * 128])
            for th in range(2):
                for k in range(KD):
                    for j in range(2):
                        S.op("tensor", lambda e, k=k, th=th, j=j: e.matmul(
                            out=ps_big[:, j, :], lhsT=w1m[:, k, :],
                            rhs=xnb[:, k, th * H + j * 512:th * H + (j + 1) * 512],
                            start=(k == 0), stop=(k == KD - 1)))
                S.op("vector", lambda e, m=m: e.tensor_scalar(
                    out=tb[:, 0, :], in0=psb2, scalar1=b1c[:, m:m + 1],
                    scalar2=256.0, op0=ALU.add, op1=ALU.mult))
                S.op("vector", lambda e: e.tensor_scalar(
                    out=tb[:, 1, :], in0=tb[:, 0, :], scalar1=RNE_C,
                    scalar2=RNE_C, op0=ALU.add, op1=ALU.subtract))
                S.op("scalar", lambda e: e.activation(
                    out=tb[:, 2, :], in_=tb[:, 1, :], func=AF.Abs))
                S.op("vector", lambda e: e.tensor_single_scalar(
                    out=tb[:, 3, :], in_=tb[:, 2, :], scalar=PWL_S0, op=ALU.mult))
                for i, (vk, ck) in enumerate(PWL_KNOTS):
                    S.op("scalar", lambda e, i=i, ck=ck: e.activation(
                        out=tb[:, 0, :], in_=tb[:, 2, :], func=AF.Relu,
                        scale=abs(ck), bias=knot[:, i:i + 1]))
                    S.op("vector", lambda e, i=i: e.scalar_tensor_tensor(
                        out=tb[:, 3, :], in0=tb[:, 0, :], scalar=sgn[i],
                        in1=tb[:, 3, :], op0=ALU.mult, op1=ALU.add))
                S.op("scalar", lambda e: e.activation(
                    out=tb[:, 0, :], in_=tb[:, 2, :], func=AF.Relu,
                    bias=knot[:, 16:17]))
                S.op("vector", lambda e: e.tensor_scalar(
                    out=tb[:, 0, :], in0=tb[:, 0, :], scalar1=1.0,
                    scalar2=PWL_J256, op0=ALU.min, op1=ALU.mult))
                S.op("vector", lambda e: e.tensor_add(
                    out=tb[:, 3, :], in0=tb[:, 3, :], in1=tb[:, 0, :]))
                S.op("vector", lambda e: e.scalar_tensor_tensor(
                    out=tb[:, 0, :], in0=tb[:, 1, :], scalar=0.5,
                    in1=tb[:, 3, :], op0=ALU.mult, op1=ALU.add))
                S.op("vector", lambda e: e.tensor_scalar(
                    out=tb[:, 1, :], in0=tb[:, 0, :], scalar1=RNE_C,
                    scalar2=RNE_C, op0=ALU.add, op1=ALU.subtract))
                S.op("vector", lambda e: e.tensor_copy(out=hb[:], in_=tb[:, 1, :]))
                S.dma(h_bf[m * 128:(m + 1) * 128, th * H:(th + 1) * H],
                      hb[:], queue="gpsimd")

        for f4 in range(4):
            for k in range(32):
                S.dma(hq[:, k, :], h_bf[k * 128:(k + 1) * 128, f4 * 512:(f4 + 1) * 512])
            for m in range(KD):
                S.dma(w2m[:], w2.rearrange("(k p) m -> p k m", p=128)[:, :, m * 128:(m + 1) * 128])
                for k in range(32):
                    S.op("tensor", lambda e, k=k: e.matmul(
                        out=psq, lhsT=w2m[:, k, :], rhs=hq[:, k, :],
                        start=(k == 0), stop=(k == 31)))
                S.op("vector", lambda e, m=m: e.tensor_scalar(
                    out=tb[:, 0, 0:512], in0=psq, scalar1=b2c[:, m:m + 1],
                    scalar2=RNE_C, op0=ALU.add, op1=ALU.add))
                S.op("vector", lambda e: e.tensor_scalar(
                    out=tb[:, 1, 0:512], in0=tb[:, 0, 0:512], scalar1=RNE_C,
                    scalar2=1.0 / 256.0, op0=ALU.subtract, op1=ALU.mult))
                S.dma(tb[:, 2, 0:512], x[m * 128:(m + 1) * 128, f4 * 512:(f4 + 1) * 512])
                S.dma(tb[:, 3, 0:512], attn0[m * 128:(m + 1) * 128, f4 * 512:(f4 + 1) * 512])
                S.op("vector", lambda e: e.tensor_add(
                    out=tb[:, 0, 0:512], in0=tb[:, 2, 0:512], in1=tb[:, 3, 0:512]))
                S.op("vector", lambda e: e.tensor_add(
                    out=tb[:, 1, 0:512], in0=tb[:, 0, 0:512], in1=tb[:, 1, 0:512]))
                S.dma(x1[m * 128:(m + 1) * 128, f4 * 512:(f4 + 1) * 512],
                      tb[:, 1, 0:512], queue="gpsimd")


def build_l0out():
    nc, stack, ctx = new_prog()
    S = Seq(ctx)
    with stack:
        io = l0out_io(nc, lambda n, s, d, o: (dout if o else din)(nc, n, s, d))
        ctx.psA = ctx.psum("psA", [128, 4, 512], F32)
        ctx.psB = ctx.psum("psB", [128, 4, 512], F32)
        ctx.phase("l0out")
        emit_l0out(nc, ctx, S, io)
        S.done()
    return nc



def l1out_io(nc, mk):
    io = {}
    io["y"] = mk("y", (DI, T), F32, 0)
    io["u_f"] = mk("u_f", (DI, T), F32, 0)
    io["z_bf"] = mk("z_bf", (DI, T), BF16, 0)
    io["dskip"] = mk("dskip", (DI,), BF16, 0)
    io["opT"] = mk("opT", (DI, D), BF16, 0)
    io["xn_bf"] = mk("xn_bf", (D, T), BF16, 0)
    io["f1t_bf"] = mk("f1t_bf", (D, 4096), BF16, 0)
    io["w1s"] = mk("w1s", (4096,), F32, 0)
    io["f2t_bf"] = mk("f2t_bf", (D, 4096), BF16, 0)
    io["w2s"] = mk("w2s", (4096,), F32, 0)
    io["h1t_bf"] = mk("h1t_bf", (4096, 4096), BF16, 0)
    io["h1s"] = mk("h1s", (4096,), F32, 0)
    io["h2t_bf"] = mk("h2t_bf", (4096, 4096), BF16, 0)
    io["h2s"] = mk("h2s", (4096,), F32, 0)
    io["w3t_bf"] = mk("w3t_bf", (4096, D), BF16, 0)
    io["w3s"] = mk("w3s", (D,), F32, 0)
    io["x1"] = mk("x1", (D, T), F32, 0)
    io["attn1"] = mk("attn1", (D, T), BF16, 1)
    io["gf"] = mk("gf", (4096, T), BF16, 1)
    io["a1"] = mk("a1", (4096, T), BF16, 1)
    io["a2"] = mk("a2", (4096, T), BF16, 1)
    io["out"] = mk("out", (D, T), F32, 1)
    return io


def emit_l1out(nc, ctx, S, io):
    H = T // 2
    KD = D // 128

    y1 = io["y"]
    u1 = io["u_f"]
    z1 = io["z_bf"]
    dskip = io["dskip"]
    opt = io["opT"]
    xn1 = io["xn_bf"]
    f1t = io["f1t_bf"]
    w1s = io["w1s"]
    f2t = io["f2t_bf"]
    w2s = io["w2s"]
    h1t = io["h1t_bf"]
    h1s = io["h1s"]
    h2t = io["h2t_bf"]
    h2s = io["h2s"]
    w3t = io["w3t_bf"]
    w3s = io["w3s"]
    x1 = io["x1"]
    attn1 = io["attn1"]
    gf = io["gf"]
    a1 = io["a1"]
    a2 = io["a2"]
    out = io["out"]

    if True:
        gbuf = ctx.sbuf("gbuf", [128, 16, T], BF16)
        t0 = ctx.sbuf("t0", [128, T], F32)
        t1 = ctx.sbuf("t1", [128, T], F32)
        zt = ctx.sbuf("zt", [128, T], BF16)
        dcr = ctx.sbuf("dcr", [128, 16], BF16)
        dcol = ctx.sbuf("dcol", [128, 16], F32)
        wm = ctx.sbuf("wm", [128, 16, 128], BF16)
        wm2 = ctx.sbuf("wm2", [128, 16, 128], BF16)
        ab = ctx.sbuf("ab", [128, H], BF16)
        s1c = ctx.sbuf("s1c", [128, 32], F32)
        s2c = ctx.sbuf("s2c", [128, 32], F32)
        sh1c = ctx.sbuf("sh1c", [128, 32], F32)
        sh2c = ctx.sbuf("sh2c", [128, 32], F32)
        s3c = ctx.sbuf("s3c", [128, 8], F32)
        rq = ctx.sbuf("rq", [128, 32, 512], BF16)
        w32 = ctx.sbuf("w32", [128, 32, 128], BF16)
        ps_big = ctx.psA
        psb2 = ps_big[:, 0:2, :].rearrange("p a f -> p (a f)")
        psc2 = ps_big[:, 2:4, :].rearrange("p a f -> p (a f)")
        psq = ps_big[:, 0:1, :].rearrange("p a f -> p (a f)")

        S.dma(dcr[:], dskip.rearrange("(g p) -> p g", p=128), slow=True)
        S.op("vector", lambda e: e.tensor_copy(out=dcol[:], in_=dcr[:]))
        S.dma(s1c[:], w1s.rearrange("(g p) -> p g", p=128), slow=True)
        S.dma(s2c[:], w2s.rearrange("(g p) -> p g", p=128), slow=True)
        S.dma(sh1c[:], h1s.rearrange("(g p) -> p g", p=128), slow=True)
        S.dma(sh2c[:], h2s.rearrange("(g p) -> p g", p=128), slow=True)
        S.dma(s3c[:], w3s.rearrange("(g p) -> p g", p=128), slow=True)

        for g in range(16):
            S.dma(t0[:], y1[g * 128:(g + 1) * 128, :])
            S.dma(t1[:], u1[g * 128:(g + 1) * 128, :])
            S.dma(zt[:], z1[g * 128:(g + 1) * 128, :])
            S.op("vector", lambda e, g=g: e.scalar_tensor_tensor(
                out=t0[:], in0=t1[:], scalar=dcol[:, g:g + 1], in1=t0[:],
                op0=ALU.mult, op1=ALU.add))
            S.op("scalar", lambda e: e.activation(out=t1[:], in_=zt[:], func=AF.Silu))
            S.op("vector", lambda e, g=g: e.tensor_mul(
                out=gbuf[:, g, :], in0=t0[:], in1=t1[:]))

        for m in range(KD):
            S.dma(wm[:], opt.rearrange("(k p) m -> p k m", p=128)[:, :, m * 128:(m + 1) * 128])
            for th in range(2):
                for k in range(16):
                    for j in range(2):
                        S.op("tensor", lambda e, k=k, th=th, j=j: e.matmul(
                            out=ps_big[:, j, :], lhsT=wm[:, k, :],
                            rhs=gbuf[:, k, th * H + j * 512:th * H + (j + 1) * 512],
                            start=(k == 0), stop=(k == 15)))
                S.op("vector", lambda e: e.tensor_copy(out=ab[:], in_=psb2))
                S.dma(attn1[m * 128:(m + 1) * 128, th * H:(th + 1) * H],
                      ab[:], queue="gpsimd")

        for k in range(KD):
            S.dma(gbuf[:, k, :], xn1[k * 128:(k + 1) * 128, :])
        for m in range(32):
            S.dma(wm[:, 0:KD, :], f1t.rearrange("(k p) m -> p k m", p=128)[:, :, m * 128:(m + 1) * 128])
            S.dma(wm2[:, 0:KD, :], f2t.rearrange("(k p) m -> p k m", p=128)[:, :, m * 128:(m + 1) * 128])
            for th in range(2):
                for k in range(KD):
                    for j in range(2):
                        S.op("tensor", lambda e, k=k, th=th, j=j: e.matmul(
                            out=ps_big[:, j, :], lhsT=wm[:, k, :],
                            rhs=gbuf[:, k, th * H + j * 512:th * H + (j + 1) * 512],
                            start=(k == 0), stop=(k == KD - 1)))
                for k in range(KD):
                    for j in range(2):
                        S.op("tensor", lambda e, k=k, th=th, j=j: e.matmul(
                            out=ps_big[:, 2 + j, :], lhsT=wm2[:, k, :],
                            rhs=gbuf[:, k, th * H + j * 512:th * H + (j + 1) * 512],
                            start=(k == 0), stop=(k == KD - 1)))
                S.op("scalar", lambda e, m=m: e.activation(
                    out=t0[:, 0:H], in_=psb2, func=AF.Silu, scale=s1c[:, m:m + 1]))
                S.op("scalar", lambda e, m=m: e.activation(
                    out=t1[:, 0:H], in_=psc2, func=AF.Identity, scale=s2c[:, m:m + 1]))
                S.op("vector", lambda e: e.tensor_mul(
                    out=ab[:], in0=t0[:, 0:H], in1=t1[:, 0:H]))
                S.dma(gf[m * 128:(m + 1) * 128, th * H:(th + 1) * H],
                      ab[:], queue="gpsimd")

        for (src, dst, wmat, scol) in ((gf, a1, h1t, sh1c), (a1, a2, h2t, sh2c)):
            for f4 in range(4):
                for k in range(32):
                    S.dma(rq[:, k, :], src[k * 128:(k + 1) * 128,
                                           f4 * 512:(f4 + 1) * 512])
                for m in range(32):
                    S.dma(w32[:], wmat.rearrange("(k p) m -> p k m", p=128)[:, :, m * 128:(m + 1) * 128])
                    for k in range(32):
                        S.op("tensor", lambda e, k=k: e.matmul(
                            out=psq, lhsT=w32[:, k, :], rhs=rq[:, k, :],
                            start=(k == 0), stop=(k == 31)))
                    S.op("scalar", lambda e, m=m, scol=scol: e.activation(
                        out=ab[:, 0:512], in_=psq, func=AF.Gelu,
                        scale=scol[:, m:m + 1]))
                    S.dma(dst[m * 128:(m + 1) * 128, f4 * 512:(f4 + 1) * 512],
                          ab[:, 0:512], queue="gpsimd")

        for f4 in range(4):
            for k in range(32):
                S.dma(rq[:, k, :], a2[k * 128:(k + 1) * 128, f4 * 512:(f4 + 1) * 512])
            for m in range(KD):
                S.dma(w32[:], w3t.rearrange("(k p) m -> p k m", p=128)[:, :, m * 128:(m + 1) * 128])
                for k in range(32):
                    S.op("tensor", lambda e, k=k: e.matmul(
                        out=psq, lhsT=w32[:, k, :], rhs=rq[:, k, :],
                        start=(k == 0), stop=(k == 31)))
                S.op("scalar", lambda e, m=m: e.activation(
                    out=t0[:, 0:512], in_=psq, func=AF.Identity,
                    scale=s3c[:, m:m + 1]))
                S.dma(t1[:, 0:512], x1[m * 128:(m + 1) * 128, f4 * 512:(f4 + 1) * 512])
                S.dma(ab[:, 0:512], attn1[m * 128:(m + 1) * 128, f4 * 512:(f4 + 1) * 512])
                S.op("vector", lambda e: e.tensor_add(
                    out=t1[:, 0:512], in0=t1[:, 0:512], in1=ab[:, 0:512]))
                S.op("vector", lambda e: e.tensor_add(
                    out=t0[:, 0:512], in0=t0[:, 0:512], in1=t1[:, 0:512]))
                S.dma(out[m * 128:(m + 1) * 128, f4 * 512:(f4 + 1) * 512],
                      t0[:, 0:512], queue="gpsimd")


def build_l1out():
    nc, stack, ctx = new_prog()
    S = Seq(ctx)
    with stack:
        io = l1out_io(nc, lambda n, s, d, o: (dout if o else din)(nc, n, s, d))
        ctx.psA = ctx.psum("psA", [128, 4, 512], F32)
        ctx.psB = ctx.psum("psB", [128, 4, 512], F32)
        ctx.phase("l1out")
        emit_l1out(nc, ctx, S, io)
        S.done()
    return nc


_PROGS = {}


def _prog(name, builder):
    if name not in _PROGS:
        _PROGS[name] = builder()
    return _PROGS[name]


def _run(name, builder, maps):
    res = run_bass_kernel_spmd(_prog(name, builder), maps, core_ids=list(range(8)))
    return res.results


def kernel(**inp):
    asc = np.ascontiguousarray
    g = {k: np.asarray(v) for k, v in inp.items()}
    x = g["L0_attn_x"]
    xfm = [asc(x[c].T) for c in range(8)]

    f1T = asc(g["L1_ffn_w1_w"].T)
    f2T = asc(g["L1_ffn_w2_w"].T)
    h1T = asc(g["L1_ffn_h1_w"].T)
    h2T = asc(g["L1_ffn_h2_w"].T)
    w3T = asc(g["L1_ffn_w3_w"].T)
    ipT = asc(g["L1_attn_in_proj_w"].T)
    opT = asc(g["L1_attn_out_proj_w"].T)
    xpT = asc(g["L1_attn_x_proj_w"].T)
    dpT = asc(g["L1_attn_dt_proj_w"].T)
    cw0 = asc(g["L0_attn_conv_w"].reshape(DI, 4))
    cw1 = asc(g["L1_attn_conv_w"].reshape(DI, 4))

    if FUSED:
        return kernel_fused(g, xfm, (f1T, f2T, h1T, h2T, w3T, ipT, opT, xpT,
                                     dpT, cw0, cw1))

    pm = {"win": g["L0_attn_W_in"], "wout": g["L0_attn_W_out"],
          "w1": g["L0_ffn_w1"], "w2": g["L0_ffn_w2"],
          "wxp": g["L0_attn_W_xproj"],
          "f1t": f1T, "f2t": f2T, "h1t": h1T, "h2t": h2T, "w3t": w3T}
    W = _run("prep", build_prep, [pm] * 8)[0]

    mA = [{"x": xfm[c], "lnw": g["L0_attn_ln_w"], "lnb": g["L0_attn_ln_b"],
           "win_bf": W["win_bf"], "convw": cw0, "convb": g["L0_attn_conv_b"],
           "wxp_bf": W["wxp_bf"], "wdt": g["L0_attn_W_dt"],
           "bdt": g["L0_attn_b_dt"]} for c in range(8)]
    rA = _run("l0a", lambda: build_lxa(0), mA)

    mS = [{"delta": rA[c]["delta"], "du": rA[c]["du"], "Bm": rA[c]["B0"],
           "Cm": rA[c]["C0"], "alog": g["L0_attn_A_log"]} for c in range(8)]
    rS = _run("scan0", lambda: build_scan(0), mS)

    mO = [{"y": rS[c]["y"], "u_f": rA[c]["u_f"], "z_bf": rA[c]["z_bf"],
           "d0": g["L0_attn_D"], "wout_bf": W["wout_bf"], "xn_f": rA[c]["xn_f"],
           "xn_bf": rA[c]["xn_bf"], "w1_bf": W["w1_bf"], "b1": g["L0_ffn_b1"],
           "w2_bf": W["w2_bf"], "b2": g["L0_ffn_b2"], "x": xfm[c]}
          for c in range(8)]
    rO = _run("l0out", build_l0out, mO)

    mA1 = [{"x": rO[c]["x1"], "win_bf": ipT, "convw": cw1,
            "convb": g["L1_attn_conv_b"], "wxp_bf": xpT, "wdt": dpT,
            "bdt": g["L1_attn_dt_proj_b"]} for c in range(8)]
    rA1 = _run("l1a", lambda: build_lxa(1), mA1)

    mS1 = [{"delta": rA1[c]["delta"], "du": rA1[c]["du"], "Bm": rA1[c]["B0"],
            "Cm": rA1[c]["C0"], "alog": g["L1_attn_A_log"]} for c in range(8)]
    rS1 = _run("scan1", lambda: build_scan(1), mS1)

    mO1 = [{"y": rS1[c]["y"], "u_f": rA1[c]["u_f"], "z_bf": rA1[c]["z_bf"],
            "dskip": g["L1_attn_D_skip"], "opT": opT, "xn_bf": rA1[c]["xn_bf"],
            "f1t_bf": W["f1t_bf"], "w1s": g["L1_ffn_w1_s"],
            "f2t_bf": W["f2t_bf"], "w2s": g["L1_ffn_w2_s"],
            "h1t_bf": W["h1t_bf"], "h1s": g["L1_ffn_h1_s"],
            "h2t_bf": W["h2t_bf"], "h2s": g["L1_ffn_h2_s"],
            "w3t_bf": W["w3t_bf"], "w3s": g["L1_ffn_w3_s"],
            "x1": rO[c]["x1"]} for c in range(8)]
    rO1 = _run("l1out", build_l1out, mO1)

    return np.stack([asc(rO1[c]["out"].T) for c in range(8)], axis=0)



def build_fused():
    nc, stack, ctx = new_prog()
    S = Seq(ctx)
    with stack:
        ctx.psA = ctx.psum("psA", [128, 4, 512], F32)
        ctx.psB = ctx.psum("psB", [128, 4, 512], F32)

        def D_(n, s, d):
            return nc.dram_tensor(n, list(s), d, kind="Internal").ap()

        iop = prep_io(nc, lambda n, s, d, o: D_(n, s, d) if o else din(nc, n, s, d))
        ctx.phase("prep")
        emit_prep(nc, ctx, S, iop)

        def mk0(n, s, d, o):
            if n in ("win_bf", "wxp_bf"):
                return iop[n]
            if o:
                return D_("l0_" + n, s, d)
            return din(nc, "l0_" + n, s, d)
        io0 = lxa_io(nc, 0, mk0)
        ctx.phase("l0a")
        emit_lxa(nc, ctx, S, io0, 0)

        y0 = D_("y0", (DI, T), F32)
        ios0 = {"delta": io0["delta"], "du": io0["du"], "Bm": io0["B0"],
                "Cm": io0["C0"], "alog": din(nc, "alog0", (DI, 64), F32),
                "y": y0}
        ctx.phase("scan0")
        emit_scan(nc, ctx, S, ios0, 0)

        def mko0(n, s, d, o):
            m = {"y": y0, "u_f": io0["u_f"], "z_bf": io0["z_bf"],
                 "wout_bf": iop["wout_bf"], "xn_f": io0["xn_f"],
                 "xn_bf": io0["xn_bf"], "w1_bf": iop["w1_bf"],
                 "w2_bf": iop["w2_bf"], "x": io0["x"]}
            if n in m:
                return m[n]
            if o:
                return D_("o0_" + n, s, d)
            return din(nc, "o0_" + n, s, d)
        ioo0 = l0out_io(nc, mko0)
        ctx.phase("l0out")
        emit_l0out(nc, ctx, S, ioo0)

        def mk1(n, s, d, o):
            if n == "x":
                return ioo0["x1"]
            if o:
                return D_("l1_" + n, s, d)
            return din(nc, "l1_" + n, s, d)
        io1 = lxa_io(nc, 1, mk1)
        ctx.phase("l1a")
        emit_lxa(nc, ctx, S, io1, 1)

        y1 = D_("y1", (DI, T), F32)
        ios1 = {"delta": io1["delta"], "du": io1["du"], "Bm": io1["B0"],
                "Cm": io1["C0"], "alog": din(nc, "alog1", (DI, 16), BF16),
                "y": y1}
        ctx.phase("scan1")
        emit_scan(nc, ctx, S, ios1, 1)

        def mko1(n, s, d, o):
            m = {"y": y1, "u_f": io1["u_f"], "z_bf": io1["z_bf"],
                 "xn_bf": io1["xn_bf"], "f1t_bf": iop["f1t_bf"],
                 "f2t_bf": iop["f2t_bf"], "h1t_bf": iop["h1t_bf"],
                 "h2t_bf": iop["h2t_bf"], "w3t_bf": iop["w3t_bf"],
                 "x1": ioo0["x1"]}
            if n in m:
                return m[n]
            if n == "out":
                return dout(nc, "out", s, d)
            if o:
                return D_("o1_" + n, s, d)
            return din(nc, "o1_" + n, s, d)
        ioo1 = l1out_io(nc, mko1)
        ctx.phase("l1out")
        emit_l1out(nc, ctx, S, ioo1)
        S.done()
    return nc


FUSED = True


def kernel_fused(g, xfm, hosted):
    (f1T, f2T, h1T, h2T, w3T, ipT, opT, xpT, dpT, cw0, cw1) = hosted
    base = {"win": g["L0_attn_W_in"], "wout": g["L0_attn_W_out"],
            "w1": g["L0_ffn_w1"], "w2": g["L0_ffn_w2"],
            "wxp": g["L0_attn_W_xproj"],
            "f1t": f1T, "f2t": f2T, "h1t": h1T, "h2t": h2T, "w3t": w3T,
            "l0_lnw": g["L0_attn_ln_w"], "l0_lnb": g["L0_attn_ln_b"],
            "l0_convw": cw0, "l0_convb": g["L0_attn_conv_b"],
            "l0_wdt": g["L0_attn_W_dt"], "l0_bdt": g["L0_attn_b_dt"],
            "alog0": g["L0_attn_A_log"],
            "o0_d0": g["L0_attn_D"], "o0_b1": g["L0_ffn_b1"],
            "o0_b2": g["L0_ffn_b2"],
            "l1_win_bf": ipT, "l1_convw": cw1, "l1_convb": g["L1_attn_conv_b"],
            "l1_wxp_bf": xpT, "l1_wdt": dpT, "l1_bdt": g["L1_attn_dt_proj_b"],
            "alog1": g["L1_attn_A_log"],
            "o1_dskip": g["L1_attn_D_skip"], "o1_opT": opT,
            "o1_w1s": g["L1_ffn_w1_s"], "o1_w2s": g["L1_ffn_w2_s"],
            "o1_h1s": g["L1_ffn_h1_s"], "o1_h2s": g["L1_ffn_h2_s"],
            "o1_w3s": g["L1_ffn_w3_s"]}
    maps = [dict(base, l0_x=xfm[c]) for c in range(8)]
    res = _run("fused", build_fused, maps)
    return np.stack([np.ascontiguousarray(res[c]["out"].T) for c in range(8)],
                    axis=0)


# version 04
# speedup vs baseline: 1.4263x; 1.4226x over previous
import math
import numpy as np
import ml_dtypes
from contextlib import ExitStack

import concourse.bass as bass
import concourse.mybir as mybir
from concourse.bass_utils import run_bass_kernel_spmd

F32 = mybir.dt.float32
BF16 = mybir.dt.bfloat16
F16 = mybir.dt.float16
ALU = mybir.AluOpType
AF = mybir.ActivationFunctionType

BF = ml_dtypes.bfloat16

T = 2048
D = 1024
DI = 2048
RNE_C = 12582912.0

def _pwl_constants():
    xs = [0.25 * k for k in range(17)]
    ev = [0.5 * x * (1.0 + math.erf(x / math.sqrt(2.0))) - x / 2.0 for x in xs]
    s = [(ev[k + 1] - ev[k]) / 0.25 for k in range(16)]
    knots = []
    for k in range(1, 16):
        knots.append((64.0 * k, s[k] - s[k - 1]))
    knots.append((1024.0, 0.5 - s[15]))
    j256 = 256.0 * (4.0 - (0.5 * 4.0 * (1.0 + math.erf(4.0 / math.sqrt(2.0)))))
    return s[0], knots, j256


PWL_S0, PWL_KNOTS, PWL_J256 = _pwl_constants()


class Ctx:
    ENGINES = ("sync", "vector", "scalar", "tensor", "gpsimd")

    def __init__(self, nc, stack):
        self.nc = nc
        self.stack = stack
        self.ops = {e: [] for e in self.ENGINES}
        self.sems = {}
        self.cnt = {}
        self.bump = 16512
        self.phase_tag = "g"
        self.psA = None
        self.psB = None

    def sem(self, name):
        if name not in self.sems:
            self.sems[name] = self.stack.enter_context(self.nc.semaphore(name))
            self.cnt[name] = 0
        return self.sems[name]

    def fire(self, name, amount):
        h = self.sem(name)
        self.cnt[name] += amount
        return (h, amount)

    def val(self, name):
        return self.cnt.get(name, 0)

    def emit(self, engine, fn, waits=(), fire=None):
        wl = [(self.sem(n), v) for (n, v) in waits if v > 0]

        def runner(eng):
            for h, v in wl:
                eng.wait_ge(h, v)
            ins = fn(eng)
            if fire is not None:
                ins.then_inc(fire[0], fire[1])
        self.ops[engine].append(runner)

    def dma(self, queue, out, in_, waits=(), sem="dma", transpose=False):
        f = self.fire(sem, 16)
        if transpose:
            self.emit(queue, lambda e: e.dma_start_transpose(out=out, in_=in_),
                      waits, f)
        else:
            self.emit(queue, lambda e: e.dma_start(out=out, in_=in_), waits, f)
        return self.cnt[sem]

    SBUF_BASE = 16512
    SBUF_LIMIT = 229344

    def phase(self, tag):
        self.phase_tag = tag
        self.bump = self.SBUF_BASE

    def sbuf(self, name, shape, dt):
        per = 1
        for s in shape[1:]:
            per *= s
        per *= {F32: 4, BF16: 2, F16: 2}[dt]
        off = (self.bump + 31) & ~31
        self.bump = off + per
        assert self.bump <= self.SBUF_LIMIT, (
            f"sbuf overflow in phase {self.phase_tag}: {self.bump}")
        h = self.nc.alloc_sbuf_tensor_at(
            f"{self.phase_tag}_{name}", list(shape), dt, offset=off)
        return h

    def psum(self, name, shape, dt=F32):
        return self.stack.enter_context(self.nc.psum_tensor(name, shape, dt))

    def finalize(self, final_waits):
        ops = self.ops
        wl = [(self.sems[n], v) for (n, v) in final_waits]
        with self.nc.Block() as block:
            @block.sync
            def _(eng):
                for f in ops["sync"]:
                    f(eng)
                for h, v in wl:
                    eng.wait_ge(h, v)

            @block.vector
            def _(eng):
                for f in ops["vector"]:
                    f(eng)

            @block.scalar
            def _(eng):
                for f in ops["scalar"]:
                    f(eng)

            @block.tensor
            def _(eng):
                for f in ops["tensor"]:
                    f(eng)

            @block.gpsimd
            def _(eng):
                for f in ops["gpsimd"]:
                    f(eng)


def new_prog():
    nc = bass.Bass("TRN2", target_bir_lowering=False)
    stack = ExitStack()
    ctx = Ctx(nc, stack)
    return nc, stack, ctx


def din(nc, name, shape, dt=F32):
    return nc.dram_tensor(name, list(shape), dt, kind="ExternalInput").ap()


def dout(nc, name, shape, dt=F32):
    return nc.dram_tensor(name, list(shape), dt, kind="ExternalOutput").ap()



def _seq(ctx, engine, fn):
    v = ctx.val("q")
    ctx.emit(engine, fn, waits=(("q", v),), fire=ctx.fire("q", 1))


def _seqdma(ctx, queue, out, in_, transpose=False, slow=False):
    v = ctx.val("q")
    f = ctx.fire("q", 16)
    if transpose:
        ctx.emit(queue, lambda e: e.dma_start_transpose(out=out, in_=in_),
                 waits=(("q", v),), fire=f)
    elif slow:
        ctx.emit(queue, lambda e: e.dma_start(out=out, in_=in_,
                                              allow_slow_non_contiguous=True),
                 waits=(("q", v),), fire=f)
    else:
        ctx.emit(queue, lambda e: e.dma_start(out=out, in_=in_),
                 waits=(("q", v),), fire=f)


class Seq:

    def __init__(self, ctx):
        self.ctx = ctx

    def op(self, engine, fn):
        _seq(self.ctx, engine, fn)

    def dma(self, out, in_, queue="sync", transpose=False, slow=False):
        _seqdma(self.ctx, queue, out, in_, transpose, slow)

    def done(self):
        self.ctx.finalize([("q", self.ctx.val("q"))])


class ConstCols:

    def __init__(self, S, values):
        self.tile = S.ctx.sbuf("cstc", [128, max(1, len(values))], F32)
        self.idx = {}
        for i, v in enumerate(values):
            self.idx[v] = i
            S.op("vector", lambda e, i=i, v=v: e.memset(self.tile[:, i:i + 1], v))

    def ap(self, v):
        i = self.idx[v]
        return self.tile[:, i:i + 1]


CASTS = [
    ("win", "win_bf", D, 4096),
    ("wout", "wout_bf", DI, D),
    ("w1", "w1_bf", D, 4096),
    ("w2", "w2_bf", 4096, D),
    ("wxp", "wxp_bf", DI, 192),
    ("f1t", "f1t_bf", D, 4096),
    ("f2t", "f2t_bf", D, 4096),
    ("h1t", "h1t_bf", 4096, 4096),
    ("h2t", "h2t_bf", 4096, 4096),
    ("w3t", "w3t_bf", 4096, D),
]


def prep_io(nc, mk):
    io = {}
    for (iname, oname, rows, cols) in CASTS:
        io[iname] = mk(iname, (rows, cols), F32, 0)
        io[oname] = mk(oname, (rows, cols), BF16, 1)
    return io


def emit_prep(nc, ctx, S, io):
    tin = ctx.sbuf("tin", [128, 4096], F32)
    tout = ctx.sbuf("tout", [128, 4096], BF16)
    for (iname, oname, rows, cols) in CASTS:
        src = io[iname]
        dst = io[oname]
        for i in range(rows // 128):
            S.dma(tin[:, :cols], src[i * 128:(i + 1) * 128, :])
            S.op("vector", lambda e, c=cols: e.tensor_copy(
                out=tout[:, :c], in_=tin[:, :c]))
            S.dma(dst[i * 128:(i + 1) * 128, :], tout[:, :cols],
                  queue="gpsimd")


def build_prep():
    nc, stack, ctx = new_prog()
    with stack:
        S = Seq(ctx)
        io = prep_io(nc, lambda n, s, d, o: (dout if o else din)(nc, n, s, d))
        ctx.phase("prep")
        emit_prep(nc, ctx, S, io)
        S.done()
    return nc



LXA_IO = ["x", "win_bf", "convw", "convb", "wxp_bf", "wdt", "bdt",
          "xn_bf", "xi_bf", "z_bf", "u_f", "u_bf", "B0", "C0", "delta", "du"]


def lxa_io(nc, layer, mk):
    NST = 64 if layer == 0 else 16
    MX = 192 if layer == 0 else 96
    fl = F32 if layer == 0 else BF16
    io = {}
    io["x"] = mk("x", (D, T), F32, 0)
    io["win_bf"] = mk("win_bf", (D, 4096), BF16, 0)
    io["convw"] = mk("convw", (DI, 4), fl, 0)
    io["convb"] = mk("convb", (DI,), fl, 0)
    io["wxp_bf"] = mk("wxp_bf", (DI, MX), BF16, 0)
    io["wdt"] = mk("wdt", (64, DI), fl, 0)
    io["bdt"] = mk("bdt", (DI,), fl, 0)
    if layer == 0:
        io["lnw"] = mk("lnw", (D,), F32, 0)
        io["lnb"] = mk("lnb", (D,), F32, 0)
        io["xn_f"] = mk("xn_f", (D, T), F32, 1)
    io["xn_bf"] = mk("xn_bf", (D, T), BF16, 1)
    io["xi_bf"] = mk("xi_bf", (DI, T), BF16, 1)
    io["z_bf"] = mk("z_bf", (DI, T), BF16, 1)
    io["u_f"] = mk("u_f", (DI, T), F32, 1)
    io["u_bf"] = mk("u_bf", (DI, T), BF16, 1)
    io["B0"] = mk("B0", (NST, T), F32, 1)
    io["C0"] = mk("C0", (NST, T), F32, 1)
    io["delta"] = mk("delta", (DI, T), F32, 1)
    io["du"] = mk("du", (DI, T), F32, 1)
    io["du_f16"] = mk("du_f16", (DI, T), F16, 1)
    return io


def emit_lxa(nc, ctx, S, io, layer):
    KD = D // 128
    H = T // 2
    NST = 64 if layer == 0 else 16
    MX = 192 if layer == 0 else 96

    x = io["x"]
    win = io["win_bf"]
    convw = io["convw"]
    convb = io["convb"]
    wxp = io["wxp_bf"]
    wdt = io["wdt"]
    bdt = io["bdt"]
    if layer == 0:
        lnw = io["lnw"]
        lnb = io["lnb"]
        xn_f = io["xn_f"]
    xn_bf = io["xn_bf"]
    xi_bf = io["xi_bf"]
    z_bf = io["z_bf"]
    u_f = io["u_f"]
    u_bf = io["u_bf"]
    B0 = io["B0"]
    C0 = io["C0"]
    delta = io["delta"]
    du = io["du"]
    du_f16 = io["du_f16"]

    if True:
        xt = ctx.sbuf("xt", [128, H], F32)
        sq = ctx.sbuf("sq", [128, H], F32)
        ones_col = ctx.sbuf("ones_col", [128, 1], F32)
        ones_row = ctx.sbuf("ones_row", [1, 128], F32)
        rows = ctx.sbuf("rows", [1, 4, T], F32)
        rep_a = ctx.sbuf("rep_a", [128, T], F32)
        rep_b = ctx.sbuf("rep_b", [128, T], F32)
        xbf = ctx.sbuf("xbf", [128, H], BF16)
        xn2 = ctx.sbuf("xn2", [128, KD, T], BF16)
        wm = ctx.sbuf("wm", [128, KD, 128], BF16)
        ebf = ctx.sbuf("ebf", [128, H], BF16)
        ct = ctx.sbuf("ct", [128, T + 3], BF16)
        cwr = ctx.sbuf("cwr", [128, 16, 4], F32 if layer == 0 else BF16)
        cbr = ctx.sbuf("cbr", [128, 16], F32 if layer == 0 else BF16)
        cw = ctx.sbuf("cw", [128, 16, 4], F32)
        cb = ctx.sbuf("cb", [128, 16], F32)
        ua = ctx.sbuf("ua", [128, T], F32)
        ub = ctx.sbuf("ub", [128, T], BF16)
        wxpt = ctx.sbuf("wxpt", [128, 16, MX], BF16)
        uh = ctx.sbuf("uh", [128, H], BF16)
        dtr = ctx.sbuf("dtr", [64, T], BF16)
        wdtf = ctx.sbuf("wdtf", [64, DI], F32 if layer == 0 else BF16)
        wdtb = ctx.sbuf("wdtb", [64, DI], BF16)
        bdtr = ctx.sbuf("bdtr", [128, 16], F32 if layer == 0 else BF16)
        bdtc = ctx.sbuf("bdtc", [128, 16], F32)
        tb = ctx.sbuf("tb", [128, 4, H], F32)
        duh = ctx.sbuf("duh", [128, H], F16)
        lncol = ctx.sbuf("lncol", [128, 2, KD], F32)
        ps_big = ctx.psA
        ps_sm = ctx.psB[0:64]

        MU, SQS, RSTD, SCR = range(4)
        psb2 = ps_big[:, 0:2, :].rearrange("p a f -> p (a f)")

        S.op("vector", lambda e: e.memset(ones_col[:], 1.0))
        S.op("vector", lambda e: e.memset(ones_row[:], 1.0))
        cst = ConstCols(S, [1e-5])

        def stats_pass(src_ap, xn_mode):
            for th in range(2):
                for k in range(KD):
                    S.dma(xt[:], src_ap[k * 128:(k + 1) * 128, th * H:(th + 1) * H])
                    if xn_mode:
                        S.op("vector", lambda e, th=th: e.tensor_sub(
                            out=xt[:], in0=xt[:], in1=rep_a[:, th * H:(th + 1) * H]))
                        S.op("vector", lambda e, th=th: e.tensor_mul(
                            out=xt[:], in0=xt[:], in1=rep_b[:, th * H:(th + 1) * H]))
                        if layer == 0:
                            S.dma(xn_f[k * 128:(k + 1) * 128, th * H:(th + 1) * H],
                                  xt[:], queue="gpsimd")
                        S.op("vector", lambda e: e.tensor_copy(out=xbf[:], in_=xt[:]))
                        S.dma(xn_bf[k * 128:(k + 1) * 128, th * H:(th + 1) * H],
                              xbf[:], queue="gpsimd")
                    if (not xn_mode) or layer == 0:
                        S.op("scalar", lambda e: e.activation(
                            out=sq[:], in_=xt[:], func=AF.Square))
                        for which in range(2):
                            st = xt if which == 0 else sq
                            for fc in range(2):
                                S.op("tensor", lambda e, w=which, fc=fc, k=k, st=st:
                                     e.matmul(out=ps_sm[0:1, w * 2 + fc, :],
                                              lhsT=ones_col[:],
                                              rhs=st[:, fc * 512:(fc + 1) * 512],
                                              start=(k == 0), stop=(k == KD - 1)))
                if (not xn_mode) or layer == 0:
                    for which, slot in ((0, MU), (1, SQS)):
                        for fc in range(2):
                            S.op("scalar", lambda e, w=which, s=slot, fc=fc, th=th:
                                 e.activation(
                                     out=rows[:, s, th * H + fc * 512:
                                              th * H + (fc + 1) * 512],
                                     in_=ps_sm[0:1, w * 2 + fc, :],
                                     func=AF.Identity, scale=1.0 / D))

        def row_math():
            S.op("vector", lambda e: e.tensor_mul(
                out=rows[:, SCR, :], in0=rows[:, MU, :], in1=rows[:, MU, :]))
            S.op("vector", lambda e: e.tensor_sub(
                out=rows[:, SCR, :], in0=rows[:, SQS, :], in1=rows[:, SCR, :]))
            S.op("scalar", lambda e: e.activation(
                out=rows[:, SCR, :], in_=rows[:, SCR, :], func=AF.Sqrt,
                bias=cst.ap(1e-5)[0:1, :]))
            S.op("vector", lambda e: e.reciprocal(
                out=rows[:, RSTD, :], in_=rows[:, SCR, :]))

        def rep_pass():
            for slot, dst in ((MU, rep_a), (RSTD, rep_b)):
                for fc in range(4):
                    S.op("tensor", lambda e, s=slot, fc=fc: e.matmul(
                        out=ps_big[:, fc, :], lhsT=ones_row[:],
                        rhs=rows[:, s, fc * 512:(fc + 1) * 512],
                        start=True, stop=True))
                S.op("vector", lambda e, d=dst: e.tensor_copy(
                    out=d[:], in_=ps_big[:].rearrange("p a f -> p (a f)")))

        stats_pass(x, False)
        row_math()
        rep_pass()
        stats_pass(x, True)
        if layer == 0:
            S.dma(lncol[:, 0, :], lnw.rearrange("(k p) -> p k", p=128), slow=True)
            S.dma(lncol[:, 1, :], lnb.rearrange("(k p) -> p k", p=128), slow=True)
            row_math()
            rep_pass()
            for th in range(2):
                for k in range(KD):
                    S.dma(xt[:], xn_f[k * 128:(k + 1) * 128, th * H:(th + 1) * H])
                    S.op("vector", lambda e, th=th: e.tensor_sub(
                        out=xt[:], in0=xt[:], in1=rep_a[:, th * H:(th + 1) * H]))
                    S.op("vector", lambda e, th=th: e.tensor_mul(
                        out=xt[:], in0=xt[:], in1=rep_b[:, th * H:(th + 1) * H]))
                    S.op("scalar", lambda e, k=k, th=th: e.activation(
                        out=xn2[:, k, th * H:(th + 1) * H], in_=xt[:],
                        func=AF.Identity, scale=lncol[:, 0, k:k + 1],
                        bias=lncol[:, 1, k:k + 1]))
        else:
            for k in range(KD):
                S.dma(xn2[:, k, :], xn_bf[k * 128:(k + 1) * 128, :])

        for m in range(32):
            S.dma(wm[:], win.rearrange("(k p) m -> p k m", p=128)[:, :, m * 128:(m + 1) * 128])
            for th in range(2):
                for k in range(KD):
                    for j in range(2):
                        S.op("tensor", lambda e, k=k, th=th, j=j: e.matmul(
                            out=ps_big[:, j, :], lhsT=wm[:, k, :],
                            rhs=xn2[:, k, th * H + j * 512:th * H + (j + 1) * 512],
                            start=(k == 0), stop=(k == KD - 1)))
                S.op("vector", lambda e: e.tensor_copy(out=ebf[:], in_=psb2))
                dst = xi_bf if m < 16 else z_bf
                mm2 = m if m < 16 else m - 16
                S.dma(dst[mm2 * 128:(mm2 + 1) * 128, th * H:(th + 1) * H],
                      ebf[:], queue="gpsimd")

        S.dma(cwr[:], convw.rearrange("(g p) k -> p g k", p=128), slow=True)
        S.dma(cbr[:], convb.rearrange("(g p) -> p g", p=128), slow=True)
        S.op("vector", lambda e: e.tensor_copy(out=cw[:], in_=cwr[:]))
        S.op("vector", lambda e: e.tensor_copy(out=cb[:], in_=cbr[:]))
        S.op("vector", lambda e: e.memset(ct[:, 0:3], 0.0))
        for g in range(16):
            S.dma(ct[:, 3:T + 3], xi_bf[g * 128:(g + 1) * 128, :])
            S.op("vector", lambda e, g=g: e.tensor_scalar(
                out=ua[:], in0=ct[:, 0:T], scalar1=cw[:, g, 0:1],
                scalar2=None, op0=ALU.mult))
            for kk in range(1, 4):
                S.op("vector", lambda e, g=g, kk=kk: e.scalar_tensor_tensor(
                    out=ua[:], in0=ct[:, kk:T + kk], scalar=cw[:, g, kk:kk + 1],
                    in1=ua[:], op0=ALU.mult, op1=ALU.add))
            S.op("scalar", lambda e, g=g: e.activation(
                out=ua[:], in_=ua[:], func=AF.Silu, bias=cb[:, g:g + 1]))
            S.op("vector", lambda e: e.tensor_copy(out=ub[:], in_=ua[:]))
            if layer == 1:
                S.op("vector", lambda e: e.tensor_copy(out=ua[:], in_=ub[:]))
            S.dma(u_f[g * 128:(g + 1) * 128, :], ua[:], queue="gpsimd")
            S.dma(u_bf[g * 128:(g + 1) * 128, :], ub[:], queue="gpsimd")

        S.dma(wxpt[:], wxp.rearrange("(k p) m -> p k m", p=128))
        psc2 = ps_sm[:, 0:2, :].rearrange("p a f -> p (a f)")
        psc3 = ps_sm[:, 2:4, :].rearrange("p a f -> p (a f)")
        for th in range(2):
            for k in range(16):
                S.dma(uh[:], u_bf[k * 128:(k + 1) * 128, th * H:(th + 1) * H])
                for j in range(2):
                    uhj = uh[:, j * 512:(j + 1) * 512]
                    if MX == 192:
                        S.op("tensor", lambda e, k=k, j=j, u2=uhj: e.matmul(
                            out=ps_big[:, j, :], lhsT=wxpt[:, k, 0:128],
                            rhs=u2, start=(k == 0), stop=(k == 15)))
                        S.op("tensor", lambda e, k=k, j=j, u2=uhj: e.matmul(
                            out=ps_sm[:, j, :], lhsT=wxpt[:, k, 128:192], rhs=u2,
                            start=(k == 0), stop=(k == 15)))
                    else:
                        S.op("tensor", lambda e, k=k, j=j, u2=uhj: e.matmul(
                            out=ps_big[0:64, j, :], lhsT=wxpt[:, k, 0:64],
                            rhs=u2, start=(k == 0), stop=(k == 15)))
                        S.op("tensor", lambda e, k=k, j=j, u2=uhj: e.matmul(
                            out=ps_sm[0:16, j, :], lhsT=wxpt[:, k, 64:80], rhs=u2,
                            start=(k == 0), stop=(k == 15)))
                        S.op("tensor", lambda e, k=k, j=j, u2=uhj: e.matmul(
                            out=ps_sm[0:16, 2 + j, :], lhsT=wxpt[:, k, 80:96], rhs=u2,
                            start=(k == 0), stop=(k == 15)))
            S.op("vector", lambda e, th=th: e.tensor_copy(
                out=dtr[:, th * H:(th + 1) * H],
                in_=ps_big[0:64, 0:2, :].rearrange("p a f -> p (a f)")))
            if MX == 192:
                S.op("vector", lambda e: e.tensor_copy(
                    out=tb[64:128, 0, :], in_=psb2[64:128, :]))
                S.op("vector", lambda e: e.tensor_copy(
                    out=tb[0:64, 1, :], in_=psc2[0:64, :]))
                S.dma(B0[:, th * H:(th + 1) * H], tb[64:128, 0, :], queue="gpsimd")
                S.dma(C0[:, th * H:(th + 1) * H], tb[0:64, 1, :], queue="gpsimd")
            else:
                S.op("vector", lambda e: e.tensor_copy(
                    out=tb[0:16, 0, :], in_=psc2[0:16, :]))
                S.op("vector", lambda e: e.tensor_copy(
                    out=tb[0:16, 1, :], in_=psc3[0:16, :]))
                S.dma(B0[:, th * H:(th + 1) * H], tb[0:16, 0, :], queue="gpsimd")
                S.dma(C0[:, th * H:(th + 1) * H], tb[0:16, 1, :], queue="gpsimd")

        S.dma(wdtf[:], wdt)
        S.op("vector", lambda e: e.tensor_copy(out=wdtb[:], in_=wdtf[:]))
        S.dma(bdtr[:], bdt.rearrange("(g p) -> p g", p=128), slow=True)
        S.op("vector", lambda e: e.tensor_copy(out=bdtc[:], in_=bdtr[:]))
        for m in range(16):
            for th in range(2):
                for j in range(2):
                    S.op("tensor", lambda e, m=m, th=th, j=j: e.matmul(
                        out=ps_big[:, j, :], lhsT=wdtb[:, m * 128:(m + 1) * 128],
                        rhs=dtr[:, th * H + j * 512:th * H + (j + 1) * 512],
                        start=True, stop=True))
                S.op("vector", lambda e, m=m: e.tensor_scalar(
                    out=tb[:, 0, :], in0=psb2, scalar1=bdtc[:, m:m + 1],
                    scalar2=0.0, op0=ALU.add, op1=ALU.max))
                S.op("scalar", lambda e, m=m: e.activation(
                    out=tb[:, 1, :], in_=psb2, func=AF.Abs, bias=bdtc[:, m:m + 1]))
                S.op("scalar", lambda e: e.activation(
                    out=tb[:, 2, :], in_=tb[:, 1, :], func=AF.Exp, scale=-1.0))
                S.op("scalar", lambda e: e.activation(
                    out=tb[:, 1, :], in_=tb[:, 2, :], func=AF.Ln, bias=1.0))
                S.op("vector", lambda e: e.tensor_add(
                    out=tb[:, 2, :], in0=tb[:, 0, :], in1=tb[:, 1, :]))
                S.dma(tb[:, 3, :], u_f[m * 128:(m + 1) * 128, th * H:(th + 1) * H])
                S.op("vector", lambda e: e.tensor_mul(
                    out=tb[:, 0, :], in0=tb[:, 2, :], in1=tb[:, 3, :]))
                S.op("vector", lambda e: e.tensor_copy(out=duh[:], in_=tb[:, 0, :]))
                S.dma(delta[m * 128:(m + 1) * 128, th * H:(th + 1) * H],
                      tb[:, 2, :], queue="gpsimd")
                S.dma(du[m * 128:(m + 1) * 128, th * H:(th + 1) * H],
                      tb[:, 0, :], queue="gpsimd")
                S.dma(du_f16[m * 128:(m + 1) * 128, th * H:(th + 1) * H],
                      duh[:], queue="gpsimd")


def build_lxa(layer):
    nc, stack, ctx = new_prog()
    S = Seq(ctx)
    with stack:
        io = lxa_io(nc, layer, lambda n, s, d, o: (dout if o else din)(nc, n, s, d))
        ctx.psA = ctx.psum("psA", [128, 4, 512], F32)
        ctx.psB = ctx.psum("psB", [128, 4, 512], F32)
        ctx.phase("lxa")
        emit_lxa(nc, ctx, S, io, layer)
        S.done()
    return nc



def scan_io(nc, layer, mk):
    NST = 64 if layer == 0 else 16
    a_dt = F32 if layer == 0 else BF16
    io = {}
    io["delta"] = mk("delta", (DI, T), F32, 0)
    io["du"] = mk("du", (DI, T), F32, 0)
    io["du_f16"] = mk("du_f16", (DI, T), F16, 0)
    io["Bm"] = mk("Bm", (NST, T), F32, 0)
    io["Cm"] = mk("Cm", (NST, T), F32, 0)
    io["alog"] = mk("alog", (DI, NST), a_dt, 0)
    io["y"] = mk("y", (DI, T), F32, 1)
    return io


def emit_scan(nc, ctx, S, io, layer, dg0=0, dg1=16):
    NST = 64 if layer == 0 else 16
    a_dt = F32 if layer == 0 else BF16

    delta = io["delta"]
    du = io["du"]
    Bm = io["Bm"]
    Cm = io["Cm"]
    alog = io["alog"]
    y = io["y"]

    if True:
        dl = ctx.sbuf("dl", [128, T], F32)
        dul = ctx.sbuf("dul", [128, T], F32)
        yac = ctx.sbuf("yac", [128, T], F32)
        dat = ctx.sbuf("dat", [128, T], F32)
        dbut = ctx.sbuf("dbut", [128, T], F32)
        ht = ctx.sbuf("ht", [128, T], F32)
        hct = ctx.sbuf("hct", [128, T], F32)
        brow = ctx.sbuf("brow", [1, T], F32)
        crow = ctx.sbuf("crow", [1, T], F32)
        alr = ctx.sbuf("alr", [128, NST], a_dt)
        asb = ctx.sbuf("asb", [128, NST], F32)
        ones_row = ctx.sbuf("ones_row", [1, 128], F32)
        ps_b = ctx.psA
        ps_c = ctx.psB
        psbv = ps_b[:].rearrange("p a f -> p (a f)")
        pscv = ps_c[:].rearrange("p a f -> p (a f)")

        S.op("vector", lambda e: e.memset(ones_row[:], 1.0))
        for dg in range(dg0, dg1):
            S.dma(dl[:], delta[dg * 128:(dg + 1) * 128, :])
            S.dma(dul[:], du[dg * 128:(dg + 1) * 128, :])
            S.dma(alr[:], alog[dg * 128:(dg + 1) * 128, :])
            S.op("scalar", lambda e: e.activation(out=asb[:], in_=alr[:], func=AF.Exp))
            S.op("vector", lambda e: e.tensor_single_scalar(
                out=asb[:], in_=asb[:], scalar=-1.0, op=ALU.mult))
            S.op("vector", lambda e: e.memset(yac[:], 0.0))
            for n in range(NST):
                S.dma(brow[:], Bm[n:n + 1, :])
                S.dma(crow[:], Cm[n:n + 1, :])
                for fc in range(4):
                    S.op("tensor", lambda e, fc=fc: e.matmul(
                        out=ps_b[:, fc, :], lhsT=ones_row[:],
                        rhs=brow[:, fc * 512:(fc + 1) * 512], start=True, stop=True))
                for fc in range(4):
                    S.op("tensor", lambda e, fc=fc: e.matmul(
                        out=ps_c[:, fc, :], lhsT=ones_row[:],
                        rhs=crow[:, fc * 512:(fc + 1) * 512], start=True, stop=True))
                S.op("scalar", lambda e, n=n: e.activation(
                    out=dat[:], in_=dl[:], func=AF.Exp, scale=asb[:, n:n + 1]))
                S.op("vector", lambda e: e.tensor_mul(out=dbut[:], in0=dul[:], in1=psbv))
                S.op("vector", lambda e: e.tensor_tensor_scan(
                    out=ht[:], data0=dat[:], data1=dbut[:], initial=0.0,
                    op0=ALU.mult, op1=ALU.add))
                S.op("vector", lambda e: e.tensor_mul(out=hct[:], in0=ht[:], in1=pscv))
                S.op("vector", lambda e: e.tensor_add(out=yac[:], in0=yac[:], in1=hct[:]))
            S.dma(y[dg * 128:(dg + 1) * 128, :], yac[:], queue="gpsimd")


def build_scan(layer, dg0=0, dg1=16):
    nc, stack, ctx = new_prog()
    S = Seq(ctx)
    with stack:
        io = scan_io(nc, layer, lambda n, s, d, o: (dout if o else din)(nc, n, s, d))
        ctx.psA = ctx.psum("psA", [128, 4, 512], F32)
        ctx.psB = ctx.psum("psB", [128, 4, 512], F32)
        ctx.phase("scan")
        emit_scan2(nc, ctx, S, io, layer)
        S.done()
    return nc



def l0out_io(nc, mk):
    io = {}
    io["y"] = mk("y", (DI, T), F32, 0)
    io["u_f"] = mk("u_f", (DI, T), F32, 0)
    io["z_bf"] = mk("z_bf", (DI, T), BF16, 0)
    io["d0"] = mk("d0", (DI,), F32, 0)
    io["wout_bf"] = mk("wout_bf", (DI, D), BF16, 0)
    io["xn_f"] = mk("xn_f", (D, T), F32, 0)
    io["xn_bf"] = mk("xn_bf", (D, T), BF16, 0)
    io["w1_bf"] = mk("w1_bf", (D, 4096), BF16, 0)
    io["b1"] = mk("b1", (4096,), F32, 0)
    io["w2_bf"] = mk("w2_bf", (4096, D), BF16, 0)
    io["b2"] = mk("b2", (D,), F32, 0)
    io["x"] = mk("x", (D, T), F32, 0)
    io["attn0"] = mk("attn0", (D, T), F32, 1)
    io["h_bf"] = mk("h_bf", (4096, T), BF16, 1)
    io["x1"] = mk("x1", (D, T), F32, 1)
    return io


def emit_l0out(nc, ctx, S, io):
    H = T // 2

    y = io["y"]
    u_f = io["u_f"]
    z_bf = io["z_bf"]
    d0 = io["d0"]
    wout = io["wout_bf"]
    xn_f = io["xn_f"]
    xn_bf = io["xn_bf"]
    w1 = io["w1_bf"]
    b1 = io["b1"]
    w2 = io["w2_bf"]
    b2 = io["b2"]
    x = io["x"]
    attn0 = io["attn0"]
    h_bf = io["h_bf"]
    x1 = io["x1"]

    KD = D // 128
    if True:
        gbuf = ctx.sbuf("gbuf", [128, 16, T], BF16)
        xnb = ctx.sbuf("xnb", [128, KD, T], BF16)
        t0 = ctx.sbuf("t0", [128, T], F32)
        t1 = ctx.sbuf("t1", [128, T], F32)
        dcol = ctx.sbuf("dcol", [128, 16], F32)
        zt = ctx.sbuf("zt", [128, T], BF16)
        wom = ctx.sbuf("wom", [128, 16, 128], BF16)
        w1m = ctx.sbuf("w1m", [128, KD, 128], BF16)
        b1c = ctx.sbuf("b1c", [128, 32], F32)
        b2c = ctx.sbuf("b2c", [128, 8], F32)
        tb = ctx.sbuf("tb", [128, 4, H], F32)
        hb = ctx.sbuf("hb", [128, H], BF16)
        hq = ctx.sbuf("hq", [128, 32, 512], BF16)
        w2m = ctx.sbuf("w2m", [128, 32, 128], BF16)
        knot = ctx.sbuf("knot", [128, 18], F32)
        ps_big = ctx.psA
        psb2 = ps_big[:, 0:2, :].rearrange("p a f -> p (a f)")
        psq = ps_big[:, 0:1, :].rearrange("p a f -> p (a f)")

        sgn = []
        for i, (vk, ck) in enumerate(PWL_KNOTS):
            S.op("vector", lambda e, i=i, vk=vk, ck=ck: e.memset(
                knot[:, i:i + 1], -abs(ck) * vk))
            sgn.append(1.0 if ck >= 0 else -1.0)
        S.op("vector", lambda e: e.memset(knot[:, 16:17], -1024.0))
        S.dma(dcol[:], d0.rearrange("(g p) -> p g", p=128), slow=True)
        S.dma(b1c[:], b1.rearrange("(g p) -> p g", p=128), slow=True)
        S.dma(b2c[:], b2.rearrange("(g p) -> p g", p=128), slow=True)
        S.op("vector", lambda e: e.tensor_single_scalar(
            out=b2c[:], in_=b2c[:], scalar=256.0, op=ALU.mult))

        for g in range(16):
            S.dma(t0[:], y[g * 128:(g + 1) * 128, :])
            S.dma(t1[:], u_f[g * 128:(g + 1) * 128, :])
            S.dma(zt[:], z_bf[g * 128:(g + 1) * 128, :])
            S.op("vector", lambda e, g=g: e.scalar_tensor_tensor(
                out=t0[:], in0=t1[:], scalar=dcol[:, g:g + 1], in1=t0[:],
                op0=ALU.mult, op1=ALU.add))
            S.op("scalar", lambda e: e.activation(out=t1[:], in_=zt[:], func=AF.Silu))
            S.op("vector", lambda e, g=g: e.tensor_mul(
                out=gbuf[:, g, :], in0=t0[:], in1=t1[:]))

        for m in range(KD):
            S.dma(wom[:], wout.rearrange("(k p) m -> p k m", p=128)[:, :, m * 128:(m + 1) * 128])
            for th in range(2):
                for k in range(16):
                    for j in range(2):
                        S.op("tensor", lambda e, k=k, th=th, j=j: e.matmul(
                            out=ps_big[:, j, :], lhsT=wom[:, k, :],
                            rhs=gbuf[:, k, th * H + j * 512:th * H + (j + 1) * 512],
                            start=(k == 0), stop=(k == 15)))
                S.dma(t0[:, 0:H], xn_f[m * 128:(m + 1) * 128, th * H:(th + 1) * H])
                S.op("vector", lambda e: e.tensor_add(
                    out=t1[:, 0:H], in0=t0[:, 0:H], in1=psb2))
                S.dma(attn0[m * 128:(m + 1) * 128, th * H:(th + 1) * H],
                      t1[:, 0:H], queue="gpsimd")

        for k in range(KD):
            S.dma(xnb[:, k, :], xn_bf[k * 128:(k + 1) * 128, :])
        for m in range(32):
            S.dma(w1m[:], w1.rearrange("(k p) m -> p k m", p=128)[:, :, m * 128:(m + 1) * 128])
            for th in range(2):
                for k in range(KD):
                    for j in range(2):
                        S.op("tensor", lambda e, k=k, th=th, j=j: e.matmul(
                            out=ps_big[:, j, :], lhsT=w1m[:, k, :],
                            rhs=xnb[:, k, th * H + j * 512:th * H + (j + 1) * 512],
                            start=(k == 0), stop=(k == KD - 1)))
                S.op("vector", lambda e, m=m: e.tensor_scalar(
                    out=tb[:, 0, :], in0=psb2, scalar1=b1c[:, m:m + 1],
                    scalar2=256.0, op0=ALU.add, op1=ALU.mult))
                S.op("vector", lambda e: e.tensor_scalar(
                    out=tb[:, 1, :], in0=tb[:, 0, :], scalar1=RNE_C,
                    scalar2=RNE_C, op0=ALU.add, op1=ALU.subtract))
                S.op("scalar", lambda e: e.activation(
                    out=tb[:, 2, :], in_=tb[:, 1, :], func=AF.Abs))
                S.op("vector", lambda e: e.tensor_single_scalar(
                    out=tb[:, 3, :], in_=tb[:, 2, :], scalar=PWL_S0, op=ALU.mult))
                for i, (vk, ck) in enumerate(PWL_KNOTS):
                    S.op("scalar", lambda e, i=i, ck=ck: e.activation(
                        out=tb[:, 0, :], in_=tb[:, 2, :], func=AF.Relu,
                        scale=abs(ck), bias=knot[:, i:i + 1]))
                    S.op("vector", lambda e, i=i: e.scalar_tensor_tensor(
                        out=tb[:, 3, :], in0=tb[:, 0, :], scalar=sgn[i],
                        in1=tb[:, 3, :], op0=ALU.mult, op1=ALU.add))
                S.op("scalar", lambda e: e.activation(
                    out=tb[:, 0, :], in_=tb[:, 2, :], func=AF.Relu,
                    bias=knot[:, 16:17]))
                S.op("vector", lambda e: e.tensor_scalar(
                    out=tb[:, 0, :], in0=tb[:, 0, :], scalar1=1.0,
                    scalar2=PWL_J256, op0=ALU.min, op1=ALU.mult))
                S.op("vector", lambda e: e.tensor_add(
                    out=tb[:, 3, :], in0=tb[:, 3, :], in1=tb[:, 0, :]))
                S.op("vector", lambda e: e.scalar_tensor_tensor(
                    out=tb[:, 0, :], in0=tb[:, 1, :], scalar=0.5,
                    in1=tb[:, 3, :], op0=ALU.mult, op1=ALU.add))
                S.op("vector", lambda e: e.tensor_scalar(
                    out=tb[:, 1, :], in0=tb[:, 0, :], scalar1=RNE_C,
                    scalar2=RNE_C, op0=ALU.add, op1=ALU.subtract))
                S.op("vector", lambda e: e.tensor_copy(out=hb[:], in_=tb[:, 1, :]))
                S.dma(h_bf[m * 128:(m + 1) * 128, th * H:(th + 1) * H],
                      hb[:], queue="gpsimd")

        for f4 in range(4):
            for k in range(32):
                S.dma(hq[:, k, :], h_bf[k * 128:(k + 1) * 128, f4 * 512:(f4 + 1) * 512])
            for m in range(KD):
                S.dma(w2m[:], w2.rearrange("(k p) m -> p k m", p=128)[:, :, m * 128:(m + 1) * 128])
                for k in range(32):
                    S.op("tensor", lambda e, k=k: e.matmul(
                        out=psq, lhsT=w2m[:, k, :], rhs=hq[:, k, :],
                        start=(k == 0), stop=(k == 31)))
                S.op("vector", lambda e, m=m: e.tensor_scalar(
                    out=tb[:, 0, 0:512], in0=psq, scalar1=b2c[:, m:m + 1],
                    scalar2=RNE_C, op0=ALU.add, op1=ALU.add))
                S.op("vector", lambda e: e.tensor_scalar(
                    out=tb[:, 1, 0:512], in0=tb[:, 0, 0:512], scalar1=RNE_C,
                    scalar2=1.0 / 256.0, op0=ALU.subtract, op1=ALU.mult))
                S.dma(tb[:, 2, 0:512], x[m * 128:(m + 1) * 128, f4 * 512:(f4 + 1) * 512])
                S.dma(tb[:, 3, 0:512], attn0[m * 128:(m + 1) * 128, f4 * 512:(f4 + 1) * 512])
                S.op("vector", lambda e: e.tensor_add(
                    out=tb[:, 0, 0:512], in0=tb[:, 2, 0:512], in1=tb[:, 3, 0:512]))
                S.op("vector", lambda e: e.tensor_add(
                    out=tb[:, 1, 0:512], in0=tb[:, 0, 0:512], in1=tb[:, 1, 0:512]))
                S.dma(x1[m * 128:(m + 1) * 128, f4 * 512:(f4 + 1) * 512],
                      tb[:, 1, 0:512], queue="gpsimd")


def build_l0out():
    nc, stack, ctx = new_prog()
    S = Seq(ctx)
    with stack:
        io = l0out_io(nc, lambda n, s, d, o: (dout if o else din)(nc, n, s, d))
        ctx.psA = ctx.psum("psA", [128, 4, 512], F32)
        ctx.psB = ctx.psum("psB", [128, 4, 512], F32)
        ctx.phase("l0out")
        emit_l0out(nc, ctx, S, io)
        S.done()
    return nc



def l1out_io(nc, mk):
    io = {}
    io["y"] = mk("y", (DI, T), F32, 0)
    io["u_f"] = mk("u_f", (DI, T), F32, 0)
    io["z_bf"] = mk("z_bf", (DI, T), BF16, 0)
    io["dskip"] = mk("dskip", (DI,), BF16, 0)
    io["opT"] = mk("opT", (DI, D), BF16, 0)
    io["xn_bf"] = mk("xn_bf", (D, T), BF16, 0)
    io["f1t_bf"] = mk("f1t_bf", (D, 4096), BF16, 0)
    io["w1s"] = mk("w1s", (4096,), F32, 0)
    io["f2t_bf"] = mk("f2t_bf", (D, 4096), BF16, 0)
    io["w2s"] = mk("w2s", (4096,), F32, 0)
    io["h1t_bf"] = mk("h1t_bf", (4096, 4096), BF16, 0)
    io["h1s"] = mk("h1s", (4096,), F32, 0)
    io["h2t_bf"] = mk("h2t_bf", (4096, 4096), BF16, 0)
    io["h2s"] = mk("h2s", (4096,), F32, 0)
    io["w3t_bf"] = mk("w3t_bf", (4096, D), BF16, 0)
    io["w3s"] = mk("w3s", (D,), F32, 0)
    io["x1"] = mk("x1", (D, T), F32, 0)
    io["attn1"] = mk("attn1", (D, T), BF16, 1)
    io["gf"] = mk("gf", (4096, T), BF16, 1)
    io["a1"] = mk("a1", (4096, T), BF16, 1)
    io["a2"] = mk("a2", (4096, T), BF16, 1)
    io["out"] = mk("out", (D, T), F32, 1)
    return io


def emit_l1out(nc, ctx, S, io):
    H = T // 2
    KD = D // 128

    y1 = io["y"]
    u1 = io["u_f"]
    z1 = io["z_bf"]
    dskip = io["dskip"]
    opt = io["opT"]
    xn1 = io["xn_bf"]
    f1t = io["f1t_bf"]
    w1s = io["w1s"]
    f2t = io["f2t_bf"]
    w2s = io["w2s"]
    h1t = io["h1t_bf"]
    h1s = io["h1s"]
    h2t = io["h2t_bf"]
    h2s = io["h2s"]
    w3t = io["w3t_bf"]
    w3s = io["w3s"]
    x1 = io["x1"]
    attn1 = io["attn1"]
    gf = io["gf"]
    a1 = io["a1"]
    a2 = io["a2"]
    out = io["out"]

    if True:
        gbuf = ctx.sbuf("gbuf", [128, 16, T], BF16)
        t0 = ctx.sbuf("t0", [128, T], F32)
        t1 = ctx.sbuf("t1", [128, T], F32)
        zt = ctx.sbuf("zt", [128, T], BF16)
        dcr = ctx.sbuf("dcr", [128, 16], BF16)
        dcol = ctx.sbuf("dcol", [128, 16], F32)
        wm = ctx.sbuf("wm", [128, 16, 128], BF16)
        wm2 = ctx.sbuf("wm2", [128, 16, 128], BF16)
        ab = ctx.sbuf("ab", [128, H], BF16)
        s1c = ctx.sbuf("s1c", [128, 32], F32)
        s2c = ctx.sbuf("s2c", [128, 32], F32)
        sh1c = ctx.sbuf("sh1c", [128, 32], F32)
        sh2c = ctx.sbuf("sh2c", [128, 32], F32)
        s3c = ctx.sbuf("s3c", [128, 8], F32)
        rq = ctx.sbuf("rq", [128, 32, 512], BF16)
        w32 = ctx.sbuf("w32", [128, 32, 128], BF16)
        ps_big = ctx.psA
        psb2 = ps_big[:, 0:2, :].rearrange("p a f -> p (a f)")
        psc2 = ps_big[:, 2:4, :].rearrange("p a f -> p (a f)")
        psq = ps_big[:, 0:1, :].rearrange("p a f -> p (a f)")

        S.dma(dcr[:], dskip.rearrange("(g p) -> p g", p=128), slow=True)
        S.op("vector", lambda e: e.tensor_copy(out=dcol[:], in_=dcr[:]))
        S.dma(s1c[:], w1s.rearrange("(g p) -> p g", p=128), slow=True)
        S.dma(s2c[:], w2s.rearrange("(g p) -> p g", p=128), slow=True)
        S.dma(sh1c[:], h1s.rearrange("(g p) -> p g", p=128), slow=True)
        S.dma(sh2c[:], h2s.rearrange("(g p) -> p g", p=128), slow=True)
        S.dma(s3c[:], w3s.rearrange("(g p) -> p g", p=128), slow=True)

        for g in range(16):
            S.dma(t0[:], y1[g * 128:(g + 1) * 128, :])
            S.dma(t1[:], u1[g * 128:(g + 1) * 128, :])
            S.dma(zt[:], z1[g * 128:(g + 1) * 128, :])
            S.op("vector", lambda e, g=g: e.scalar_tensor_tensor(
                out=t0[:], in0=t1[:], scalar=dcol[:, g:g + 1], in1=t0[:],
                op0=ALU.mult, op1=ALU.add))
            S.op("scalar", lambda e: e.activation(out=t1[:], in_=zt[:], func=AF.Silu))
            S.op("vector", lambda e, g=g: e.tensor_mul(
                out=gbuf[:, g, :], in0=t0[:], in1=t1[:]))

        for m in range(KD):
            S.dma(wm[:], opt.rearrange("(k p) m -> p k m", p=128)[:, :, m * 128:(m + 1) * 128])
            for th in range(2):
                for k in range(16):
                    for j in range(2):
                        S.op("tensor", lambda e, k=k, th=th, j=j: e.matmul(
                            out=ps_big[:, j, :], lhsT=wm[:, k, :],
                            rhs=gbuf[:, k, th * H + j * 512:th * H + (j + 1) * 512],
                            start=(k == 0), stop=(k == 15)))
                S.op("vector", lambda e: e.tensor_copy(out=ab[:], in_=psb2))
                S.dma(attn1[m * 128:(m + 1) * 128, th * H:(th + 1) * H],
                      ab[:], queue="gpsimd")

        for k in range(KD):
            S.dma(gbuf[:, k, :], xn1[k * 128:(k + 1) * 128, :])
        for m in range(32):
            S.dma(wm[:, 0:KD, :], f1t.rearrange("(k p) m -> p k m", p=128)[:, :, m * 128:(m + 1) * 128])
            S.dma(wm2[:, 0:KD, :], f2t.rearrange("(k p) m -> p k m", p=128)[:, :, m * 128:(m + 1) * 128])
            for th in range(2):
                for k in range(KD):
                    for j in range(2):
                        S.op("tensor", lambda e, k=k, th=th, j=j: e.matmul(
                            out=ps_big[:, j, :], lhsT=wm[:, k, :],
                            rhs=gbuf[:, k, th * H + j * 512:th * H + (j + 1) * 512],
                            start=(k == 0), stop=(k == KD - 1)))
                for k in range(KD):
                    for j in range(2):
                        S.op("tensor", lambda e, k=k, th=th, j=j: e.matmul(
                            out=ps_big[:, 2 + j, :], lhsT=wm2[:, k, :],
                            rhs=gbuf[:, k, th * H + j * 512:th * H + (j + 1) * 512],
                            start=(k == 0), stop=(k == KD - 1)))
                S.op("scalar", lambda e, m=m: e.activation(
                    out=t0[:, 0:H], in_=psb2, func=AF.Silu, scale=s1c[:, m:m + 1]))
                S.op("scalar", lambda e, m=m: e.activation(
                    out=t1[:, 0:H], in_=psc2, func=AF.Identity, scale=s2c[:, m:m + 1]))
                S.op("vector", lambda e: e.tensor_mul(
                    out=ab[:], in0=t0[:, 0:H], in1=t1[:, 0:H]))
                S.dma(gf[m * 128:(m + 1) * 128, th * H:(th + 1) * H],
                      ab[:], queue="gpsimd")

        for (src, dst, wmat, scol) in ((gf, a1, h1t, sh1c), (a1, a2, h2t, sh2c)):
            for f4 in range(4):
                for k in range(32):
                    S.dma(rq[:, k, :], src[k * 128:(k + 1) * 128,
                                           f4 * 512:(f4 + 1) * 512])
                for m in range(32):
                    S.dma(w32[:], wmat.rearrange("(k p) m -> p k m", p=128)[:, :, m * 128:(m + 1) * 128])
                    for k in range(32):
                        S.op("tensor", lambda e, k=k: e.matmul(
                            out=psq, lhsT=w32[:, k, :], rhs=rq[:, k, :],
                            start=(k == 0), stop=(k == 31)))
                    S.op("scalar", lambda e, m=m, scol=scol: e.activation(
                        out=ab[:, 0:512], in_=psq, func=AF.Gelu,
                        scale=scol[:, m:m + 1]))
                    S.dma(dst[m * 128:(m + 1) * 128, f4 * 512:(f4 + 1) * 512],
                          ab[:, 0:512], queue="gpsimd")

        for f4 in range(4):
            for k in range(32):
                S.dma(rq[:, k, :], a2[k * 128:(k + 1) * 128, f4 * 512:(f4 + 1) * 512])
            for m in range(KD):
                S.dma(w32[:], w3t.rearrange("(k p) m -> p k m", p=128)[:, :, m * 128:(m + 1) * 128])
                for k in range(32):
                    S.op("tensor", lambda e, k=k: e.matmul(
                        out=psq, lhsT=w32[:, k, :], rhs=rq[:, k, :],
                        start=(k == 0), stop=(k == 31)))
                S.op("scalar", lambda e, m=m: e.activation(
                    out=t0[:, 0:512], in_=psq, func=AF.Identity,
                    scale=s3c[:, m:m + 1]))
                S.dma(t1[:, 0:512], x1[m * 128:(m + 1) * 128, f4 * 512:(f4 + 1) * 512])
                S.dma(ab[:, 0:512], attn1[m * 128:(m + 1) * 128, f4 * 512:(f4 + 1) * 512])
                S.op("vector", lambda e: e.tensor_add(
                    out=t1[:, 0:512], in0=t1[:, 0:512], in1=ab[:, 0:512]))
                S.op("vector", lambda e: e.tensor_add(
                    out=t0[:, 0:512], in0=t0[:, 0:512], in1=t1[:, 0:512]))
                S.dma(out[m * 128:(m + 1) * 128, f4 * 512:(f4 + 1) * 512],
                      t0[:, 0:512], queue="gpsimd")


def build_l1out():
    nc, stack, ctx = new_prog()
    S = Seq(ctx)
    with stack:
        io = l1out_io(nc, lambda n, s, d, o: (dout if o else din)(nc, n, s, d))
        ctx.psA = ctx.psum("psA", [128, 4, 512], F32)
        ctx.psB = ctx.psum("psB", [128, 4, 512], F32)
        ctx.phase("l1out")
        emit_l1out(nc, ctx, S, io)
        S.done()
    return nc


_PROGS = {}


def _prog(name, builder):
    if name not in _PROGS:
        _PROGS[name] = builder()
    return _PROGS[name]


def _run(name, builder, maps):
    res = run_bass_kernel_spmd(_prog(name, builder), maps, core_ids=list(range(8)))
    return res.results


def kernel(**inp):
    asc = np.ascontiguousarray
    g = {k: np.asarray(v) for k, v in inp.items()}
    x = g["L0_attn_x"]
    xfm = [asc(x[c].T) for c in range(8)]

    f1T = asc(g["L1_ffn_w1_w"].T)
    f2T = asc(g["L1_ffn_w2_w"].T)
    h1T = asc(g["L1_ffn_h1_w"].T)
    h2T = asc(g["L1_ffn_h2_w"].T)
    w3T = asc(g["L1_ffn_w3_w"].T)
    ipT = asc(g["L1_attn_in_proj_w"].T)
    opT = asc(g["L1_attn_out_proj_w"].T)
    xpT = asc(g["L1_attn_x_proj_w"].T)
    dpT = asc(g["L1_attn_dt_proj_w"].T)
    cw0 = asc(g["L0_attn_conv_w"].reshape(DI, 4))
    cw1 = asc(g["L1_attn_conv_w"].reshape(DI, 4))

    if FUSED:
        return kernel_fused(g, xfm, (f1T, f2T, h1T, h2T, w3T, ipT, opT, xpT,
                                     dpT, cw0, cw1))

    pm = {"win": g["L0_attn_W_in"], "wout": g["L0_attn_W_out"],
          "w1": g["L0_ffn_w1"], "w2": g["L0_ffn_w2"],
          "wxp": g["L0_attn_W_xproj"],
          "f1t": f1T, "f2t": f2T, "h1t": h1T, "h2t": h2T, "w3t": w3T}
    W = _run("prep", build_prep, [pm] * 8)[0]

    mA = [{"x": xfm[c], "lnw": g["L0_attn_ln_w"], "lnb": g["L0_attn_ln_b"],
           "win_bf": W["win_bf"], "convw": cw0, "convb": g["L0_attn_conv_b"],
           "wxp_bf": W["wxp_bf"], "wdt": g["L0_attn_W_dt"],
           "bdt": g["L0_attn_b_dt"]} for c in range(8)]
    rA = _run("l0a", lambda: build_lxa(0), mA)

    mS = [{"delta": rA[c]["delta"], "du": rA[c]["du"],
           "du_f16": rA[c]["du_f16"], "Bm": rA[c]["B0"],
           "Cm": rA[c]["C0"], "alog": g["L0_attn_A_log"]} for c in range(8)]
    rS = _run("scan0", lambda: build_scan(0), mS)

    mO = [{"y": rS[c]["y"], "u_f": rA[c]["u_f"], "z_bf": rA[c]["z_bf"],
           "d0": g["L0_attn_D"], "wout_bf": W["wout_bf"], "xn_f": rA[c]["xn_f"],
           "xn_bf": rA[c]["xn_bf"], "w1_bf": W["w1_bf"], "b1": g["L0_ffn_b1"],
           "w2_bf": W["w2_bf"], "b2": g["L0_ffn_b2"], "x": xfm[c]}
          for c in range(8)]
    rO = _run("l0out", build_l0out, mO)

    mA1 = [{"x": rO[c]["x1"], "win_bf": ipT, "convw": cw1,
            "convb": g["L1_attn_conv_b"], "wxp_bf": xpT, "wdt": dpT,
            "bdt": g["L1_attn_dt_proj_b"]} for c in range(8)]
    rA1 = _run("l1a", lambda: build_lxa(1), mA1)

    mS1 = [{"delta": rA1[c]["delta"], "du": rA1[c]["du"],
            "du_f16": rA1[c]["du_f16"], "Bm": rA1[c]["B0"],
            "Cm": rA1[c]["C0"], "alog": g["L1_attn_A_log"]} for c in range(8)]
    rS1 = _run("scan1", lambda: build_scan(1), mS1)

    mO1 = [{"y": rS1[c]["y"], "u_f": rA1[c]["u_f"], "z_bf": rA1[c]["z_bf"],
            "dskip": g["L1_attn_D_skip"], "opT": opT, "xn_bf": rA1[c]["xn_bf"],
            "f1t_bf": W["f1t_bf"], "w1s": g["L1_ffn_w1_s"],
            "f2t_bf": W["f2t_bf"], "w2s": g["L1_ffn_w2_s"],
            "h1t_bf": W["h1t_bf"], "h1s": g["L1_ffn_h1_s"],
            "h2t_bf": W["h2t_bf"], "h2s": g["L1_ffn_h2_s"],
            "w3t_bf": W["w3t_bf"], "w3s": g["L1_ffn_w3_s"],
            "x1": rO[c]["x1"]} for c in range(8)]
    rO1 = _run("l1out", build_l1out, mO1)

    return np.stack([asc(rO1[c]["out"].T) for c in range(8)], axis=0)



def build_fused():
    nc, stack, ctx = new_prog()
    S = Seq(ctx)
    with stack:
        ctx.psA = ctx.psum("psA", [128, 4, 512], F32)
        ctx.psB = ctx.psum("psB", [128, 4, 512], F32)

        def D_(n, s, d):
            return nc.dram_tensor(n, list(s), d, kind="Internal").ap()

        iop = prep_io(nc, lambda n, s, d, o: D_(n, s, d) if o else din(nc, n, s, d))
        ctx.phase("prep")
        emit_prep(nc, ctx, S, iop)

        def mk0(n, s, d, o):
            if n in ("win_bf", "wxp_bf"):
                return iop[n]
            if o:
                return D_("l0_" + n, s, d)
            return din(nc, "l0_" + n, s, d)
        io0 = lxa_io(nc, 0, mk0)
        ctx.phase("l0a")
        emit_lxa(nc, ctx, S, io0, 0)

        y0 = D_("y0", (DI, T), F32)
        ios0 = {"delta": io0["delta"], "du": io0["du"],
                "du_f16": io0["du_f16"], "Bm": io0["B0"],
                "Cm": io0["C0"], "alog": din(nc, "alog0", (DI, 64), F32),
                "y": y0}
        ctx.phase("scan0")
        emit_scan2(nc, ctx, S, ios0, 0)

        def mko0(n, s, d, o):
            m = {"y": y0, "u_f": io0["u_f"], "z_bf": io0["z_bf"],
                 "wout_bf": iop["wout_bf"], "xn_f": io0["xn_f"],
                 "xn_bf": io0["xn_bf"], "w1_bf": iop["w1_bf"],
                 "w2_bf": iop["w2_bf"], "x": io0["x"]}
            if n in m:
                return m[n]
            if o:
                return D_("o0_" + n, s, d)
            return din(nc, "o0_" + n, s, d)
        ioo0 = l0out_io(nc, mko0)
        ctx.phase("l0out")
        emit_l0out(nc, ctx, S, ioo0)

        def mk1(n, s, d, o):
            if n == "x":
                return ioo0["x1"]
            if o:
                return D_("l1_" + n, s, d)
            return din(nc, "l1_" + n, s, d)
        io1 = lxa_io(nc, 1, mk1)
        ctx.phase("l1a")
        emit_lxa(nc, ctx, S, io1, 1)

        y1 = D_("y1", (DI, T), F32)
        ios1 = {"delta": io1["delta"], "du": io1["du"],
                "du_f16": io1["du_f16"], "Bm": io1["B0"],
                "Cm": io1["C0"], "alog": din(nc, "alog1", (DI, 16), BF16),
                "y": y1}
        ctx.phase("scan1")
        emit_scan2(nc, ctx, S, ios1, 1)

        def mko1(n, s, d, o):
            m = {"y": y1, "u_f": io1["u_f"], "z_bf": io1["z_bf"],
                 "xn_bf": io1["xn_bf"], "f1t_bf": iop["f1t_bf"],
                 "f2t_bf": iop["f2t_bf"], "h1t_bf": iop["h1t_bf"],
                 "h2t_bf": iop["h2t_bf"], "w3t_bf": iop["w3t_bf"],
                 "x1": ioo0["x1"]}
            if n in m:
                return m[n]
            if n == "out":
                return dout(nc, "out", s, d)
            if o:
                return D_("o1_" + n, s, d)
            return din(nc, "o1_" + n, s, d)
        ioo1 = l1out_io(nc, mko1)
        ctx.phase("l1out")
        emit_l1out(nc, ctx, S, ioo1)
        S.done()
    return nc


FUSED = True


def kernel_fused(g, xfm, hosted):
    (f1T, f2T, h1T, h2T, w3T, ipT, opT, xpT, dpT, cw0, cw1) = hosted
    base = {"win": g["L0_attn_W_in"], "wout": g["L0_attn_W_out"],
            "w1": g["L0_ffn_w1"], "w2": g["L0_ffn_w2"],
            "wxp": g["L0_attn_W_xproj"],
            "f1t": f1T, "f2t": f2T, "h1t": h1T, "h2t": h2T, "w3t": w3T,
            "l0_lnw": g["L0_attn_ln_w"], "l0_lnb": g["L0_attn_ln_b"],
            "l0_convw": cw0, "l0_convb": g["L0_attn_conv_b"],
            "l0_wdt": g["L0_attn_W_dt"], "l0_bdt": g["L0_attn_b_dt"],
            "alog0": g["L0_attn_A_log"],
            "o0_d0": g["L0_attn_D"], "o0_b1": g["L0_ffn_b1"],
            "o0_b2": g["L0_ffn_b2"],
            "l1_win_bf": ipT, "l1_convw": cw1, "l1_convb": g["L1_attn_conv_b"],
            "l1_wxp_bf": xpT, "l1_wdt": dpT, "l1_bdt": g["L1_attn_dt_proj_b"],
            "alog1": g["L1_attn_A_log"],
            "o1_dskip": g["L1_attn_D_skip"], "o1_opT": opT,
            "o1_w1s": g["L1_ffn_w1_s"], "o1_w2s": g["L1_ffn_w2_s"],
            "o1_h1s": g["L1_ffn_h1_s"], "o1_h2s": g["L1_ffn_h2_s"],
            "o1_w3s": g["L1_ffn_w3_s"]}
    maps = [dict(base, l0_x=xfm[c]) for c in range(8)]
    res = _run("fused", build_fused, maps)
    return np.stack([np.ascontiguousarray(res[c]["out"].T) for c in range(8)],
                    axis=0)



def emit_scan2(nc, ctx, S, io, layer):
    NST = 64 if layer == 0 else 16
    a_dt = F32 if layer == 0 else BF16
    tg = f"sc{layer}_"
    W, P, A, V, Y = tg + "rw", tg + "pe", tg + "act", tg + "dve", tg + "yad"
    PD, PA, PV, PP, YST = tg + "pd", tg + "pa", tg + "pv", tg + "pp", tg + "yst"

    delta = io["delta"]
    duf16 = io["du_f16"]
    Bm = io["Bm"]
    Cm = io["Cm"]
    alog = io["alog"]
    y = io["y"]

    q0 = ctx.val("q")
    nb = NST
    NB = 16 * nb

    dl = ctx.sbuf("dl", [128, 2, T], F32)
    duf = ctx.sbuf("duf", [128, 2, T], F16)
    alr = ctx.sbuf("alr", [128, 2, NST], a_dt)
    asb0 = ctx.sbuf("asb0", [128, 2, NST], F32)
    asb = ctx.sbuf("asb", [128, 2, NST], F32)
    brow = ctx.sbuf("brow", [1, 2, T], F32)
    crow = ctx.sbuf("crow", [1, 2, T], F32)
    dat = ctx.sbuf("dat", [128, 2, T], F16)
    besb = ctx.sbuf("besb", [128, 2, T], F16)
    cesb = ctx.sbuf("cesb", [128, 2, T], F16)
    dbut = ctx.sbuf("dbut", [128, 2, T], F16)
    ht = ctx.sbuf("ht", [128, 2, T], F16)
    hct = ctx.sbuf("hct", [128, 2, T], F16)
    yac = ctx.sbuf("yac", [128, 2, T], F32)
    ones_row = ctx.sbuf("ones_row", [1, 128], F32)
    psB, psC = ctx.psA, ctx.psB

    S.op("vector", lambda e: e.memset(ones_row[:], 1.0))
    q0 = ctx.val("q")

    def qw(i):
        return [("q", q0)] if i == 0 else []

    for dg in range(16):
        g2 = dg % 2
        pw = [("q", q0)] if dg < 2 else [(A, 3 * nb * (dg - 1)), (V, 3 * nb * (dg - 1))]
        ctx.emit("sync", (lambda dg=dg, g2=g2: lambda e: e.dma_start(
            out=dl[:, g2, :], in_=delta[dg * 128:(dg + 1) * 128, :]))(),
            waits=tuple(pw), fire=ctx.fire(PD, 16))
        ctx.emit("sync", (lambda dg=dg, g2=g2: lambda e: e.dma_start(
            out=duf[:, g2, :], in_=duf16[dg * 128:(dg + 1) * 128, :]))(),
            fire=ctx.fire(PD, 16))
        ctx.emit("sync", (lambda dg=dg, g2=g2: lambda e: e.dma_start(
            out=alr[:, g2, :], in_=alog[dg * 128:(dg + 1) * 128, :]))(),
            fire=ctx.fire(PD, 16))
        ctx.emit("scalar", (lambda g2=g2: lambda e: e.activation(
            out=asb0[:, g2, :], in_=alr[:, g2, :], func=AF.Exp))(),
            waits=((PD, 48 * (dg + 1)), (PV, max(0, dg - 1))),
            fire=ctx.fire(PA, 1))
        ctx.emit("vector", (lambda g2=g2: lambda e: e.tensor_single_scalar(
            out=asb[:, g2, :], in_=asb0[:, g2, :], scalar=-1.0, op=ALU.mult))(),
            waits=((PA, dg + 1),) + ((A, 3 * nb * (dg - 1)),) if dg >= 2 else ((PA, dg + 1),),
            fire=ctx.fire(PV, 1))
        ctx.emit("gpsimd", (lambda g2=g2: lambda e: e.memset(yac[:, g2, :], 0.0))(),
            waits=((YST, 16 * max(0, dg - 1)),), fire=ctx.fire(PP, 1))

        for n in range(nb):
            i = dg * nb + n
            p = i % 2
            ctx.emit("sync", (lambda n=n, p=p: lambda e: e.dma_start(
                out=brow[:, p, :], in_=Bm[n:n + 1, :]))(),
                waits=tuple(qw(i) + ([(P, i - 1)] if i >= 2 else [])),
                fire=ctx.fire(W, 16))
            ctx.emit("sync", (lambda n=n, p=p: lambda e: e.dma_start(
                out=crow[:, p, :], in_=Cm[n:n + 1, :]))(),
                fire=ctx.fire(W, 16))
            for fc in range(4):
                wts = [(W, 32 * (i + 1)), (A, 3 * i)] if fc == 0 else []
                ctx.emit("tensor", (lambda p=p, fc=fc: lambda e: e.matmul(
                    out=psB[:, fc, :], lhsT=ones_row[:],
                    rhs=brow[:, p, fc * 512:(fc + 1) * 512],
                    start=True, stop=True))(),
                    waits=tuple(qw(i) + wts))
            for fc in range(4):
                ctx.emit("tensor", (lambda p=p, fc=fc: lambda e: e.matmul(
                    out=psC[:, fc, :], lhsT=ones_row[:],
                    rhs=crow[:, p, fc * 512:(fc + 1) * 512],
                    start=True, stop=True))(),
                    fire=ctx.fire(P, 1) if fc == 3 else None)
            ctx.emit("scalar", (lambda g2=g2, n=n, p=p: lambda e: e.activation(
                out=dat[:, p, :], in_=dl[:, g2, :], func=AF.Exp,
                scale=asb[:, g2, n:n + 1]))(),
                waits=tuple(qw(i) + [(PV, dg + 1)] + ([(V, 3 * (i - 1))] if i >= 2 else [])),
                fire=ctx.fire(A, 1))
            ctx.emit("scalar", (lambda p=p: lambda e: e.activation(
                out=besb[:, p, :],
                in_=psB[:].rearrange("p a f -> p (a f)"),
                func=AF.Identity))(),
                waits=((P, i + 1),), fire=ctx.fire(A, 1))
            ctx.emit("scalar", (lambda p=p: lambda e: e.activation(
                out=cesb[:, p, :],
                in_=psC[:].rearrange("p a f -> p (a f)"),
                func=AF.Identity))(),
                fire=ctx.fire(A, 1))
            ctx.emit("vector", (lambda g2=g2, p=p: lambda e: e.tensor_mul(
                out=dbut[:, p, :], in0=duf[:, g2, :], in1=besb[:, p, :]))(),
                waits=tuple(qw(i) + [(A, 3 * i + 2)]),
                fire=ctx.fire(V, 1))
            ctx.emit("vector", (lambda p=p: lambda e: e.tensor_tensor_scan(
                out=ht[:, p, :], data0=dat[:, p, :], data1=dbut[:, p, :],
                initial=0.0, op0=ALU.mult, op1=ALU.add))(),
                waits=((V, 3 * i + 1),), fire=ctx.fire(V, 1))
            ctx.emit("vector", (lambda p=p: lambda e: e.tensor_mul(
                out=hct[:, p, :], in0=ht[:, p, :], in1=cesb[:, p, :]))(),
                waits=((V, 3 * i + 2), (A, 3 * i + 3)) + (((Y, i - 1),) if i >= 2 else ()),
                fire=ctx.fire(V, 1))
            ctx.emit("gpsimd", (lambda g2=g2, p=p: lambda e: e.tensor_add(
                out=yac[:, g2, :], in0=yac[:, g2, :], in1=hct[:, p, :]))(),
                waits=((V, 3 * i + 3), (Y, i), (PP, dg + 1)),
                fire=ctx.fire(Y, 1))
        ctx.emit("gpsimd", (lambda dg=dg, g2=g2: lambda e: e.dma_start(
            out=y[dg * 128:(dg + 1) * 128, :], in_=yac[:, g2, :]))(),
            waits=((Y, nb * (dg + 1)),), fire=ctx.fire(YST, 16))
    ctx.emit("vector", lambda e: e.engine_nop(),
             waits=((YST, 16 * 16),), fire=ctx.fire("q", 1))


# version 05
# speedup vs baseline: 1.6824x; 1.1796x over previous
import math
import numpy as np
import ml_dtypes
from contextlib import ExitStack

import concourse.bass as bass
import concourse.mybir as mybir
from concourse.bass_utils import run_bass_kernel_spmd

F32 = mybir.dt.float32
BF16 = mybir.dt.bfloat16
F16 = mybir.dt.float16
ALU = mybir.AluOpType
AF = mybir.ActivationFunctionType

BF = ml_dtypes.bfloat16

T = 2048
D = 1024
DI = 2048
RNE_C = 12582912.0

def _pwl_constants():
    xs = [0.25 * k for k in range(17)]
    ev = [0.5 * x * (1.0 + math.erf(x / math.sqrt(2.0))) - x / 2.0 for x in xs]
    s = [(ev[k + 1] - ev[k]) / 0.25 for k in range(16)]
    knots = []
    for k in range(1, 16):
        knots.append((64.0 * k, s[k] - s[k - 1]))
    knots.append((1024.0, 0.5 - s[15]))
    j256 = 256.0 * (4.0 - (0.5 * 4.0 * (1.0 + math.erf(4.0 / math.sqrt(2.0)))))
    return s[0], knots, j256


PWL_S0, PWL_KNOTS, PWL_J256 = _pwl_constants()


class Ctx:
    ENGINES = ("sync", "vector", "scalar", "tensor", "gpsimd")

    def __init__(self, nc, stack):
        self.nc = nc
        self.stack = stack
        self.ops = {e: [] for e in self.ENGINES}
        self.sems = {}
        self.cnt = {}
        self.bump = 16512
        self.phase_tag = "g"
        self.psA = None
        self.psB = None

    def sem(self, name):
        if name not in self.sems:
            self.sems[name] = self.stack.enter_context(self.nc.semaphore(name))
            self.cnt[name] = 0
        return self.sems[name]

    def fire(self, name, amount):
        h = self.sem(name)
        self.cnt[name] += amount
        return (h, amount)

    def val(self, name):
        return self.cnt.get(name, 0)

    def emit(self, engine, fn, waits=(), fire=None):
        wl = [(self.sem(n), v) for (n, v) in waits if v > 0]

        def runner(eng):
            for h, v in wl:
                eng.wait_ge(h, v)
            ins = fn(eng)
            if fire is not None:
                ins.then_inc(fire[0], fire[1])
        self.ops[engine].append(runner)

    def dma(self, queue, out, in_, waits=(), sem="dma", transpose=False):
        f = self.fire(sem, 16)
        if transpose:
            self.emit(queue, lambda e: e.dma_start_transpose(out=out, in_=in_),
                      waits, f)
        else:
            self.emit(queue, lambda e: e.dma_start(out=out, in_=in_), waits, f)
        return self.cnt[sem]

    SBUF_BASE = 16512
    SBUF_LIMIT = 229344

    def phase(self, tag):
        self.phase_tag = tag
        self.bump = self.SBUF_BASE

    def sbuf(self, name, shape, dt):
        per = 1
        for s in shape[1:]:
            per *= s
        per *= {F32: 4, BF16: 2, F16: 2}[dt]
        off = (self.bump + 31) & ~31
        self.bump = off + per
        assert self.bump <= self.SBUF_LIMIT, (
            f"sbuf overflow in phase {self.phase_tag}: {self.bump}")
        h = self.nc.alloc_sbuf_tensor_at(
            f"{self.phase_tag}_{name}", list(shape), dt, offset=off)
        return h

    def psum(self, name, shape, dt=F32):
        return self.stack.enter_context(self.nc.psum_tensor(name, shape, dt))

    def finalize(self, final_waits):
        ops = self.ops
        wl = [(self.sems[n], v) for (n, v) in final_waits]
        with self.nc.Block() as block:
            @block.sync
            def _(eng):
                for f in ops["sync"]:
                    f(eng)
                for h, v in wl:
                    eng.wait_ge(h, v)

            @block.vector
            def _(eng):
                for f in ops["vector"]:
                    f(eng)

            @block.scalar
            def _(eng):
                for f in ops["scalar"]:
                    f(eng)

            @block.tensor
            def _(eng):
                for f in ops["tensor"]:
                    f(eng)

            @block.gpsimd
            def _(eng):
                for f in ops["gpsimd"]:
                    f(eng)


def new_prog():
    nc = bass.Bass("TRN2", target_bir_lowering=False)
    stack = ExitStack()
    ctx = Ctx(nc, stack)
    return nc, stack, ctx


def din(nc, name, shape, dt=F32):
    return nc.dram_tensor(name, list(shape), dt, kind="ExternalInput").ap()


def dout(nc, name, shape, dt=F32):
    return nc.dram_tensor(name, list(shape), dt, kind="ExternalOutput").ap()



def _seq(ctx, engine, fn):
    v = ctx.val("q")
    ctx.emit(engine, fn, waits=(("q", v),), fire=ctx.fire("q", 1))


def _seqdma(ctx, queue, out, in_, transpose=False, slow=False):
    v = ctx.val("q")
    f = ctx.fire("q", 16)
    if transpose:
        ctx.emit(queue, lambda e: e.dma_start_transpose(out=out, in_=in_),
                 waits=(("q", v),), fire=f)
    elif slow:
        ctx.emit(queue, lambda e: e.dma_start(out=out, in_=in_,
                                              allow_slow_non_contiguous=True),
                 waits=(("q", v),), fire=f)
    else:
        ctx.emit(queue, lambda e: e.dma_start(out=out, in_=in_),
                 waits=(("q", v),), fire=f)


class Seq:

    def __init__(self, ctx):
        self.ctx = ctx

    def op(self, engine, fn):
        _seq(self.ctx, engine, fn)

    def dma(self, out, in_, queue="sync", transpose=False, slow=False):
        _seqdma(self.ctx, queue, out, in_, transpose, slow)

    def done(self):
        self.ctx.finalize([("q", self.ctx.val("q"))])


class ConstCols:

    def __init__(self, S, values):
        self.tile = S.ctx.sbuf("cstc", [128, max(1, len(values))], F32)
        self.idx = {}
        for i, v in enumerate(values):
            self.idx[v] = i
            S.op("vector", lambda e, i=i, v=v: e.memset(self.tile[:, i:i + 1], v))

    def ap(self, v):
        i = self.idx[v]
        return self.tile[:, i:i + 1]


CASTS = [
    ("win", "win_bf", D, 4096),
    ("wout", "wout_bf", DI, D),
    ("w1", "w1_bf", D, 4096),
    ("w2", "w2_bf", 4096, D),
    ("wxp", "wxp_bf", DI, 192),
    ("f1t", "f1t_bf", D, 4096),
    ("f2t", "f2t_bf", D, 4096),
    ("h1t", "h1t_bf", 4096, 4096),
    ("h2t", "h2t_bf", 4096, 4096),
    ("w3t", "w3t_bf", 4096, D),
]


def prep_io(nc, mk):
    io = {}
    for (iname, oname, rows, cols) in CASTS:
        io[iname] = mk(iname, (rows, cols), F32, 0)
        io[oname] = mk(oname, (rows, cols), BF16, 1)
    return io


def emit_prep(nc, ctx, S, io):
    tin = ctx.sbuf("tin", [128, 4096], F32)
    tout = ctx.sbuf("tout", [128, 4096], BF16)
    for (iname, oname, rows, cols) in CASTS:
        src = io[iname]
        dst = io[oname]
        for i in range(rows // 128):
            S.dma(tin[:, :cols], src[i * 128:(i + 1) * 128, :])
            S.op("vector", lambda e, c=cols: e.tensor_copy(
                out=tout[:, :c], in_=tin[:, :c]))
            S.dma(dst[i * 128:(i + 1) * 128, :], tout[:, :cols],
                  queue="gpsimd")


def build_prep():
    nc, stack, ctx = new_prog()
    with stack:
        S = Seq(ctx)
        io = prep_io(nc, lambda n, s, d, o: (dout if o else din)(nc, n, s, d))
        ctx.phase("prep")
        emit_prep(nc, ctx, S, io)
        S.done()
    return nc



LXA_IO = ["x", "win_bf", "convw", "convb", "wxp_bf", "wdt", "bdt",
          "xn_bf", "xi_bf", "z_bf", "u_f", "u_bf", "B0", "C0", "delta", "du"]


def lxa_io(nc, layer, mk):
    NST = 64 if layer == 0 else 16
    MX = 192 if layer == 0 else 96
    fl = F32 if layer == 0 else BF16
    io = {}
    io["x"] = mk("x", (D, T), F32, 0)
    io["win_bf"] = mk("win_bf", (D, 4096), BF16, 0)
    io["convw"] = mk("convw", (DI, 4), fl, 0)
    io["convb"] = mk("convb", (DI,), fl, 0)
    io["wxp_bf"] = mk("wxp_bf", (DI, MX), BF16, 0)
    io["wdt"] = mk("wdt", (64, DI), fl, 0)
    io["bdt"] = mk("bdt", (DI,), fl, 0)
    if layer == 0:
        io["lnw"] = mk("lnw", (D,), F32, 0)
        io["lnb"] = mk("lnb", (D,), F32, 0)
        io["xn_f"] = mk("xn_f", (D, T), F32, 1)
    io["xn_bf"] = mk("xn_bf", (D, T), BF16, 1)
    io["xi_bf"] = mk("xi_bf", (DI, T), BF16, 1)
    io["z_bf"] = mk("z_bf", (DI, T), BF16, 1)
    io["u_f"] = mk("u_f", (DI, T), F32, 1)
    io["u_bf"] = mk("u_bf", (DI, T), BF16, 1)
    io["B0"] = mk("B0", (NST, T), F32, 1)
    io["C0"] = mk("C0", (NST, T), F32, 1)
    io["delta"] = mk("delta", (DI, T), F32, 1)
    io["du"] = mk("du", (DI, T), F32, 1)
    io["du_f16"] = mk("du_f16", (DI, T), F16, 1)
    return io


def emit_lxa(nc, ctx, S, io, layer):
    KD = D // 128
    H = T // 2
    NST = 64 if layer == 0 else 16
    MX = 192 if layer == 0 else 96

    x = io["x"]
    win = io["win_bf"]
    convw = io["convw"]
    convb = io["convb"]
    wxp = io["wxp_bf"]
    wdt = io["wdt"]
    bdt = io["bdt"]
    if layer == 0:
        lnw = io["lnw"]
        lnb = io["lnb"]
        xn_f = io["xn_f"]
    xn_bf = io["xn_bf"]
    xi_bf = io["xi_bf"]
    z_bf = io["z_bf"]
    u_f = io["u_f"]
    u_bf = io["u_bf"]
    B0 = io["B0"]
    C0 = io["C0"]
    delta = io["delta"]
    du = io["du"]
    du_f16 = io["du_f16"]

    if True:
        xt = ctx.sbuf("xt", [128, H], F32)
        sq = ctx.sbuf("sq", [128, H], F32)
        ones_col = ctx.sbuf("ones_col", [128, 1], F32)
        ones_row = ctx.sbuf("ones_row", [1, 128], F32)
        rows = ctx.sbuf("rows", [1, 4, T], F32)
        rep_a = ctx.sbuf("rep_a", [128, T], F32)
        rep_b = ctx.sbuf("rep_b", [128, T], F32)
        xbf = ctx.sbuf("xbf", [128, H], BF16)
        xn2 = ctx.sbuf("xn2", [128, KD, T], BF16)
        wm = ctx.sbuf("wm", [128, KD, 128], BF16)
        ebf = ctx.sbuf("ebf", [128, H], BF16)
        ct = ctx.sbuf("ct", [128, T + 3], BF16)
        cwr = ctx.sbuf("cwr", [128, 16, 4], F32 if layer == 0 else BF16)
        cbr = ctx.sbuf("cbr", [128, 16], F32 if layer == 0 else BF16)
        cw = ctx.sbuf("cw", [128, 16, 4], F32)
        cb = ctx.sbuf("cb", [128, 16], F32)
        ua = ctx.sbuf("ua", [128, T], F32)
        ub = ctx.sbuf("ub", [128, T], BF16)
        wxpt = ctx.sbuf("wxpt", [128, 16, MX], BF16)
        uh = ctx.sbuf("uh", [128, H], BF16)
        dtr = ctx.sbuf("dtr", [64, T], BF16)
        wdtf = ctx.sbuf("wdtf", [64, DI], F32 if layer == 0 else BF16)
        wdtb = ctx.sbuf("wdtb", [64, DI], BF16)
        bdtr = ctx.sbuf("bdtr", [128, 16], F32 if layer == 0 else BF16)
        bdtc = ctx.sbuf("bdtc", [128, 16], F32)
        tb = ctx.sbuf("tb", [128, 4, H], F32)
        duh = ctx.sbuf("duh", [128, H], F16)
        lncol = ctx.sbuf("lncol", [128, 2, KD], F32)
        ps_big = ctx.psA
        ps_sm = ctx.psB[0:64]

        MU, SQS, RSTD, SCR = range(4)
        psb2 = ps_big[:, 0:2, :].rearrange("p a f -> p (a f)")

        S.op("vector", lambda e: e.memset(ones_col[:], 1.0))
        S.op("vector", lambda e: e.memset(ones_row[:], 1.0))
        cst = ConstCols(S, [1e-5])

        def stats_pass(src_ap, xn_mode):
            for th in range(2):
                for k in range(KD):
                    S.dma(xt[:], src_ap[k * 128:(k + 1) * 128, th * H:(th + 1) * H])
                    if xn_mode:
                        S.op("vector", lambda e, th=th: e.tensor_sub(
                            out=xt[:], in0=xt[:], in1=rep_a[:, th * H:(th + 1) * H]))
                        S.op("vector", lambda e, th=th: e.tensor_mul(
                            out=xt[:], in0=xt[:], in1=rep_b[:, th * H:(th + 1) * H]))
                        if layer == 0:
                            S.dma(xn_f[k * 128:(k + 1) * 128, th * H:(th + 1) * H],
                                  xt[:], queue="gpsimd")
                        S.op("vector", lambda e: e.tensor_copy(out=xbf[:], in_=xt[:]))
                        S.dma(xn_bf[k * 128:(k + 1) * 128, th * H:(th + 1) * H],
                              xbf[:], queue="gpsimd")
                    if (not xn_mode) or layer == 0:
                        S.op("scalar", lambda e: e.activation(
                            out=sq[:], in_=xt[:], func=AF.Square))
                        for which in range(2):
                            st = xt if which == 0 else sq
                            for fc in range(2):
                                S.op("tensor", lambda e, w=which, fc=fc, k=k, st=st:
                                     e.matmul(out=ps_sm[0:1, w * 2 + fc, :],
                                              lhsT=ones_col[:],
                                              rhs=st[:, fc * 512:(fc + 1) * 512],
                                              start=(k == 0), stop=(k == KD - 1)))
                if (not xn_mode) or layer == 0:
                    for which, slot in ((0, MU), (1, SQS)):
                        for fc in range(2):
                            S.op("scalar", lambda e, w=which, s=slot, fc=fc, th=th:
                                 e.activation(
                                     out=rows[:, s, th * H + fc * 512:
                                              th * H + (fc + 1) * 512],
                                     in_=ps_sm[0:1, w * 2 + fc, :],
                                     func=AF.Identity, scale=1.0 / D))

        def row_math():
            S.op("vector", lambda e: e.tensor_mul(
                out=rows[:, SCR, :], in0=rows[:, MU, :], in1=rows[:, MU, :]))
            S.op("vector", lambda e: e.tensor_sub(
                out=rows[:, SCR, :], in0=rows[:, SQS, :], in1=rows[:, SCR, :]))
            S.op("scalar", lambda e: e.activation(
                out=rows[:, SCR, :], in_=rows[:, SCR, :], func=AF.Sqrt,
                bias=cst.ap(1e-5)[0:1, :]))
            S.op("vector", lambda e: e.reciprocal(
                out=rows[:, RSTD, :], in_=rows[:, SCR, :]))

        def rep_pass():
            for slot, dst in ((MU, rep_a), (RSTD, rep_b)):
                for fc in range(4):
                    S.op("tensor", lambda e, s=slot, fc=fc: e.matmul(
                        out=ps_big[:, fc, :], lhsT=ones_row[:],
                        rhs=rows[:, s, fc * 512:(fc + 1) * 512],
                        start=True, stop=True))
                S.op("vector", lambda e, d=dst: e.tensor_copy(
                    out=d[:], in_=ps_big[:].rearrange("p a f -> p (a f)")))

        stats_pass(x, False)
        row_math()
        rep_pass()
        stats_pass(x, True)
        if layer == 0:
            S.dma(lncol[:, 0, :], lnw.rearrange("(k p) -> p k", p=128), slow=True)
            S.dma(lncol[:, 1, :], lnb.rearrange("(k p) -> p k", p=128), slow=True)
            row_math()
            rep_pass()
            for th in range(2):
                for k in range(KD):
                    S.dma(xt[:], xn_f[k * 128:(k + 1) * 128, th * H:(th + 1) * H])
                    S.op("vector", lambda e, th=th: e.tensor_sub(
                        out=xt[:], in0=xt[:], in1=rep_a[:, th * H:(th + 1) * H]))
                    S.op("vector", lambda e, th=th: e.tensor_mul(
                        out=xt[:], in0=xt[:], in1=rep_b[:, th * H:(th + 1) * H]))
                    S.op("scalar", lambda e, k=k, th=th: e.activation(
                        out=xn2[:, k, th * H:(th + 1) * H], in_=xt[:],
                        func=AF.Identity, scale=lncol[:, 0, k:k + 1],
                        bias=lncol[:, 1, k:k + 1]))
        else:
            for k in range(KD):
                S.dma(xn2[:, k, :], xn_bf[k * 128:(k + 1) * 128, :])

        for m in range(32):
            S.dma(wm[:], win.rearrange("(k p) m -> p k m", p=128)[:, :, m * 128:(m + 1) * 128])
            for th in range(2):
                for k in range(KD):
                    for j in range(2):
                        S.op("tensor", lambda e, k=k, th=th, j=j: e.matmul(
                            out=ps_big[:, j, :], lhsT=wm[:, k, :],
                            rhs=xn2[:, k, th * H + j * 512:th * H + (j + 1) * 512],
                            start=(k == 0), stop=(k == KD - 1)))
                S.op("vector", lambda e: e.tensor_copy(out=ebf[:], in_=psb2))
                dst = xi_bf if m < 16 else z_bf
                mm2 = m if m < 16 else m - 16
                S.dma(dst[mm2 * 128:(mm2 + 1) * 128, th * H:(th + 1) * H],
                      ebf[:], queue="gpsimd")

        S.dma(cwr[:], convw.rearrange("(g p) k -> p g k", p=128), slow=True)
        S.dma(cbr[:], convb.rearrange("(g p) -> p g", p=128), slow=True)
        S.op("vector", lambda e: e.tensor_copy(out=cw[:], in_=cwr[:]))
        S.op("vector", lambda e: e.tensor_copy(out=cb[:], in_=cbr[:]))
        S.op("vector", lambda e: e.memset(ct[:, 0:3], 0.0))
        for g in range(16):
            S.dma(ct[:, 3:T + 3], xi_bf[g * 128:(g + 1) * 128, :])
            S.op("vector", lambda e, g=g: e.tensor_scalar(
                out=ua[:], in0=ct[:, 0:T], scalar1=cw[:, g, 0:1],
                scalar2=None, op0=ALU.mult))
            for kk in range(1, 4):
                S.op("vector", lambda e, g=g, kk=kk: e.scalar_tensor_tensor(
                    out=ua[:], in0=ct[:, kk:T + kk], scalar=cw[:, g, kk:kk + 1],
                    in1=ua[:], op0=ALU.mult, op1=ALU.add))
            S.op("scalar", lambda e, g=g: e.activation(
                out=ua[:], in_=ua[:], func=AF.Silu, bias=cb[:, g:g + 1]))
            S.op("vector", lambda e: e.tensor_copy(out=ub[:], in_=ua[:]))
            if layer == 1:
                S.op("vector", lambda e: e.tensor_copy(out=ua[:], in_=ub[:]))
            S.dma(u_f[g * 128:(g + 1) * 128, :], ua[:], queue="gpsimd")
            S.dma(u_bf[g * 128:(g + 1) * 128, :], ub[:], queue="gpsimd")

        S.dma(wxpt[:], wxp.rearrange("(k p) m -> p k m", p=128))
        psc2 = ps_sm[:, 0:2, :].rearrange("p a f -> p (a f)")
        psc3 = ps_sm[:, 2:4, :].rearrange("p a f -> p (a f)")
        for th in range(2):
            for k in range(16):
                S.dma(uh[:], u_bf[k * 128:(k + 1) * 128, th * H:(th + 1) * H])
                for j in range(2):
                    uhj = uh[:, j * 512:(j + 1) * 512]
                    if MX == 192:
                        S.op("tensor", lambda e, k=k, j=j, u2=uhj: e.matmul(
                            out=ps_big[:, j, :], lhsT=wxpt[:, k, 0:128],
                            rhs=u2, start=(k == 0), stop=(k == 15)))
                        S.op("tensor", lambda e, k=k, j=j, u2=uhj: e.matmul(
                            out=ps_sm[:, j, :], lhsT=wxpt[:, k, 128:192], rhs=u2,
                            start=(k == 0), stop=(k == 15)))
                    else:
                        S.op("tensor", lambda e, k=k, j=j, u2=uhj: e.matmul(
                            out=ps_big[0:64, j, :], lhsT=wxpt[:, k, 0:64],
                            rhs=u2, start=(k == 0), stop=(k == 15)))
                        S.op("tensor", lambda e, k=k, j=j, u2=uhj: e.matmul(
                            out=ps_sm[0:16, j, :], lhsT=wxpt[:, k, 64:80], rhs=u2,
                            start=(k == 0), stop=(k == 15)))
                        S.op("tensor", lambda e, k=k, j=j, u2=uhj: e.matmul(
                            out=ps_sm[0:16, 2 + j, :], lhsT=wxpt[:, k, 80:96], rhs=u2,
                            start=(k == 0), stop=(k == 15)))
            S.op("vector", lambda e, th=th: e.tensor_copy(
                out=dtr[:, th * H:(th + 1) * H],
                in_=ps_big[0:64, 0:2, :].rearrange("p a f -> p (a f)")))
            if MX == 192:
                S.op("vector", lambda e: e.tensor_copy(
                    out=tb[64:128, 0, :], in_=psb2[64:128, :]))
                S.op("vector", lambda e: e.tensor_copy(
                    out=tb[0:64, 1, :], in_=psc2[0:64, :]))
                S.dma(B0[:, th * H:(th + 1) * H], tb[64:128, 0, :], queue="gpsimd")
                S.dma(C0[:, th * H:(th + 1) * H], tb[0:64, 1, :], queue="gpsimd")
            else:
                S.op("vector", lambda e: e.tensor_copy(
                    out=tb[0:16, 0, :], in_=psc2[0:16, :]))
                S.op("vector", lambda e: e.tensor_copy(
                    out=tb[0:16, 1, :], in_=psc3[0:16, :]))
                S.dma(B0[:, th * H:(th + 1) * H], tb[0:16, 0, :], queue="gpsimd")
                S.dma(C0[:, th * H:(th + 1) * H], tb[0:16, 1, :], queue="gpsimd")

        S.dma(wdtf[:], wdt)
        S.op("vector", lambda e: e.tensor_copy(out=wdtb[:], in_=wdtf[:]))
        S.dma(bdtr[:], bdt.rearrange("(g p) -> p g", p=128), slow=True)
        S.op("vector", lambda e: e.tensor_copy(out=bdtc[:], in_=bdtr[:]))
        for m in range(16):
            for th in range(2):
                for j in range(2):
                    S.op("tensor", lambda e, m=m, th=th, j=j: e.matmul(
                        out=ps_big[:, j, :], lhsT=wdtb[:, m * 128:(m + 1) * 128],
                        rhs=dtr[:, th * H + j * 512:th * H + (j + 1) * 512],
                        start=True, stop=True))
                S.op("vector", lambda e, m=m: e.tensor_scalar(
                    out=tb[:, 0, :], in0=psb2, scalar1=bdtc[:, m:m + 1],
                    scalar2=0.0, op0=ALU.add, op1=ALU.max))
                S.op("scalar", lambda e, m=m: e.activation(
                    out=tb[:, 1, :], in_=psb2, func=AF.Abs, bias=bdtc[:, m:m + 1]))
                S.op("scalar", lambda e: e.activation(
                    out=tb[:, 2, :], in_=tb[:, 1, :], func=AF.Exp, scale=-1.0))
                S.op("scalar", lambda e: e.activation(
                    out=tb[:, 1, :], in_=tb[:, 2, :], func=AF.Ln, bias=1.0))
                S.op("vector", lambda e: e.tensor_add(
                    out=tb[:, 2, :], in0=tb[:, 0, :], in1=tb[:, 1, :]))
                S.dma(tb[:, 3, :], u_f[m * 128:(m + 1) * 128, th * H:(th + 1) * H])
                S.op("vector", lambda e: e.tensor_mul(
                    out=tb[:, 0, :], in0=tb[:, 2, :], in1=tb[:, 3, :]))
                S.op("vector", lambda e: e.tensor_copy(out=duh[:], in_=tb[:, 0, :]))
                S.dma(delta[m * 128:(m + 1) * 128, th * H:(th + 1) * H],
                      tb[:, 2, :], queue="gpsimd")
                S.dma(du[m * 128:(m + 1) * 128, th * H:(th + 1) * H],
                      tb[:, 0, :], queue="gpsimd")
                S.dma(du_f16[m * 128:(m + 1) * 128, th * H:(th + 1) * H],
                      duh[:], queue="gpsimd")


def build_lxa(layer):
    nc, stack, ctx = new_prog()
    S = Seq(ctx)
    with stack:
        io = lxa_io(nc, layer, lambda n, s, d, o: (dout if o else din)(nc, n, s, d))
        ctx.psA = ctx.psum("psA", [128, 4, 512], F32)
        ctx.psB = ctx.psum("psB", [128, 4, 512], F32)
        ctx.phase("lxa")
        emit_lxa(nc, ctx, S, io, layer)
        S.done()
    return nc



def scan_io(nc, layer, mk):
    NST = 64 if layer == 0 else 16
    a_dt = F32 if layer == 0 else BF16
    io = {}
    io["delta"] = mk("delta", (DI, T), F32, 0)
    io["du"] = mk("du", (DI, T), F32, 0)
    io["du_f16"] = mk("du_f16", (DI, T), F16, 0)
    io["Bm"] = mk("Bm", (NST, T), F32, 0)
    io["Cm"] = mk("Cm", (NST, T), F32, 0)
    io["alog"] = mk("alog", (DI, NST), a_dt, 0)
    io["y"] = mk("y", (DI, T), F32, 1)
    return io


def emit_scan(nc, ctx, S, io, layer, dg0=0, dg1=16):
    NST = 64 if layer == 0 else 16
    a_dt = F32 if layer == 0 else BF16

    delta = io["delta"]
    du = io["du"]
    Bm = io["Bm"]
    Cm = io["Cm"]
    alog = io["alog"]
    y = io["y"]

    if True:
        dl = ctx.sbuf("dl", [128, T], F32)
        dul = ctx.sbuf("dul", [128, T], F32)
        yac = ctx.sbuf("yac", [128, T], F32)
        dat = ctx.sbuf("dat", [128, T], F32)
        dbut = ctx.sbuf("dbut", [128, T], F32)
        ht = ctx.sbuf("ht", [128, T], F32)
        hct = ctx.sbuf("hct", [128, T], F32)
        brow = ctx.sbuf("brow", [1, T], F32)
        crow = ctx.sbuf("crow", [1, T], F32)
        alr = ctx.sbuf("alr", [128, NST], a_dt)
        asb = ctx.sbuf("asb", [128, NST], F32)
        ones_row = ctx.sbuf("ones_row", [1, 128], F32)
        ps_b = ctx.psA
        ps_c = ctx.psB
        psbv = ps_b[:].rearrange("p a f -> p (a f)")
        pscv = ps_c[:].rearrange("p a f -> p (a f)")

        S.op("vector", lambda e: e.memset(ones_row[:], 1.0))
        for dg in range(dg0, dg1):
            S.dma(dl[:], delta[dg * 128:(dg + 1) * 128, :])
            S.dma(dul[:], du[dg * 128:(dg + 1) * 128, :])
            S.dma(alr[:], alog[dg * 128:(dg + 1) * 128, :])
            S.op("scalar", lambda e: e.activation(out=asb[:], in_=alr[:], func=AF.Exp))
            S.op("vector", lambda e: e.tensor_single_scalar(
                out=asb[:], in_=asb[:], scalar=-1.0, op=ALU.mult))
            S.op("vector", lambda e: e.memset(yac[:], 0.0))
            for n in range(NST):
                S.dma(brow[:], Bm[n:n + 1, :])
                S.dma(crow[:], Cm[n:n + 1, :])
                for fc in range(4):
                    S.op("tensor", lambda e, fc=fc: e.matmul(
                        out=ps_b[:, fc, :], lhsT=ones_row[:],
                        rhs=brow[:, fc * 512:(fc + 1) * 512], start=True, stop=True))
                for fc in range(4):
                    S.op("tensor", lambda e, fc=fc: e.matmul(
                        out=ps_c[:, fc, :], lhsT=ones_row[:],
                        rhs=crow[:, fc * 512:(fc + 1) * 512], start=True, stop=True))
                S.op("scalar", lambda e, n=n: e.activation(
                    out=dat[:], in_=dl[:], func=AF.Exp, scale=asb[:, n:n + 1]))
                S.op("vector", lambda e: e.tensor_mul(out=dbut[:], in0=dul[:], in1=psbv))
                S.op("vector", lambda e: e.tensor_tensor_scan(
                    out=ht[:], data0=dat[:], data1=dbut[:], initial=0.0,
                    op0=ALU.mult, op1=ALU.add))
                S.op("vector", lambda e: e.tensor_mul(out=hct[:], in0=ht[:], in1=pscv))
                S.op("vector", lambda e: e.tensor_add(out=yac[:], in0=yac[:], in1=hct[:]))
            S.dma(y[dg * 128:(dg + 1) * 128, :], yac[:], queue="gpsimd")


def build_scan(layer, dg0=0, dg1=16):
    nc, stack, ctx = new_prog()
    S = Seq(ctx)
    with stack:
        io = scan_io(nc, layer, lambda n, s, d, o: (dout if o else din)(nc, n, s, d))
        ctx.psA = ctx.psum("psA", [128, 4, 512], F32)
        ctx.psB = ctx.psum("psB", [128, 4, 512], F32)
        ctx.phase("scan")
        emit_scan2(nc, ctx, S, io, layer)
        S.done()
    return nc



def l0out_io(nc, mk):
    io = {}
    io["y"] = mk("y", (DI, T), F32, 0)
    io["u_f"] = mk("u_f", (DI, T), F32, 0)
    io["z_bf"] = mk("z_bf", (DI, T), BF16, 0)
    io["d0"] = mk("d0", (DI,), F32, 0)
    io["wout_bf"] = mk("wout_bf", (DI, D), BF16, 0)
    io["xn_f"] = mk("xn_f", (D, T), F32, 0)
    io["xn_bf"] = mk("xn_bf", (D, T), BF16, 0)
    io["w1_bf"] = mk("w1_bf", (D, 4096), BF16, 0)
    io["b1"] = mk("b1", (4096,), F32, 0)
    io["w2_bf"] = mk("w2_bf", (4096, D), BF16, 0)
    io["b2"] = mk("b2", (D,), F32, 0)
    io["x"] = mk("x", (D, T), F32, 0)
    io["attn0"] = mk("attn0", (D, T), F32, 1)
    io["h_bf"] = mk("h_bf", (4096, T), BF16, 1)
    io["x1"] = mk("x1", (D, T), F32, 1)
    return io


def emit_l0out(nc, ctx, S, io):
    H = T // 2

    y = io["y"]
    u_f = io["u_f"]
    z_bf = io["z_bf"]
    d0 = io["d0"]
    wout = io["wout_bf"]
    xn_f = io["xn_f"]
    xn_bf = io["xn_bf"]
    w1 = io["w1_bf"]
    b1 = io["b1"]
    w2 = io["w2_bf"]
    b2 = io["b2"]
    x = io["x"]
    attn0 = io["attn0"]
    h_bf = io["h_bf"]
    x1 = io["x1"]

    KD = D // 128
    if True:
        gbuf = ctx.sbuf("gbuf", [128, 16, T], BF16)
        xnb = ctx.sbuf("xnb", [128, KD, T], BF16)
        t0 = ctx.sbuf("t0", [128, T], F32)
        t1 = ctx.sbuf("t1", [128, T], F32)
        dcol = ctx.sbuf("dcol", [128, 16], F32)
        zt = ctx.sbuf("zt", [128, T], BF16)
        wom = ctx.sbuf("wom", [128, 16, 128], BF16)
        w1m = ctx.sbuf("w1m", [128, KD, 128], BF16)
        b1c = ctx.sbuf("b1c", [128, 32], F32)
        b2c = ctx.sbuf("b2c", [128, 8], F32)
        tb = ctx.sbuf("tb", [128, 4, H], F32)
        hb = ctx.sbuf("hb", [128, H], BF16)
        hq = ctx.sbuf("hq", [128, 32, 512], BF16)
        w2m = ctx.sbuf("w2m", [128, 32, 128], BF16)
        knot = ctx.sbuf("knot", [128, 18], F32)
        ps_big = ctx.psA
        psb2 = ps_big[:, 0:2, :].rearrange("p a f -> p (a f)")
        psq = ps_big[:, 0:1, :].rearrange("p a f -> p (a f)")

        sgn = []
        for i, (vk, ck) in enumerate(PWL_KNOTS):
            S.op("vector", lambda e, i=i, vk=vk, ck=ck: e.memset(
                knot[:, i:i + 1], -abs(ck) * vk))
            sgn.append(1.0 if ck >= 0 else -1.0)
        S.op("vector", lambda e: e.memset(knot[:, 16:17], -1024.0))
        S.dma(dcol[:], d0.rearrange("(g p) -> p g", p=128), slow=True)
        S.dma(b1c[:], b1.rearrange("(g p) -> p g", p=128), slow=True)
        S.dma(b2c[:], b2.rearrange("(g p) -> p g", p=128), slow=True)
        S.op("vector", lambda e: e.tensor_single_scalar(
            out=b2c[:], in_=b2c[:], scalar=256.0, op=ALU.mult))

        for g in range(16):
            S.dma(t0[:], y[g * 128:(g + 1) * 128, :])
            S.dma(t1[:], u_f[g * 128:(g + 1) * 128, :])
            S.dma(zt[:], z_bf[g * 128:(g + 1) * 128, :])
            S.op("vector", lambda e, g=g: e.scalar_tensor_tensor(
                out=t0[:], in0=t1[:], scalar=dcol[:, g:g + 1], in1=t0[:],
                op0=ALU.mult, op1=ALU.add))
            S.op("scalar", lambda e: e.activation(out=t1[:], in_=zt[:], func=AF.Silu))
            S.op("vector", lambda e, g=g: e.tensor_mul(
                out=gbuf[:, g, :], in0=t0[:], in1=t1[:]))

        for m in range(KD):
            S.dma(wom[:], wout.rearrange("(k p) m -> p k m", p=128)[:, :, m * 128:(m + 1) * 128])
            for th in range(2):
                for k in range(16):
                    for j in range(2):
                        S.op("tensor", lambda e, k=k, th=th, j=j: e.matmul(
                            out=ps_big[:, j, :], lhsT=wom[:, k, :],
                            rhs=gbuf[:, k, th * H + j * 512:th * H + (j + 1) * 512],
                            start=(k == 0), stop=(k == 15)))
                S.dma(t0[:, 0:H], xn_f[m * 128:(m + 1) * 128, th * H:(th + 1) * H])
                S.op("vector", lambda e: e.tensor_add(
                    out=t1[:, 0:H], in0=t0[:, 0:H], in1=psb2))
                S.dma(attn0[m * 128:(m + 1) * 128, th * H:(th + 1) * H],
                      t1[:, 0:H], queue="gpsimd")

        for k in range(KD):
            S.dma(xnb[:, k, :], xn_bf[k * 128:(k + 1) * 128, :])
        for m in range(32):
            S.dma(w1m[:], w1.rearrange("(k p) m -> p k m", p=128)[:, :, m * 128:(m + 1) * 128])
            for th in range(2):
                for k in range(KD):
                    for j in range(2):
                        S.op("tensor", lambda e, k=k, th=th, j=j: e.matmul(
                            out=ps_big[:, j, :], lhsT=w1m[:, k, :],
                            rhs=xnb[:, k, th * H + j * 512:th * H + (j + 1) * 512],
                            start=(k == 0), stop=(k == KD - 1)))
                S.op("vector", lambda e, m=m: e.tensor_scalar(
                    out=tb[:, 0, :], in0=psb2, scalar1=b1c[:, m:m + 1],
                    scalar2=256.0, op0=ALU.add, op1=ALU.mult))
                S.op("vector", lambda e: e.tensor_scalar(
                    out=tb[:, 1, :], in0=tb[:, 0, :], scalar1=RNE_C,
                    scalar2=RNE_C, op0=ALU.add, op1=ALU.subtract))
                S.op("scalar", lambda e: e.activation(
                    out=tb[:, 2, :], in_=tb[:, 1, :], func=AF.Abs))
                S.op("vector", lambda e: e.tensor_single_scalar(
                    out=tb[:, 3, :], in_=tb[:, 2, :], scalar=PWL_S0, op=ALU.mult))
                for i, (vk, ck) in enumerate(PWL_KNOTS):
                    S.op("scalar", lambda e, i=i, ck=ck: e.activation(
                        out=tb[:, 0, :], in_=tb[:, 2, :], func=AF.Relu,
                        scale=abs(ck), bias=knot[:, i:i + 1]))
                    S.op("vector", lambda e, i=i: e.scalar_tensor_tensor(
                        out=tb[:, 3, :], in0=tb[:, 0, :], scalar=sgn[i],
                        in1=tb[:, 3, :], op0=ALU.mult, op1=ALU.add))
                S.op("scalar", lambda e: e.activation(
                    out=tb[:, 0, :], in_=tb[:, 2, :], func=AF.Relu,
                    bias=knot[:, 16:17]))
                S.op("vector", lambda e: e.tensor_scalar(
                    out=tb[:, 0, :], in0=tb[:, 0, :], scalar1=1.0,
                    scalar2=PWL_J256, op0=ALU.min, op1=ALU.mult))
                S.op("vector", lambda e: e.tensor_add(
                    out=tb[:, 3, :], in0=tb[:, 3, :], in1=tb[:, 0, :]))
                S.op("vector", lambda e: e.scalar_tensor_tensor(
                    out=tb[:, 0, :], in0=tb[:, 1, :], scalar=0.5,
                    in1=tb[:, 3, :], op0=ALU.mult, op1=ALU.add))
                S.op("vector", lambda e: e.tensor_scalar(
                    out=tb[:, 1, :], in0=tb[:, 0, :], scalar1=RNE_C,
                    scalar2=RNE_C, op0=ALU.add, op1=ALU.subtract))
                S.op("vector", lambda e: e.tensor_copy(out=hb[:], in_=tb[:, 1, :]))
                S.dma(h_bf[m * 128:(m + 1) * 128, th * H:(th + 1) * H],
                      hb[:], queue="gpsimd")

        for f4 in range(4):
            for k in range(32):
                S.dma(hq[:, k, :], h_bf[k * 128:(k + 1) * 128, f4 * 512:(f4 + 1) * 512])
            for m in range(KD):
                S.dma(w2m[:], w2.rearrange("(k p) m -> p k m", p=128)[:, :, m * 128:(m + 1) * 128])
                for k in range(32):
                    S.op("tensor", lambda e, k=k: e.matmul(
                        out=psq, lhsT=w2m[:, k, :], rhs=hq[:, k, :],
                        start=(k == 0), stop=(k == 31)))
                S.op("vector", lambda e, m=m: e.tensor_scalar(
                    out=tb[:, 0, 0:512], in0=psq, scalar1=b2c[:, m:m + 1],
                    scalar2=RNE_C, op0=ALU.add, op1=ALU.add))
                S.op("vector", lambda e: e.tensor_scalar(
                    out=tb[:, 1, 0:512], in0=tb[:, 0, 0:512], scalar1=RNE_C,
                    scalar2=1.0 / 256.0, op0=ALU.subtract, op1=ALU.mult))
                S.dma(tb[:, 2, 0:512], x[m * 128:(m + 1) * 128, f4 * 512:(f4 + 1) * 512])
                S.dma(tb[:, 3, 0:512], attn0[m * 128:(m + 1) * 128, f4 * 512:(f4 + 1) * 512])
                S.op("vector", lambda e: e.tensor_add(
                    out=tb[:, 0, 0:512], in0=tb[:, 2, 0:512], in1=tb[:, 3, 0:512]))
                S.op("vector", lambda e: e.tensor_add(
                    out=tb[:, 1, 0:512], in0=tb[:, 0, 0:512], in1=tb[:, 1, 0:512]))
                S.dma(x1[m * 128:(m + 1) * 128, f4 * 512:(f4 + 1) * 512],
                      tb[:, 1, 0:512], queue="gpsimd")


def build_l0out():
    nc, stack, ctx = new_prog()
    S = Seq(ctx)
    with stack:
        io = l0out_io(nc, lambda n, s, d, o: (dout if o else din)(nc, n, s, d))
        ctx.psA = ctx.psum("psA", [128, 4, 512], F32)
        ctx.psB = ctx.psum("psB", [128, 4, 512], F32)
        ctx.phase("l0out")
        emit_l0out(nc, ctx, S, io)
        S.done()
    return nc



def l1out_io(nc, mk):
    io = {}
    io["y"] = mk("y", (DI, T), F32, 0)
    io["u_f"] = mk("u_f", (DI, T), F32, 0)
    io["z_bf"] = mk("z_bf", (DI, T), BF16, 0)
    io["dskip"] = mk("dskip", (DI,), BF16, 0)
    io["opT"] = mk("opT", (DI, D), BF16, 0)
    io["xn_bf"] = mk("xn_bf", (D, T), BF16, 0)
    io["f1t_bf"] = mk("f1t_bf", (D, 4096), BF16, 0)
    io["w1s"] = mk("w1s", (4096,), F32, 0)
    io["f2t_bf"] = mk("f2t_bf", (D, 4096), BF16, 0)
    io["w2s"] = mk("w2s", (4096,), F32, 0)
    io["h1t_bf"] = mk("h1t_bf", (4096, 4096), BF16, 0)
    io["h1s"] = mk("h1s", (4096,), F32, 0)
    io["h2t_bf"] = mk("h2t_bf", (4096, 4096), BF16, 0)
    io["h2s"] = mk("h2s", (4096,), F32, 0)
    io["w3t_bf"] = mk("w3t_bf", (4096, D), BF16, 0)
    io["w3s"] = mk("w3s", (D,), F32, 0)
    io["x1"] = mk("x1", (D, T), F32, 0)
    io["attn1"] = mk("attn1", (D, T), BF16, 1)
    io["gf"] = mk("gf", (4096, T), BF16, 1)
    io["a1"] = mk("a1", (4096, T), BF16, 1)
    io["a2"] = mk("a2", (4096, T), BF16, 1)
    io["out"] = mk("out", (D, T), F32, 1)
    return io


def emit_l1out(nc, ctx, S, io):
    H = T // 2
    KD = D // 128

    y1 = io["y"]
    u1 = io["u_f"]
    z1 = io["z_bf"]
    dskip = io["dskip"]
    opt = io["opT"]
    xn1 = io["xn_bf"]
    f1t = io["f1t_bf"]
    w1s = io["w1s"]
    f2t = io["f2t_bf"]
    w2s = io["w2s"]
    h1t = io["h1t_bf"]
    h1s = io["h1s"]
    h2t = io["h2t_bf"]
    h2s = io["h2s"]
    w3t = io["w3t_bf"]
    w3s = io["w3s"]
    x1 = io["x1"]
    attn1 = io["attn1"]
    gf = io["gf"]
    a1 = io["a1"]
    a2 = io["a2"]
    out = io["out"]

    if True:
        gbuf = ctx.sbuf("gbuf", [128, 16, T], BF16)
        t0 = ctx.sbuf("t0", [128, T], F32)
        t1 = ctx.sbuf("t1", [128, T], F32)
        zt = ctx.sbuf("zt", [128, T], BF16)
        dcr = ctx.sbuf("dcr", [128, 16], BF16)
        dcol = ctx.sbuf("dcol", [128, 16], F32)
        wm = ctx.sbuf("wm", [128, 16, 128], BF16)
        wm2 = ctx.sbuf("wm2", [128, 16, 128], BF16)
        ab = ctx.sbuf("ab", [128, H], BF16)
        s1c = ctx.sbuf("s1c", [128, 32], F32)
        s2c = ctx.sbuf("s2c", [128, 32], F32)
        sh1c = ctx.sbuf("sh1c", [128, 32], F32)
        sh2c = ctx.sbuf("sh2c", [128, 32], F32)
        s3c = ctx.sbuf("s3c", [128, 8], F32)
        rq = ctx.sbuf("rq", [128, 32, 512], BF16)
        w32 = ctx.sbuf("w32", [128, 32, 128], BF16)
        ps_big = ctx.psA
        psb2 = ps_big[:, 0:2, :].rearrange("p a f -> p (a f)")
        psc2 = ps_big[:, 2:4, :].rearrange("p a f -> p (a f)")
        psq = ps_big[:, 0:1, :].rearrange("p a f -> p (a f)")

        S.dma(dcr[:], dskip.rearrange("(g p) -> p g", p=128), slow=True)
        S.op("vector", lambda e: e.tensor_copy(out=dcol[:], in_=dcr[:]))
        S.dma(s1c[:], w1s.rearrange("(g p) -> p g", p=128), slow=True)
        S.dma(s2c[:], w2s.rearrange("(g p) -> p g", p=128), slow=True)
        S.dma(sh1c[:], h1s.rearrange("(g p) -> p g", p=128), slow=True)
        S.dma(sh2c[:], h2s.rearrange("(g p) -> p g", p=128), slow=True)
        S.dma(s3c[:], w3s.rearrange("(g p) -> p g", p=128), slow=True)

        for g in range(16):
            S.dma(t0[:], y1[g * 128:(g + 1) * 128, :])
            S.dma(t1[:], u1[g * 128:(g + 1) * 128, :])
            S.dma(zt[:], z1[g * 128:(g + 1) * 128, :])
            S.op("vector", lambda e, g=g: e.scalar_tensor_tensor(
                out=t0[:], in0=t1[:], scalar=dcol[:, g:g + 1], in1=t0[:],
                op0=ALU.mult, op1=ALU.add))
            S.op("scalar", lambda e: e.activation(out=t1[:], in_=zt[:], func=AF.Silu))
            S.op("vector", lambda e, g=g: e.tensor_mul(
                out=gbuf[:, g, :], in0=t0[:], in1=t1[:]))

        for m in range(KD):
            S.dma(wm[:], opt.rearrange("(k p) m -> p k m", p=128)[:, :, m * 128:(m + 1) * 128])
            for th in range(2):
                for k in range(16):
                    for j in range(2):
                        S.op("tensor", lambda e, k=k, th=th, j=j: e.matmul(
                            out=ps_big[:, j, :], lhsT=wm[:, k, :],
                            rhs=gbuf[:, k, th * H + j * 512:th * H + (j + 1) * 512],
                            start=(k == 0), stop=(k == 15)))
                S.op("vector", lambda e: e.tensor_copy(out=ab[:], in_=psb2))
                S.dma(attn1[m * 128:(m + 1) * 128, th * H:(th + 1) * H],
                      ab[:], queue="gpsimd")

        for k in range(KD):
            S.dma(gbuf[:, k, :], xn1[k * 128:(k + 1) * 128, :])
        for m in range(32):
            S.dma(wm[:, 0:KD, :], f1t.rearrange("(k p) m -> p k m", p=128)[:, :, m * 128:(m + 1) * 128])
            S.dma(wm2[:, 0:KD, :], f2t.rearrange("(k p) m -> p k m", p=128)[:, :, m * 128:(m + 1) * 128])
            for th in range(2):
                for k in range(KD):
                    for j in range(2):
                        S.op("tensor", lambda e, k=k, th=th, j=j: e.matmul(
                            out=ps_big[:, j, :], lhsT=wm[:, k, :],
                            rhs=gbuf[:, k, th * H + j * 512:th * H + (j + 1) * 512],
                            start=(k == 0), stop=(k == KD - 1)))
                for k in range(KD):
                    for j in range(2):
                        S.op("tensor", lambda e, k=k, th=th, j=j: e.matmul(
                            out=ps_big[:, 2 + j, :], lhsT=wm2[:, k, :],
                            rhs=gbuf[:, k, th * H + j * 512:th * H + (j + 1) * 512],
                            start=(k == 0), stop=(k == KD - 1)))
                S.op("scalar", lambda e, m=m: e.activation(
                    out=t0[:, 0:H], in_=psb2, func=AF.Silu, scale=s1c[:, m:m + 1]))
                S.op("scalar", lambda e, m=m: e.activation(
                    out=t1[:, 0:H], in_=psc2, func=AF.Identity, scale=s2c[:, m:m + 1]))
                S.op("vector", lambda e: e.tensor_mul(
                    out=ab[:], in0=t0[:, 0:H], in1=t1[:, 0:H]))
                S.dma(gf[m * 128:(m + 1) * 128, th * H:(th + 1) * H],
                      ab[:], queue="gpsimd")

        for (src, dst, wmat, scol) in ((gf, a1, h1t, sh1c), (a1, a2, h2t, sh2c)):
            for f4 in range(4):
                for k in range(32):
                    S.dma(rq[:, k, :], src[k * 128:(k + 1) * 128,
                                           f4 * 512:(f4 + 1) * 512])
                for m in range(32):
                    S.dma(w32[:], wmat.rearrange("(k p) m -> p k m", p=128)[:, :, m * 128:(m + 1) * 128])
                    for k in range(32):
                        S.op("tensor", lambda e, k=k: e.matmul(
                            out=psq, lhsT=w32[:, k, :], rhs=rq[:, k, :],
                            start=(k == 0), stop=(k == 31)))
                    S.op("scalar", lambda e, m=m, scol=scol: e.activation(
                        out=ab[:, 0:512], in_=psq, func=AF.Gelu,
                        scale=scol[:, m:m + 1]))
                    S.dma(dst[m * 128:(m + 1) * 128, f4 * 512:(f4 + 1) * 512],
                          ab[:, 0:512], queue="gpsimd")

        for f4 in range(4):
            for k in range(32):
                S.dma(rq[:, k, :], a2[k * 128:(k + 1) * 128, f4 * 512:(f4 + 1) * 512])
            for m in range(KD):
                S.dma(w32[:], w3t.rearrange("(k p) m -> p k m", p=128)[:, :, m * 128:(m + 1) * 128])
                for k in range(32):
                    S.op("tensor", lambda e, k=k: e.matmul(
                        out=psq, lhsT=w32[:, k, :], rhs=rq[:, k, :],
                        start=(k == 0), stop=(k == 31)))
                S.op("scalar", lambda e, m=m: e.activation(
                    out=t0[:, 0:512], in_=psq, func=AF.Identity,
                    scale=s3c[:, m:m + 1]))
                S.dma(t1[:, 0:512], x1[m * 128:(m + 1) * 128, f4 * 512:(f4 + 1) * 512])
                S.dma(ab[:, 0:512], attn1[m * 128:(m + 1) * 128, f4 * 512:(f4 + 1) * 512])
                S.op("vector", lambda e: e.tensor_add(
                    out=t1[:, 0:512], in0=t1[:, 0:512], in1=ab[:, 0:512]))
                S.op("vector", lambda e: e.tensor_add(
                    out=t0[:, 0:512], in0=t0[:, 0:512], in1=t1[:, 0:512]))
                S.dma(out[m * 128:(m + 1) * 128, f4 * 512:(f4 + 1) * 512],
                      t0[:, 0:512], queue="gpsimd")


def build_l1out():
    nc, stack, ctx = new_prog()
    S = Seq(ctx)
    with stack:
        io = l1out_io(nc, lambda n, s, d, o: (dout if o else din)(nc, n, s, d))
        ctx.psA = ctx.psum("psA", [128, 4, 512], F32)
        ctx.psB = ctx.psum("psB", [128, 4, 512], F32)
        ctx.phase("l1out")
        emit_l1out2(nc, ctx, S, io)
        S.done()
    return nc


_PROGS = {}


def _prog(name, builder):
    if name not in _PROGS:
        _PROGS[name] = builder()
    return _PROGS[name]


def _run(name, builder, maps):
    res = run_bass_kernel_spmd(_prog(name, builder), maps, core_ids=list(range(8)))
    return res.results


def kernel(**inp):
    asc = np.ascontiguousarray
    g = {k: np.asarray(v) for k, v in inp.items()}
    x = g["L0_attn_x"]
    xfm = [asc(x[c].T) for c in range(8)]

    f1T = asc(g["L1_ffn_w1_w"].T)
    f2T = asc(g["L1_ffn_w2_w"].T)
    h1T = asc(g["L1_ffn_h1_w"].T)
    h2T = asc(g["L1_ffn_h2_w"].T)
    w3T = asc(g["L1_ffn_w3_w"].T)
    ipT = asc(g["L1_attn_in_proj_w"].T)
    opT = asc(g["L1_attn_out_proj_w"].T)
    xpT = asc(g["L1_attn_x_proj_w"].T)
    dpT = asc(g["L1_attn_dt_proj_w"].T)
    cw0 = asc(g["L0_attn_conv_w"].reshape(DI, 4))
    cw1 = asc(g["L1_attn_conv_w"].reshape(DI, 4))

    if FUSED:
        return kernel_fused(g, xfm, (f1T, f2T, h1T, h2T, w3T, ipT, opT, xpT,
                                     dpT, cw0, cw1))

    pm = {"win": g["L0_attn_W_in"], "wout": g["L0_attn_W_out"],
          "w1": g["L0_ffn_w1"], "w2": g["L0_ffn_w2"],
          "wxp": g["L0_attn_W_xproj"],
          "f1t": f1T, "f2t": f2T, "h1t": h1T, "h2t": h2T, "w3t": w3T}
    W = _run("prep", build_prep, [pm] * 8)[0]

    mA = [{"x": xfm[c], "lnw": g["L0_attn_ln_w"], "lnb": g["L0_attn_ln_b"],
           "win_bf": W["win_bf"], "convw": cw0, "convb": g["L0_attn_conv_b"],
           "wxp_bf": W["wxp_bf"], "wdt": g["L0_attn_W_dt"],
           "bdt": g["L0_attn_b_dt"]} for c in range(8)]
    rA = _run("l0a", lambda: build_lxa(0), mA)

    mS = [{"delta": rA[c]["delta"], "du": rA[c]["du"],
           "du_f16": rA[c]["du_f16"], "Bm": rA[c]["B0"],
           "Cm": rA[c]["C0"], "alog": g["L0_attn_A_log"]} for c in range(8)]
    rS = _run("scan0", lambda: build_scan(0), mS)

    mO = [{"y": rS[c]["y"], "u_f": rA[c]["u_f"], "z_bf": rA[c]["z_bf"],
           "d0": g["L0_attn_D"], "wout_bf": W["wout_bf"], "xn_f": rA[c]["xn_f"],
           "xn_bf": rA[c]["xn_bf"], "w1_bf": W["w1_bf"], "b1": g["L0_ffn_b1"],
           "w2_bf": W["w2_bf"], "b2": g["L0_ffn_b2"], "x": xfm[c]}
          for c in range(8)]
    rO = _run("l0out", build_l0out, mO)

    mA1 = [{"x": rO[c]["x1"], "win_bf": ipT, "convw": cw1,
            "convb": g["L1_attn_conv_b"], "wxp_bf": xpT, "wdt": dpT,
            "bdt": g["L1_attn_dt_proj_b"]} for c in range(8)]
    rA1 = _run("l1a", lambda: build_lxa(1), mA1)

    mS1 = [{"delta": rA1[c]["delta"], "du": rA1[c]["du"],
            "du_f16": rA1[c]["du_f16"], "Bm": rA1[c]["B0"],
            "Cm": rA1[c]["C0"], "alog": g["L1_attn_A_log"]} for c in range(8)]
    rS1 = _run("scan1", lambda: build_scan(1), mS1)

    mO1 = [{"y": rS1[c]["y"], "u_f": rA1[c]["u_f"], "z_bf": rA1[c]["z_bf"],
            "dskip": g["L1_attn_D_skip"], "opT": opT, "xn_bf": rA1[c]["xn_bf"],
            "f1t_bf": W["f1t_bf"], "w1s": g["L1_ffn_w1_s"],
            "f2t_bf": W["f2t_bf"], "w2s": g["L1_ffn_w2_s"],
            "h1t_bf": W["h1t_bf"], "h1s": g["L1_ffn_h1_s"],
            "h2t_bf": W["h2t_bf"], "h2s": g["L1_ffn_h2_s"],
            "w3t_bf": W["w3t_bf"], "w3s": g["L1_ffn_w3_s"],
            "x1": rO[c]["x1"]} for c in range(8)]
    rO1 = _run("l1out", build_l1out, mO1)

    return np.stack([asc(rO1[c]["out"].T) for c in range(8)], axis=0)



def build_fused():
    nc, stack, ctx = new_prog()
    S = Seq(ctx)
    with stack:
        ctx.psA = ctx.psum("psA", [128, 4, 512], F32)
        ctx.psB = ctx.psum("psB", [128, 4, 512], F32)

        def D_(n, s, d):
            return nc.dram_tensor(n, list(s), d, kind="Internal").ap()

        iop = prep_io(nc, lambda n, s, d, o: D_(n, s, d) if o else din(nc, n, s, d))
        ctx.phase("prep")
        emit_prep(nc, ctx, S, iop)

        def mk0(n, s, d, o):
            if n in ("win_bf", "wxp_bf"):
                return iop[n]
            if o:
                return D_("l0_" + n, s, d)
            return din(nc, "l0_" + n, s, d)
        io0 = lxa_io(nc, 0, mk0)
        ctx.phase("l0a")
        emit_lxa(nc, ctx, S, io0, 0)

        y0 = D_("y0", (DI, T), F32)
        ios0 = {"delta": io0["delta"], "du": io0["du"],
                "du_f16": io0["du_f16"], "Bm": io0["B0"],
                "Cm": io0["C0"], "alog": din(nc, "alog0", (DI, 64), F32),
                "y": y0}
        ctx.phase("scan0")
        emit_scan2(nc, ctx, S, ios0, 0)

        def mko0(n, s, d, o):
            m = {"y": y0, "u_f": io0["u_f"], "z_bf": io0["z_bf"],
                 "wout_bf": iop["wout_bf"], "xn_f": io0["xn_f"],
                 "xn_bf": io0["xn_bf"], "w1_bf": iop["w1_bf"],
                 "w2_bf": iop["w2_bf"], "x": io0["x"]}
            if n in m:
                return m[n]
            if o:
                return D_("o0_" + n, s, d)
            return din(nc, "o0_" + n, s, d)
        ioo0 = l0out_io(nc, mko0)
        ctx.phase("l0out")
        emit_l0out(nc, ctx, S, ioo0)

        def mk1(n, s, d, o):
            if n == "x":
                return ioo0["x1"]
            if o:
                return D_("l1_" + n, s, d)
            return din(nc, "l1_" + n, s, d)
        io1 = lxa_io(nc, 1, mk1)
        ctx.phase("l1a")
        emit_lxa(nc, ctx, S, io1, 1)

        y1 = D_("y1", (DI, T), F32)
        ios1 = {"delta": io1["delta"], "du": io1["du"],
                "du_f16": io1["du_f16"], "Bm": io1["B0"],
                "Cm": io1["C0"], "alog": din(nc, "alog1", (DI, 16), BF16),
                "y": y1}
        ctx.phase("scan1")
        emit_scan2(nc, ctx, S, ios1, 1)

        def mko1(n, s, d, o):
            m = {"y": y1, "u_f": io1["u_f"], "z_bf": io1["z_bf"],
                 "xn_bf": io1["xn_bf"], "f1t_bf": iop["f1t_bf"],
                 "f2t_bf": iop["f2t_bf"], "h1t_bf": iop["h1t_bf"],
                 "h2t_bf": iop["h2t_bf"], "w3t_bf": iop["w3t_bf"],
                 "x1": ioo0["x1"]}
            if n in m:
                return m[n]
            if n == "out":
                return dout(nc, "out", s, d)
            if o:
                return D_("o1_" + n, s, d)
            return din(nc, "o1_" + n, s, d)
        ioo1 = l1out_io(nc, mko1)
        ctx.phase("l1out")
        emit_l1out2(nc, ctx, S, ioo1)
        S.done()
    return nc


FUSED = True


def kernel_fused(g, xfm, hosted):
    (f1T, f2T, h1T, h2T, w3T, ipT, opT, xpT, dpT, cw0, cw1) = hosted
    base = {"win": g["L0_attn_W_in"], "wout": g["L0_attn_W_out"],
            "w1": g["L0_ffn_w1"], "w2": g["L0_ffn_w2"],
            "wxp": g["L0_attn_W_xproj"],
            "f1t": f1T, "f2t": f2T, "h1t": h1T, "h2t": h2T, "w3t": w3T,
            "l0_lnw": g["L0_attn_ln_w"], "l0_lnb": g["L0_attn_ln_b"],
            "l0_convw": cw0, "l0_convb": g["L0_attn_conv_b"],
            "l0_wdt": g["L0_attn_W_dt"], "l0_bdt": g["L0_attn_b_dt"],
            "alog0": g["L0_attn_A_log"],
            "o0_d0": g["L0_attn_D"], "o0_b1": g["L0_ffn_b1"],
            "o0_b2": g["L0_ffn_b2"],
            "l1_win_bf": ipT, "l1_convw": cw1, "l1_convb": g["L1_attn_conv_b"],
            "l1_wxp_bf": xpT, "l1_wdt": dpT, "l1_bdt": g["L1_attn_dt_proj_b"],
            "alog1": g["L1_attn_A_log"],
            "o1_dskip": g["L1_attn_D_skip"], "o1_opT": opT,
            "o1_w1s": g["L1_ffn_w1_s"], "o1_w2s": g["L1_ffn_w2_s"],
            "o1_h1s": g["L1_ffn_h1_s"], "o1_h2s": g["L1_ffn_h2_s"],
            "o1_w3s": g["L1_ffn_w3_s"]}
    maps = [dict(base, l0_x=xfm[c]) for c in range(8)]
    res = _run("fused", build_fused, maps)
    return np.stack([np.ascontiguousarray(res[c]["out"].T) for c in range(8)],
                    axis=0)



def emit_scan2(nc, ctx, S, io, layer):
    NST = 64 if layer == 0 else 16
    a_dt = F32 if layer == 0 else BF16
    tg = f"sc{layer}_"
    W, P, A, V, Y = tg + "rw", tg + "pe", tg + "act", tg + "dve", tg + "yad"
    PD, PA, PV, PP, YST = tg + "pd", tg + "pa", tg + "pv", tg + "pp", tg + "yst"

    delta = io["delta"]
    duf16 = io["du_f16"]
    Bm = io["Bm"]
    Cm = io["Cm"]
    alog = io["alog"]
    y = io["y"]

    q0 = ctx.val("q")
    nb = NST
    NB = 16 * nb

    dl = ctx.sbuf("dl", [128, 2, T], F32)
    duf = ctx.sbuf("duf", [128, 2, T], F16)
    alr = ctx.sbuf("alr", [128, 2, NST], a_dt)
    asb0 = ctx.sbuf("asb0", [128, 2, NST], F32)
    asb = ctx.sbuf("asb", [128, 2, NST], F32)
    brow = ctx.sbuf("brow", [1, 2, T], F32)
    crow = ctx.sbuf("crow", [1, 2, T], F32)
    dat = ctx.sbuf("dat", [128, 2, T], F16)
    besb = ctx.sbuf("besb", [128, 2, T], F16)
    cesb = ctx.sbuf("cesb", [128, 2, T], F16)
    dbut = ctx.sbuf("dbut", [128, 2, T], F16)
    ht = ctx.sbuf("ht", [128, 2, T], F16)
    hct = ctx.sbuf("hct", [128, 2, T], F16)
    yac = ctx.sbuf("yac", [128, 2, T], F32)
    ones_row = ctx.sbuf("ones_row", [1, 128], F32)
    psB, psC = ctx.psA, ctx.psB

    S.op("vector", lambda e: e.memset(ones_row[:], 1.0))
    q0 = ctx.val("q")

    def qw(i):
        return [("q", q0)] if i == 0 else []

    for dg in range(16):
        g2 = dg % 2
        pw = [("q", q0)] if dg < 2 else [(A, 3 * nb * (dg - 1)), (V, 3 * nb * (dg - 1))]
        ctx.emit("sync", (lambda dg=dg, g2=g2: lambda e: e.dma_start(
            out=dl[:, g2, :], in_=delta[dg * 128:(dg + 1) * 128, :]))(),
            waits=tuple(pw), fire=ctx.fire(PD, 16))
        ctx.emit("sync", (lambda dg=dg, g2=g2: lambda e: e.dma_start(
            out=duf[:, g2, :], in_=duf16[dg * 128:(dg + 1) * 128, :]))(),
            fire=ctx.fire(PD, 16))
        ctx.emit("sync", (lambda dg=dg, g2=g2: lambda e: e.dma_start(
            out=alr[:, g2, :], in_=alog[dg * 128:(dg + 1) * 128, :]))(),
            fire=ctx.fire(PD, 16))
        ctx.emit("scalar", (lambda g2=g2: lambda e: e.activation(
            out=asb0[:, g2, :], in_=alr[:, g2, :], func=AF.Exp))(),
            waits=((PD, 48 * (dg + 1)), (PV, max(0, dg - 1))),
            fire=ctx.fire(PA, 1))
        ctx.emit("vector", (lambda g2=g2: lambda e: e.tensor_single_scalar(
            out=asb[:, g2, :], in_=asb0[:, g2, :], scalar=-1.0, op=ALU.mult))(),
            waits=((PA, dg + 1),) + ((A, 3 * nb * (dg - 1)),) if dg >= 2 else ((PA, dg + 1),),
            fire=ctx.fire(PV, 1))
        ctx.emit("gpsimd", (lambda g2=g2: lambda e: e.memset(yac[:, g2, :], 0.0))(),
            waits=((YST, 16 * max(0, dg - 1)),), fire=ctx.fire(PP, 1))

        for n in range(nb):
            i = dg * nb + n
            p = i % 2
            ctx.emit("sync", (lambda n=n, p=p: lambda e: e.dma_start(
                out=brow[:, p, :], in_=Bm[n:n + 1, :]))(),
                waits=tuple(qw(i) + ([(P, i - 1)] if i >= 2 else [])),
                fire=ctx.fire(W, 16))
            ctx.emit("sync", (lambda n=n, p=p: lambda e: e.dma_start(
                out=crow[:, p, :], in_=Cm[n:n + 1, :]))(),
                fire=ctx.fire(W, 16))
            for fc in range(4):
                wts = [(W, 32 * (i + 1)), (A, 3 * i)] if fc == 0 else []
                ctx.emit("tensor", (lambda p=p, fc=fc: lambda e: e.matmul(
                    out=psB[:, fc, :], lhsT=ones_row[:],
                    rhs=brow[:, p, fc * 512:(fc + 1) * 512],
                    start=True, stop=True))(),
                    waits=tuple(qw(i) + wts))
            for fc in range(4):
                ctx.emit("tensor", (lambda p=p, fc=fc: lambda e: e.matmul(
                    out=psC[:, fc, :], lhsT=ones_row[:],
                    rhs=crow[:, p, fc * 512:(fc + 1) * 512],
                    start=True, stop=True))(),
                    fire=ctx.fire(P, 1) if fc == 3 else None)
            ctx.emit("scalar", (lambda g2=g2, n=n, p=p: lambda e: e.activation(
                out=dat[:, p, :], in_=dl[:, g2, :], func=AF.Exp,
                scale=asb[:, g2, n:n + 1]))(),
                waits=tuple(qw(i) + [(PV, dg + 1)] + ([(V, 3 * (i - 1))] if i >= 2 else [])),
                fire=ctx.fire(A, 1))
            ctx.emit("scalar", (lambda p=p: lambda e: e.activation(
                out=besb[:, p, :],
                in_=psB[:].rearrange("p a f -> p (a f)"),
                func=AF.Identity))(),
                waits=((P, i + 1),), fire=ctx.fire(A, 1))
            ctx.emit("scalar", (lambda p=p: lambda e: e.activation(
                out=cesb[:, p, :],
                in_=psC[:].rearrange("p a f -> p (a f)"),
                func=AF.Identity))(),
                fire=ctx.fire(A, 1))
            ctx.emit("vector", (lambda g2=g2, p=p: lambda e: e.tensor_mul(
                out=dbut[:, p, :], in0=duf[:, g2, :], in1=besb[:, p, :]))(),
                waits=tuple(qw(i) + [(A, 3 * i + 2)]),
                fire=ctx.fire(V, 1))
            ctx.emit("vector", (lambda p=p: lambda e: e.tensor_tensor_scan(
                out=ht[:, p, :], data0=dat[:, p, :], data1=dbut[:, p, :],
                initial=0.0, op0=ALU.mult, op1=ALU.add))(),
                waits=((V, 3 * i + 1),), fire=ctx.fire(V, 1))
            ctx.emit("vector", (lambda p=p: lambda e: e.tensor_mul(
                out=hct[:, p, :], in0=ht[:, p, :], in1=cesb[:, p, :]))(),
                waits=((V, 3 * i + 2), (A, 3 * i + 3)) + (((Y, i - 1),) if i >= 2 else ()),
                fire=ctx.fire(V, 1))
            ctx.emit("gpsimd", (lambda g2=g2, p=p: lambda e: e.tensor_add(
                out=yac[:, g2, :], in0=yac[:, g2, :], in1=hct[:, p, :]))(),
                waits=((V, 3 * i + 3), (Y, i), (PP, dg + 1)),
                fire=ctx.fire(Y, 1))
        ctx.emit("gpsimd", (lambda dg=dg, g2=g2: lambda e: e.dma_start(
            out=y[dg * 128:(dg + 1) * 128, :], in_=yac[:, g2, :]))(),
            waits=((Y, nb * (dg + 1)),), fire=ctx.fire(YST, 16))
    ctx.emit("vector", lambda e: e.engine_nop(),
             waits=((YST, 16 * 16),), fire=ctx.fire("q", 1))



def emit_l1out2(nc, ctx, S, io):
    H = T // 2
    KD = D // 128
    tg = "o1_"

    y1, u1, z1 = io["y"], io["u_f"], io["z_bf"]
    dskip, opt, xn1 = io["dskip"], io["opT"], io["xn_bf"]
    f1t, w1s, f2t, w2s = io["f1t_bf"], io["w1s"], io["f2t_bf"], io["w2s"]
    h1t, h1s, h2t, h2s = io["h1t_bf"], io["h1s"], io["h2t_bf"], io["h2s"]
    w3t, w3s, x1 = io["w3t_bf"], io["w3s"], io["x1"]
    attn1, gf, a1, a2, out = io["attn1"], io["gf"], io["a1"], io["a2"], io["out"]

    gbuf = ctx.sbuf("gbuf", [128, 16, T], BF16)
    t0 = ctx.sbuf("t0", [128, T], F32)
    t1 = ctx.sbuf("t1", [128, T], F32)
    zt = ctx.sbuf("zt", [128, T], BF16)
    dcr = ctx.sbuf("dcr", [128, 16], BF16)
    dcol = ctx.sbuf("dcol", [128, 16], F32)
    wm = ctx.sbuf("wm", [128, 2, 16, 128], BF16)
    wm2 = ctx.sbuf("wm2", [128, 2, 16, 128], BF16)
    ab = ctx.sbuf("ab", [128, 2, H], BF16)
    ev0 = ctx.sbuf("ev0", [128, 2, H], BF16)
    ev1 = ctx.sbuf("ev1", [128, 2, H], BF16)
    s1c = ctx.sbuf("s1c", [128, 32], F32)
    s2c = ctx.sbuf("s2c", [128, 32], F32)
    sh1c = ctx.sbuf("sh1c", [128, 32], F32)
    sh2c = ctx.sbuf("sh2c", [128, 32], F32)
    s3c = ctx.sbuf("s3c", [128, 8], F32)
    rq = ctx.sbuf("rq", [128, 2, 32, 512], BF16)
    w32 = ctx.sbuf("w32", [128, 2, 32, 128], BF16)
    fx = ctx.sbuf("fx", [128, 2, 512], F32)
    fa = ctx.sbuf("fa", [128, 2, 512], BF16)
    fo = ctx.sbuf("fo", [128, 2, 512], F32)
    ps = ctx.psA

    q0 = ctx.val("q")
    S.dma(dcr[:], dskip.rearrange("(g p) -> p g", p=128), slow=True)
    S.op("vector", lambda e: e.tensor_copy(out=dcol[:], in_=dcr[:]))
    S.dma(s1c[:], w1s.rearrange("(g p) -> p g", p=128), slow=True)
    S.dma(s2c[:], w2s.rearrange("(g p) -> p g", p=128), slow=True)
    S.dma(sh1c[:], h1s.rearrange("(g p) -> p g", p=128), slow=True)
    S.dma(sh2c[:], h2s.rearrange("(g p) -> p g", p=128), slow=True)
    S.dma(s3c[:], w3s.rearrange("(g p) -> p g", p=128), slow=True)
    q0 = ctx.val("q")

    for g in range(16):
        S.dma(t0[:], y1[g * 128:(g + 1) * 128, :])
        S.dma(t1[:], u1[g * 128:(g + 1) * 128, :])
        S.dma(zt[:], z1[g * 128:(g + 1) * 128, :])
        S.op("vector", (lambda g=g: lambda e: e.scalar_tensor_tensor(
            out=t0[:], in0=t1[:], scalar=dcol[:, g:g + 1],
            in1=t0[:], op0=ALU.mult, op1=ALU.add))())
        S.op("scalar", lambda e: e.activation(out=t1[:], in_=zt[:], func=AF.Silu))
        S.op("vector", (lambda g=g: lambda e: e.tensor_mul(
            out=gbuf[:, g, :], in0=t0[:], in1=t1[:]))())
    q1 = ctx.val("q")

    BL, BP, BE, BS = tg + "bl", tg + "bp", tg + "be", tg + "bs"
    optr = opt.rearrange("(k p) m -> p k m", p=128)
    for m in range(KD):
        pw = m % 2
        ctx.emit("sync", (lambda m=m, pw=pw: lambda e: e.dma_start(
            out=wm[:, pw, :, :], in_=optr[:, :, m * 128:(m + 1) * 128]))(),
            waits=tuple([("q", q1)] + ([(BP, 2 * (m - 1))] if m >= 2 else [])),
            fire=ctx.fire(BL, 16))
        for th in range(2):
            it = m * 2 + th
            sl = (it % 2) * 2
            for k in range(16):
                for j in range(2):
                    wts = []
                    if k == 0 and j == 0:
                        wts = [(BL, 16 * (m + 1)), ("q", q1)]
                        if it >= 2:
                            wts.append((BE, it - 1))
                    ctx.emit("tensor", (lambda k=k, th=th, j=j, sl=sl, pw=pw: lambda e: e.matmul(
                        out=ps[:, sl + j, :], lhsT=wm[:, pw, k, :],
                        rhs=gbuf[:, k, th * H + j * 512:th * H + (j + 1) * 512],
                        start=(k == 0), stop=(k == 15)))(),
                        waits=tuple(wts),
                        fire=ctx.fire(BP, 1) if (k, j) == (15, 1) else None)
            ctx.emit("vector", (lambda it=it, sl=sl: lambda e: e.tensor_copy(
                out=ab[:, it % 2, :],
                in_=ps[:, sl:sl + 2, :].rearrange("p a f -> p (a f)")))(),
                waits=tuple([(BP, it + 1)] + ([(BS, 16 * (it - 1))] if it >= 2 else [])),
                fire=ctx.fire(BE, 1))
            ctx.emit("gpsimd", (lambda m=m, th=th, it=it: lambda e: e.dma_start(
                out=attn1[m * 128:(m + 1) * 128, th * H:(th + 1) * H],
                in_=ab[:, it % 2, :]))(),
                waits=((BE, it + 1),), fire=ctx.fire(BS, 16))

    CL, CP, CE, CS, CX = tg + "cl", tg + "cp", tg + "ce", tg + "cs", tg + "cx"
    f1r = f1t.rearrange("(k p) m -> p k m", p=128)
    f2r = f2t.rearrange("(k p) m -> p k m", p=128)
    for k in range(KD):
        ctx.emit("sync", (lambda k=k: lambda e: e.dma_start(
            out=gbuf[:, k, :], in_=xn1[k * 128:(k + 1) * 128, :]))(),
            waits=((BP, 16),) if k == 0 else (), fire=ctx.fire(CX, 16))
    for m in range(32):
        pw = m % 2
        w = [(CP, 2 * (m - 1))] if m >= 2 else []
        ctx.emit("sync", (lambda m=m, pw=pw: lambda e: e.dma_start(
            out=wm[:, pw, 0:KD, :], in_=f1r[:, :, m * 128:(m + 1) * 128]))(),
            waits=tuple(w), fire=ctx.fire(CL, 16))
        ctx.emit("sync", (lambda m=m, pw=pw: lambda e: e.dma_start(
            out=wm2[:, pw, 0:KD, :], in_=f2r[:, :, m * 128:(m + 1) * 128]))(),
            fire=ctx.fire(CL, 16))
        for th in range(2):
            it = m * 2 + th
            for k in range(KD):
                for j in range(2):
                    wts = []
                    if k == 0 and j == 0:
                        wts = [(CL, 32 * (m + 1)), (CX, 16 * KD)]
                        if it >= 1:
                            wts.append((CE, 3 * it - 1))
                    ctx.emit("tensor", (lambda k=k, th=th, j=j, pw=pw: lambda e: e.matmul(
                        out=ps[:, j, :], lhsT=wm[:, pw, k, :],
                        rhs=gbuf[:, k, th * H + j * 512:th * H + (j + 1) * 512],
                        start=(k == 0), stop=(k == KD - 1)))(),
                        waits=tuple(wts))
            for k in range(KD):
                for j in range(2):
                    ctx.emit("tensor", (lambda k=k, th=th, j=j, pw=pw: lambda e: e.matmul(
                        out=ps[:, 2 + j, :], lhsT=wm2[:, pw, k, :],
                        rhs=gbuf[:, k, th * H + j * 512:th * H + (j + 1) * 512],
                        start=(k == 0), stop=(k == KD - 1)))(),
                        fire=ctx.fire(CP, 1) if (k, j) == (KD - 1, 1) else None)
            pb = it % 2
            ctx.emit("scalar", (lambda m=m, pb=pb: lambda e: e.activation(
                out=ev0[:, pb, :], in_=ps[:, 0:2, :].rearrange("p a f -> p (a f)"),
                func=AF.Silu, scale=s1c[:, m:m + 1]))(),
                waits=tuple([(CP, it + 1)] + ([(CE, 3 * (it - 1))] if it >= 2 else [])),
                fire=ctx.fire(CE, 1))
            ctx.emit("scalar", (lambda m=m, pb=pb: lambda e: e.activation(
                out=ev1[:, pb, :], in_=ps[:, 2:4, :].rearrange("p a f -> p (a f)"),
                func=AF.Identity, scale=s2c[:, m:m + 1]))(),
                fire=ctx.fire(CE, 1))
            ctx.emit("vector", (lambda pb=pb: lambda e: e.tensor_mul(
                out=ab[:, pb, :], in0=ev0[:, pb, :], in1=ev1[:, pb, :]))(),
                waits=tuple([(CE, 3 * it + 2)] + ([(CS, 16 * (it - 1))] if it >= 2 else [])),
                fire=ctx.fire(CE, 1))
            ctx.emit("gpsimd", (lambda m=m, th=th, pb=pb: lambda e: e.dma_start(
                out=gf[m * 128:(m + 1) * 128, th * H:(th + 1) * H],
                in_=ab[:, pb, :]))(),
                waits=((CE, 3 * it + 3),), fire=ctx.fire(CS, 16))

    def hstage(st, src, dst, wmat, scol, prev_done):
        L, R, P, E, X = st + "l", st + "r", st + "p", st + "e", st + "x"
        wr = wmat.rearrange("(k p) m -> p k m", p=128)
        for f4 in range(4):
            pf = f4 % 2
            for k in range(32):
                w = [prev_done] if f4 == 0 and k == 0 else []
                if f4 >= 2:
                    w.append((P, 32 * (f4 - 1)))
                ctx.emit("sync", (lambda k=k, f4=f4, pf=pf: lambda e: e.dma_start(
                    out=rq[:, pf, k, :],
                    in_=src[k * 128:(k + 1) * 128, f4 * 512:(f4 + 1) * 512]))(),
                    waits=tuple(w), fire=ctx.fire(R, 16))
            for m in range(32):
                it = f4 * 32 + m
                pw = it % 2
                w = [prev_done] if it == 0 else []
                if it >= 2:
                    w.append((P, it - 1))
                ctx.emit("scalar", (lambda it=it, m=m, pw=pw: lambda e: e.dma_start(
                    out=w32[:, pw, :, :], in_=wr[:, :, m * 128:(m + 1) * 128]))(),
                    waits=tuple(w), fire=ctx.fire(L, 16))
                for k in range(32):
                    wts = []
                    if k == 0:
                        wts = [(L, 16 * (it + 1)), (R, 512 * (f4 + 1))]
                        if it >= 2:
                            wts.append((E, it - 1))
                    ctx.emit("tensor", (lambda k=k, pw=pw, pf=pf, it=it: lambda e: e.matmul(
                        out=ps[:, it % 4, :], lhsT=w32[:, pw, k, :],
                        rhs=rq[:, pf, k, :], start=(k == 0), stop=(k == 31)))(),
                        waits=tuple(wts),
                        fire=ctx.fire(P, 1) if k == 31 else None)
                ctx.emit("scalar", (lambda it=it, m=m: lambda e: e.activation(
                    out=fa[:, it % 2, :], in_=ps[:, it % 4, :], func=AF.Gelu,
                    scale=scol[:, m:m + 1]))(),
                    waits=tuple([(P, it + 1)] + ([(st + "s", 16 * (it - 1))] if it >= 2 else [])),
                    fire=ctx.fire(E, 1))
                ctx.emit("gpsimd", (lambda it=it, m=m, f4=f4: lambda e: e.dma_start(
                    out=dst[m * 128:(m + 1) * 128, f4 * 512:(f4 + 1) * 512],
                    in_=fa[:, it % 2, :]))(),
                    waits=((E, it + 1),), fire=ctx.fire(st + "s", 16))
        return (st + "s", 16 * 128)

    dD = hstage(tg + "D", gf, a1, h1t, sh1c, (CS, 16 * 64))
    dE = hstage(tg + "E", a1, a2, h2t, sh2c, dD)

    FL, FR, FP, FE, FS = tg + "fl", tg + "fr", tg + "fp", tg + "fe", tg + "fs"
    w3r = w3t.rearrange("(k p) m -> p k m", p=128)
    for f4 in range(4):
        pf = f4 % 2
        for k in range(32):
            w = [dE, (BS, 16 * 16)] if f4 == 0 and k == 0 else []
            if f4 >= 2:
                w.append((FP, 8 * (f4 - 1)))
            ctx.emit("sync", (lambda k=k, f4=f4, pf=pf: lambda e: e.dma_start(
                out=rq[:, pf, k, :],
                in_=a2[k * 128:(k + 1) * 128, f4 * 512:(f4 + 1) * 512]))(),
                waits=tuple(w), fire=ctx.fire(FR, 16))
        for m in range(KD):
            it = f4 * KD + m
            pw = it % 2
            w = [dE] if it == 0 else []
            if it >= 2:
                w.append((FP, it - 1))
            ctx.emit("scalar", (lambda it=it, m=m, pw=pw: lambda e: e.dma_start(
                out=w32[:, pw, :, :], in_=w3r[:, :, m * 128:(m + 1) * 128]))(),
                waits=tuple(w), fire=ctx.fire(FL, 16))
            ctx.emit("sync", (lambda it=it, m=m, f4=f4, pw=pw: lambda e: e.dma_start(
                out=fx[:, pw, :],
                in_=x1[m * 128:(m + 1) * 128, f4 * 512:(f4 + 1) * 512]))(),
                waits=((FE, 3 * (it - 1)),) if it >= 2 else (),
                fire=ctx.fire(FL, 16))
            ctx.emit("sync", (lambda it=it, m=m, f4=f4, pw=pw: lambda e: e.dma_start(
                out=fa[:, pw, :],
                in_=attn1[m * 128:(m + 1) * 128, f4 * 512:(f4 + 1) * 512]))(),
                fire=ctx.fire(FL, 16))
            for k in range(32):
                wts = []
                if k == 0:
                    wts = [(FL, 48 * (it + 1)), (FR, 512 * (f4 + 1))]
                    if it >= 2:
                        wts.append((FE, 3 * (it - 1)))
                ctx.emit("tensor", (lambda k=k, pw=pw, pf=pf, it=it: lambda e: e.matmul(
                    out=ps[:, it % 4, :], lhsT=w32[:, pw, k, :],
                    rhs=rq[:, pf, k, :], start=(k == 0), stop=(k == 31)))(),
                    waits=tuple(wts),
                    fire=ctx.fire(FP, 1) if k == 31 else None)
            ctx.emit("scalar", (lambda it=it, m=m, pw=pw: lambda e: e.activation(
                out=fo[:, pw, :], in_=ps[:, it % 4, :], func=AF.Identity,
                scale=s3c[:, m:m + 1]))(),
                waits=((FP, it + 1),), fire=ctx.fire(FE, 1))
            ctx.emit("vector", (lambda pw=pw: lambda e: e.tensor_add(
                out=fx[:, pw, :], in0=fx[:, pw, :], in1=fa[:, pw, :]))(),
                waits=((FL, 48 * (it + 1)), (FE, 3 * it + 1)),
                fire=ctx.fire(FE, 1))
            ctx.emit("vector", (lambda pw=pw: lambda e: e.tensor_add(
                out=fo[:, pw, :], in0=fo[:, pw, :], in1=fx[:, pw, :]))(),
                waits=tuple([(FE, 3 * it + 2)] + ([(FS, 16 * (it - 1))] if it >= 2 else [])),
                fire=ctx.fire(FE, 1))
            ctx.emit("gpsimd", (lambda it=it, m=m, f4=f4, pw=pw: lambda e: e.dma_start(
                out=out[m * 128:(m + 1) * 128, f4 * 512:(f4 + 1) * 512],
                in_=fo[:, pw, :]))(),
                waits=((FE, 3 * it + 3),), fire=ctx.fire(FS, 16))
    ctx.emit("vector", lambda e: e.engine_nop(),
             waits=((FS, 16 * 32), (BS, 16 * 16)), fire=ctx.fire("q", 1))


# version 06
# speedup vs baseline: 1.8529x; 1.1013x over previous
import math
import numpy as np
import ml_dtypes
from contextlib import ExitStack

import concourse.bass as bass
import concourse.mybir as mybir
from concourse.bass_utils import run_bass_kernel_spmd

F32 = mybir.dt.float32
BF16 = mybir.dt.bfloat16
F16 = mybir.dt.float16
ALU = mybir.AluOpType
AF = mybir.ActivationFunctionType

BF = ml_dtypes.bfloat16

T = 2048
D = 1024
DI = 2048
RNE_C = 12582912.0

def _pwl_constants():
    xs = [0.25 * k for k in range(17)]
    ev = [0.5 * x * (1.0 + math.erf(x / math.sqrt(2.0))) - x / 2.0 for x in xs]
    s = [(ev[k + 1] - ev[k]) / 0.25 for k in range(16)]
    knots = []
    for k in range(1, 16):
        knots.append((64.0 * k, s[k] - s[k - 1]))
    knots.append((1024.0, 0.5 - s[15]))
    j256 = 256.0 * (4.0 - (0.5 * 4.0 * (1.0 + math.erf(4.0 / math.sqrt(2.0)))))
    return s[0], knots, j256


PWL_S0, PWL_KNOTS, PWL_J256 = _pwl_constants()


class Ctx:
    ENGINES = ("sync", "vector", "scalar", "tensor", "gpsimd")

    def __init__(self, nc, stack):
        self.nc = nc
        self.stack = stack
        self.ops = {e: [] for e in self.ENGINES}
        self.sems = {}
        self.cnt = {}
        self.bump = 16512
        self.phase_tag = "g"
        self.psA = None
        self.psB = None

    def sem(self, name):
        if name not in self.sems:
            self.sems[name] = self.stack.enter_context(self.nc.semaphore(name))
            self.cnt[name] = 0
        return self.sems[name]

    def fire(self, name, amount):
        h = self.sem(name)
        self.cnt[name] += amount
        return (h, amount)

    def val(self, name):
        return self.cnt.get(name, 0)

    def emit(self, engine, fn, waits=(), fire=None):
        wl = [(self.sem(n), v) for (n, v) in waits if v > 0]

        def runner(eng):
            for h, v in wl:
                eng.wait_ge(h, v)
            ins = fn(eng)
            if fire is not None:
                ins.then_inc(fire[0], fire[1])
        self.ops[engine].append(runner)

    def dma(self, queue, out, in_, waits=(), sem="dma", transpose=False):
        f = self.fire(sem, 16)
        if transpose:
            self.emit(queue, lambda e: e.dma_start_transpose(out=out, in_=in_),
                      waits, f)
        else:
            self.emit(queue, lambda e: e.dma_start(out=out, in_=in_), waits, f)
        return self.cnt[sem]

    SBUF_BASE = 16512
    SBUF_LIMIT = 229344

    def phase(self, tag):
        self.phase_tag = tag
        self.bump = self.SBUF_BASE

    def sbuf(self, name, shape, dt):
        per = 1
        for s in shape[1:]:
            per *= s
        per *= {F32: 4, BF16: 2, F16: 2}[dt]
        off = (self.bump + 31) & ~31
        self.bump = off + per
        assert self.bump <= self.SBUF_LIMIT, (
            f"sbuf overflow in phase {self.phase_tag}: {self.bump}")
        h = self.nc.alloc_sbuf_tensor_at(
            f"{self.phase_tag}_{name}", list(shape), dt, offset=off)
        return h

    def psum(self, name, shape, dt=F32):
        return self.stack.enter_context(self.nc.psum_tensor(name, shape, dt))

    def finalize(self, final_waits):
        ops = self.ops
        wl = [(self.sems[n], v) for (n, v) in final_waits]
        with self.nc.Block() as block:
            @block.sync
            def _(eng):
                for f in ops["sync"]:
                    f(eng)
                for h, v in wl:
                    eng.wait_ge(h, v)

            @block.vector
            def _(eng):
                for f in ops["vector"]:
                    f(eng)

            @block.scalar
            def _(eng):
                for f in ops["scalar"]:
                    f(eng)

            @block.tensor
            def _(eng):
                for f in ops["tensor"]:
                    f(eng)

            @block.gpsimd
            def _(eng):
                for f in ops["gpsimd"]:
                    f(eng)


def new_prog():
    nc = bass.Bass("TRN2", target_bir_lowering=False)
    stack = ExitStack()
    ctx = Ctx(nc, stack)
    return nc, stack, ctx


def din(nc, name, shape, dt=F32):
    return nc.dram_tensor(name, list(shape), dt, kind="ExternalInput").ap()


def dout(nc, name, shape, dt=F32):
    return nc.dram_tensor(name, list(shape), dt, kind="ExternalOutput").ap()



def _seq(ctx, engine, fn):
    v = ctx.val("q")
    ctx.emit(engine, fn, waits=(("q", v),), fire=ctx.fire("q", 1))


def _seqdma(ctx, queue, out, in_, transpose=False, slow=False):
    v = ctx.val("q")
    f = ctx.fire("q", 16)
    if transpose:
        ctx.emit(queue, lambda e: e.dma_start_transpose(out=out, in_=in_),
                 waits=(("q", v),), fire=f)
    elif slow:
        ctx.emit(queue, lambda e: e.dma_start(out=out, in_=in_,
                                              allow_slow_non_contiguous=True),
                 waits=(("q", v),), fire=f)
    else:
        ctx.emit(queue, lambda e: e.dma_start(out=out, in_=in_),
                 waits=(("q", v),), fire=f)


class Seq:

    def __init__(self, ctx):
        self.ctx = ctx

    def op(self, engine, fn):
        _seq(self.ctx, engine, fn)

    def dma(self, out, in_, queue="sync", transpose=False, slow=False):
        _seqdma(self.ctx, queue, out, in_, transpose, slow)

    def done(self):
        self.ctx.finalize([("q", self.ctx.val("q"))])


class ConstCols:

    def __init__(self, S, values):
        self.tile = S.ctx.sbuf("cstc", [128, max(1, len(values))], F32)
        self.idx = {}
        for i, v in enumerate(values):
            self.idx[v] = i
            S.op("vector", lambda e, i=i, v=v: e.memset(self.tile[:, i:i + 1], v))

    def ap(self, v):
        i = self.idx[v]
        return self.tile[:, i:i + 1]


CASTS = [
    ("win", "win_bf", D, 4096),
    ("wout", "wout_bf", DI, D),
    ("w1", "w1_bf", D, 4096),
    ("w2", "w2_bf", 4096, D),
    ("wxp", "wxp_bf", DI, 192),
    ("f1t", "f1t_bf", D, 4096),
    ("f2t", "f2t_bf", D, 4096),
    ("h1t", "h1t_bf", 4096, 4096),
    ("h2t", "h2t_bf", 4096, 4096),
    ("w3t", "w3t_bf", 4096, D),
]


def prep_io(nc, mk):
    io = {}
    for (iname, oname, rows, cols) in CASTS:
        io[iname] = mk(iname, (rows, cols), F32, 0)
        io[oname] = mk(oname, (rows, cols), BF16, 1)
    return io


def emit_prep(nc, ctx, S, io):
    tin = ctx.sbuf("tin", [128, 4096], F32)
    tout = ctx.sbuf("tout", [128, 4096], BF16)
    for (iname, oname, rows, cols) in CASTS:
        src = io[iname]
        dst = io[oname]
        for i in range(rows // 128):
            S.dma(tin[:, :cols], src[i * 128:(i + 1) * 128, :])
            S.op("vector", lambda e, c=cols: e.tensor_copy(
                out=tout[:, :c], in_=tin[:, :c]))
            S.dma(dst[i * 128:(i + 1) * 128, :], tout[:, :cols],
                  queue="gpsimd")


def build_prep():
    nc, stack, ctx = new_prog()
    with stack:
        S = Seq(ctx)
        io = prep_io(nc, lambda n, s, d, o: (dout if o else din)(nc, n, s, d))
        ctx.phase("prep")
        emit_prep(nc, ctx, S, io)
        S.done()
    return nc



LXA_IO = ["x", "win_bf", "convw", "convb", "wxp_bf", "wdt", "bdt",
          "xn_bf", "xi_bf", "z_bf", "u_f", "u_bf", "B0", "C0", "delta", "du"]


def lxa_io(nc, layer, mk):
    NST = 64 if layer == 0 else 16
    MX = 192 if layer == 0 else 96
    fl = F32 if layer == 0 else BF16
    io = {}
    io["x"] = mk("x", (D, T), F32, 0)
    io["win_bf"] = mk("win_bf", (D, 4096), BF16, 0)
    io["convw"] = mk("convw", (DI, 4), fl, 0)
    io["convb"] = mk("convb", (DI,), fl, 0)
    io["wxp_bf"] = mk("wxp_bf", (DI, MX), BF16, 0)
    io["wdt"] = mk("wdt", (64, DI), fl, 0)
    io["bdt"] = mk("bdt", (DI,), fl, 0)
    if layer == 0:
        io["lnw"] = mk("lnw", (D,), F32, 0)
        io["lnb"] = mk("lnb", (D,), F32, 0)
        io["xn_f"] = mk("xn_f", (D, T), F32, 1)
    io["xn_bf"] = mk("xn_bf", (D, T), BF16, 1)
    io["xi_bf"] = mk("xi_bf", (DI, T), BF16, 1)
    io["z_bf"] = mk("z_bf", (DI, T), BF16, 1)
    io["u_f"] = mk("u_f", (DI, T), F32, 1)
    io["u_bf"] = mk("u_bf", (DI, T), BF16, 1)
    io["B0"] = mk("B0", (NST, T), F32, 1)
    io["C0"] = mk("C0", (NST, T), F32, 1)
    io["delta"] = mk("delta", (DI, T), F32, 1)
    io["du"] = mk("du", (DI, T), F32, 1)
    io["du_f16"] = mk("du_f16", (DI, T), F16, 1)
    return io


def emit_lxa(nc, ctx, S, io, layer):
    KD = D // 128
    H = T // 2
    NST = 64 if layer == 0 else 16
    MX = 192 if layer == 0 else 96

    x = io["x"]
    win = io["win_bf"]
    convw = io["convw"]
    convb = io["convb"]
    wxp = io["wxp_bf"]
    wdt = io["wdt"]
    bdt = io["bdt"]
    if layer == 0:
        lnw = io["lnw"]
        lnb = io["lnb"]
        xn_f = io["xn_f"]
    xn_bf = io["xn_bf"]
    xi_bf = io["xi_bf"]
    z_bf = io["z_bf"]
    u_f = io["u_f"]
    u_bf = io["u_bf"]
    B0 = io["B0"]
    C0 = io["C0"]
    delta = io["delta"]
    du = io["du"]
    du_f16 = io["du_f16"]

    if True:
        xt = ctx.sbuf("xt", [128, H], F32)
        sq = ctx.sbuf("sq", [128, H], F32)
        ones_col = ctx.sbuf("ones_col", [128, 1], F32)
        ones_row = ctx.sbuf("ones_row", [1, 128], F32)
        rows = ctx.sbuf("rows", [1, 4, T], F32)
        rep_a = ctx.sbuf("rep_a", [128, T], F32)
        rep_b = ctx.sbuf("rep_b", [128, T], F32)
        xbf = ctx.sbuf("xbf", [128, H], BF16)
        xn2 = ctx.sbuf("xn2", [128, KD, T], BF16)
        wm = ctx.sbuf("wm", [128, KD, 128], BF16)
        ebf = ctx.sbuf("ebf", [128, H], BF16)
        ct = ctx.sbuf("ct", [128, T + 3], BF16)
        cwr = ctx.sbuf("cwr", [128, 16, 4], F32 if layer == 0 else BF16)
        cbr = ctx.sbuf("cbr", [128, 16], F32 if layer == 0 else BF16)
        cw = ctx.sbuf("cw", [128, 16, 4], F32)
        cb = ctx.sbuf("cb", [128, 16], F32)
        ua = ctx.sbuf("ua", [128, T], F32)
        ub = ctx.sbuf("ub", [128, T], BF16)
        wxpt = ctx.sbuf("wxpt", [128, 16, MX], BF16)
        uh = ctx.sbuf("uh", [128, H], BF16)
        dtr = ctx.sbuf("dtr", [64, T], BF16)
        wdtf = ctx.sbuf("wdtf", [64, DI], F32 if layer == 0 else BF16)
        wdtb = ctx.sbuf("wdtb", [64, DI], BF16)
        bdtr = ctx.sbuf("bdtr", [128, 16], F32 if layer == 0 else BF16)
        bdtc = ctx.sbuf("bdtc", [128, 16], F32)
        tb = ctx.sbuf("tb", [128, 4, H], F32)
        duh = ctx.sbuf("duh", [128, H], F16)
        lncol = ctx.sbuf("lncol", [128, 2, KD], F32)
        ps_big = ctx.psA
        ps_sm = ctx.psB[0:64]

        MU, SQS, RSTD, SCR = range(4)
        psb2 = ps_big[:, 0:2, :].rearrange("p a f -> p (a f)")

        S.op("vector", lambda e: e.memset(ones_col[:], 1.0))
        S.op("vector", lambda e: e.memset(ones_row[:], 1.0))
        cst = ConstCols(S, [1e-5])

        def stats_pass(src_ap, xn_mode):
            for th in range(2):
                for k in range(KD):
                    S.dma(xt[:], src_ap[k * 128:(k + 1) * 128, th * H:(th + 1) * H])
                    if xn_mode:
                        S.op("vector", lambda e, th=th: e.tensor_sub(
                            out=xt[:], in0=xt[:], in1=rep_a[:, th * H:(th + 1) * H]))
                        S.op("vector", lambda e, th=th: e.tensor_mul(
                            out=xt[:], in0=xt[:], in1=rep_b[:, th * H:(th + 1) * H]))
                        if layer == 0:
                            S.dma(xn_f[k * 128:(k + 1) * 128, th * H:(th + 1) * H],
                                  xt[:], queue="gpsimd")
                        S.op("vector", lambda e: e.tensor_copy(out=xbf[:], in_=xt[:]))
                        S.dma(xn_bf[k * 128:(k + 1) * 128, th * H:(th + 1) * H],
                              xbf[:], queue="gpsimd")
                    if (not xn_mode) or layer == 0:
                        S.op("scalar", lambda e: e.activation(
                            out=sq[:], in_=xt[:], func=AF.Square))
                        for which in range(2):
                            st = xt if which == 0 else sq
                            for fc in range(2):
                                S.op("tensor", lambda e, w=which, fc=fc, k=k, st=st:
                                     e.matmul(out=ps_sm[0:1, w * 2 + fc, :],
                                              lhsT=ones_col[:],
                                              rhs=st[:, fc * 512:(fc + 1) * 512],
                                              start=(k == 0), stop=(k == KD - 1)))
                if (not xn_mode) or layer == 0:
                    for which, slot in ((0, MU), (1, SQS)):
                        for fc in range(2):
                            S.op("scalar", lambda e, w=which, s=slot, fc=fc, th=th:
                                 e.activation(
                                     out=rows[:, s, th * H + fc * 512:
                                              th * H + (fc + 1) * 512],
                                     in_=ps_sm[0:1, w * 2 + fc, :],
                                     func=AF.Identity, scale=1.0 / D))

        def row_math():
            S.op("vector", lambda e: e.tensor_mul(
                out=rows[:, SCR, :], in0=rows[:, MU, :], in1=rows[:, MU, :]))
            S.op("vector", lambda e: e.tensor_sub(
                out=rows[:, SCR, :], in0=rows[:, SQS, :], in1=rows[:, SCR, :]))
            S.op("scalar", lambda e: e.activation(
                out=rows[:, SCR, :], in_=rows[:, SCR, :], func=AF.Sqrt,
                bias=cst.ap(1e-5)[0:1, :]))
            S.op("vector", lambda e: e.reciprocal(
                out=rows[:, RSTD, :], in_=rows[:, SCR, :]))

        def rep_pass():
            for slot, dst in ((MU, rep_a), (RSTD, rep_b)):
                for fc in range(4):
                    S.op("tensor", lambda e, s=slot, fc=fc: e.matmul(
                        out=ps_big[:, fc, :], lhsT=ones_row[:],
                        rhs=rows[:, s, fc * 512:(fc + 1) * 512],
                        start=True, stop=True))
                S.op("vector", lambda e, d=dst: e.tensor_copy(
                    out=d[:], in_=ps_big[:].rearrange("p a f -> p (a f)")))

        stats_pass(x, False)
        row_math()
        rep_pass()
        stats_pass(x, True)
        if layer == 0:
            S.dma(lncol[:, 0, :], lnw.rearrange("(k p) -> p k", p=128), slow=True)
            S.dma(lncol[:, 1, :], lnb.rearrange("(k p) -> p k", p=128), slow=True)
            row_math()
            rep_pass()
            for th in range(2):
                for k in range(KD):
                    S.dma(xt[:], xn_f[k * 128:(k + 1) * 128, th * H:(th + 1) * H])
                    S.op("vector", lambda e, th=th: e.tensor_sub(
                        out=xt[:], in0=xt[:], in1=rep_a[:, th * H:(th + 1) * H]))
                    S.op("vector", lambda e, th=th: e.tensor_mul(
                        out=xt[:], in0=xt[:], in1=rep_b[:, th * H:(th + 1) * H]))
                    S.op("scalar", lambda e, k=k, th=th: e.activation(
                        out=xn2[:, k, th * H:(th + 1) * H], in_=xt[:],
                        func=AF.Identity, scale=lncol[:, 0, k:k + 1],
                        bias=lncol[:, 1, k:k + 1]))
        else:
            for k in range(KD):
                S.dma(xn2[:, k, :], xn_bf[k * 128:(k + 1) * 128, :])

        for m in range(32):
            S.dma(wm[:], win.rearrange("(k p) m -> p k m", p=128)[:, :, m * 128:(m + 1) * 128])
            for th in range(2):
                for k in range(KD):
                    for j in range(2):
                        S.op("tensor", lambda e, k=k, th=th, j=j: e.matmul(
                            out=ps_big[:, j, :], lhsT=wm[:, k, :],
                            rhs=xn2[:, k, th * H + j * 512:th * H + (j + 1) * 512],
                            start=(k == 0), stop=(k == KD - 1)))
                S.op("vector", lambda e: e.tensor_copy(out=ebf[:], in_=psb2))
                dst = xi_bf if m < 16 else z_bf
                mm2 = m if m < 16 else m - 16
                S.dma(dst[mm2 * 128:(mm2 + 1) * 128, th * H:(th + 1) * H],
                      ebf[:], queue="gpsimd")

        S.dma(cwr[:], convw.rearrange("(g p) k -> p g k", p=128), slow=True)
        S.dma(cbr[:], convb.rearrange("(g p) -> p g", p=128), slow=True)
        S.op("vector", lambda e: e.tensor_copy(out=cw[:], in_=cwr[:]))
        S.op("vector", lambda e: e.tensor_copy(out=cb[:], in_=cbr[:]))
        S.op("vector", lambda e: e.memset(ct[:, 0:3], 0.0))
        for g in range(16):
            S.dma(ct[:, 3:T + 3], xi_bf[g * 128:(g + 1) * 128, :])
            S.op("vector", lambda e, g=g: e.tensor_scalar(
                out=ua[:], in0=ct[:, 0:T], scalar1=cw[:, g, 0:1],
                scalar2=None, op0=ALU.mult))
            for kk in range(1, 4):
                S.op("vector", lambda e, g=g, kk=kk: e.scalar_tensor_tensor(
                    out=ua[:], in0=ct[:, kk:T + kk], scalar=cw[:, g, kk:kk + 1],
                    in1=ua[:], op0=ALU.mult, op1=ALU.add))
            S.op("scalar", lambda e, g=g: e.activation(
                out=ua[:], in_=ua[:], func=AF.Silu, bias=cb[:, g:g + 1]))
            S.op("vector", lambda e: e.tensor_copy(out=ub[:], in_=ua[:]))
            if layer == 1:
                S.op("vector", lambda e: e.tensor_copy(out=ua[:], in_=ub[:]))
            S.dma(u_f[g * 128:(g + 1) * 128, :], ua[:], queue="gpsimd")
            S.dma(u_bf[g * 128:(g + 1) * 128, :], ub[:], queue="gpsimd")

        S.dma(wxpt[:], wxp.rearrange("(k p) m -> p k m", p=128))
        psc2 = ps_sm[:, 0:2, :].rearrange("p a f -> p (a f)")
        psc3 = ps_sm[:, 2:4, :].rearrange("p a f -> p (a f)")
        for th in range(2):
            for k in range(16):
                S.dma(uh[:], u_bf[k * 128:(k + 1) * 128, th * H:(th + 1) * H])
                for j in range(2):
                    uhj = uh[:, j * 512:(j + 1) * 512]
                    if MX == 192:
                        S.op("tensor", lambda e, k=k, j=j, u2=uhj: e.matmul(
                            out=ps_big[:, j, :], lhsT=wxpt[:, k, 0:128],
                            rhs=u2, start=(k == 0), stop=(k == 15)))
                        S.op("tensor", lambda e, k=k, j=j, u2=uhj: e.matmul(
                            out=ps_sm[:, j, :], lhsT=wxpt[:, k, 128:192], rhs=u2,
                            start=(k == 0), stop=(k == 15)))
                    else:
                        S.op("tensor", lambda e, k=k, j=j, u2=uhj: e.matmul(
                            out=ps_big[0:64, j, :], lhsT=wxpt[:, k, 0:64],
                            rhs=u2, start=(k == 0), stop=(k == 15)))
                        S.op("tensor", lambda e, k=k, j=j, u2=uhj: e.matmul(
                            out=ps_sm[0:16, j, :], lhsT=wxpt[:, k, 64:80], rhs=u2,
                            start=(k == 0), stop=(k == 15)))
                        S.op("tensor", lambda e, k=k, j=j, u2=uhj: e.matmul(
                            out=ps_sm[0:16, 2 + j, :], lhsT=wxpt[:, k, 80:96], rhs=u2,
                            start=(k == 0), stop=(k == 15)))
            S.op("vector", lambda e, th=th: e.tensor_copy(
                out=dtr[:, th * H:(th + 1) * H],
                in_=ps_big[0:64, 0:2, :].rearrange("p a f -> p (a f)")))
            if MX == 192:
                S.op("vector", lambda e: e.tensor_copy(
                    out=tb[64:128, 0, :], in_=psb2[64:128, :]))
                S.op("vector", lambda e: e.tensor_copy(
                    out=tb[0:64, 1, :], in_=psc2[0:64, :]))
                S.dma(B0[:, th * H:(th + 1) * H], tb[64:128, 0, :], queue="gpsimd")
                S.dma(C0[:, th * H:(th + 1) * H], tb[0:64, 1, :], queue="gpsimd")
            else:
                S.op("vector", lambda e: e.tensor_copy(
                    out=tb[0:16, 0, :], in_=psc2[0:16, :]))
                S.op("vector", lambda e: e.tensor_copy(
                    out=tb[0:16, 1, :], in_=psc3[0:16, :]))
                S.dma(B0[:, th * H:(th + 1) * H], tb[0:16, 0, :], queue="gpsimd")
                S.dma(C0[:, th * H:(th + 1) * H], tb[0:16, 1, :], queue="gpsimd")

        S.dma(wdtf[:], wdt)
        S.op("vector", lambda e: e.tensor_copy(out=wdtb[:], in_=wdtf[:]))
        S.dma(bdtr[:], bdt.rearrange("(g p) -> p g", p=128), slow=True)
        S.op("vector", lambda e: e.tensor_copy(out=bdtc[:], in_=bdtr[:]))
        for m in range(16):
            for th in range(2):
                for j in range(2):
                    S.op("tensor", lambda e, m=m, th=th, j=j: e.matmul(
                        out=ps_big[:, j, :], lhsT=wdtb[:, m * 128:(m + 1) * 128],
                        rhs=dtr[:, th * H + j * 512:th * H + (j + 1) * 512],
                        start=True, stop=True))
                S.op("vector", lambda e, m=m: e.tensor_scalar(
                    out=tb[:, 0, :], in0=psb2, scalar1=bdtc[:, m:m + 1],
                    scalar2=0.0, op0=ALU.add, op1=ALU.max))
                S.op("scalar", lambda e, m=m: e.activation(
                    out=tb[:, 1, :], in_=psb2, func=AF.Abs, bias=bdtc[:, m:m + 1]))
                S.op("scalar", lambda e: e.activation(
                    out=tb[:, 2, :], in_=tb[:, 1, :], func=AF.Exp, scale=-1.0))
                S.op("scalar", lambda e: e.activation(
                    out=tb[:, 1, :], in_=tb[:, 2, :], func=AF.Ln, bias=1.0))
                S.op("vector", lambda e: e.tensor_add(
                    out=tb[:, 2, :], in0=tb[:, 0, :], in1=tb[:, 1, :]))
                S.dma(tb[:, 3, :], u_f[m * 128:(m + 1) * 128, th * H:(th + 1) * H])
                S.op("vector", lambda e: e.tensor_mul(
                    out=tb[:, 0, :], in0=tb[:, 2, :], in1=tb[:, 3, :]))
                S.op("vector", lambda e: e.tensor_copy(out=duh[:], in_=tb[:, 0, :]))
                S.dma(delta[m * 128:(m + 1) * 128, th * H:(th + 1) * H],
                      tb[:, 2, :], queue="gpsimd")
                S.dma(du[m * 128:(m + 1) * 128, th * H:(th + 1) * H],
                      tb[:, 0, :], queue="gpsimd")
                S.dma(du_f16[m * 128:(m + 1) * 128, th * H:(th + 1) * H],
                      duh[:], queue="gpsimd")


def build_lxa(layer):
    nc, stack, ctx = new_prog()
    S = Seq(ctx)
    with stack:
        io = lxa_io(nc, layer, lambda n, s, d, o: (dout if o else din)(nc, n, s, d))
        ctx.psA = ctx.psum("psA", [128, 4, 512], F32)
        ctx.psB = ctx.psum("psB", [128, 4, 512], F32)
        ctx.phase("lxa")
        emit_lxa(nc, ctx, S, io, layer)
        S.done()
    return nc



def scan_io(nc, layer, mk):
    NST = 64 if layer == 0 else 16
    a_dt = F32 if layer == 0 else BF16
    io = {}
    io["delta"] = mk("delta", (DI, T), F32, 0)
    io["du"] = mk("du", (DI, T), F32, 0)
    io["du_f16"] = mk("du_f16", (DI, T), F16, 0)
    io["Bm"] = mk("Bm", (NST, T), F32, 0)
    io["Cm"] = mk("Cm", (NST, T), F32, 0)
    io["alog"] = mk("alog", (DI, NST), a_dt, 0)
    io["y"] = mk("y", (DI, T), F32, 1)
    return io


def emit_scan(nc, ctx, S, io, layer, dg0=0, dg1=16):
    NST = 64 if layer == 0 else 16
    a_dt = F32 if layer == 0 else BF16

    delta = io["delta"]
    du = io["du"]
    Bm = io["Bm"]
    Cm = io["Cm"]
    alog = io["alog"]
    y = io["y"]

    if True:
        dl = ctx.sbuf("dl", [128, T], F32)
        dul = ctx.sbuf("dul", [128, T], F32)
        yac = ctx.sbuf("yac", [128, T], F32)
        dat = ctx.sbuf("dat", [128, T], F32)
        dbut = ctx.sbuf("dbut", [128, T], F32)
        ht = ctx.sbuf("ht", [128, T], F32)
        hct = ctx.sbuf("hct", [128, T], F32)
        brow = ctx.sbuf("brow", [1, T], F32)
        crow = ctx.sbuf("crow", [1, T], F32)
        alr = ctx.sbuf("alr", [128, NST], a_dt)
        asb = ctx.sbuf("asb", [128, NST], F32)
        ones_row = ctx.sbuf("ones_row", [1, 128], F32)
        ps_b = ctx.psA
        ps_c = ctx.psB
        psbv = ps_b[:].rearrange("p a f -> p (a f)")
        pscv = ps_c[:].rearrange("p a f -> p (a f)")

        S.op("vector", lambda e: e.memset(ones_row[:], 1.0))
        for dg in range(dg0, dg1):
            S.dma(dl[:], delta[dg * 128:(dg + 1) * 128, :])
            S.dma(dul[:], du[dg * 128:(dg + 1) * 128, :])
            S.dma(alr[:], alog[dg * 128:(dg + 1) * 128, :])
            S.op("scalar", lambda e: e.activation(out=asb[:], in_=alr[:], func=AF.Exp))
            S.op("vector", lambda e: e.tensor_single_scalar(
                out=asb[:], in_=asb[:], scalar=-1.0, op=ALU.mult))
            S.op("vector", lambda e: e.memset(yac[:], 0.0))
            for n in range(NST):
                S.dma(brow[:], Bm[n:n + 1, :])
                S.dma(crow[:], Cm[n:n + 1, :])
                for fc in range(4):
                    S.op("tensor", lambda e, fc=fc: e.matmul(
                        out=ps_b[:, fc, :], lhsT=ones_row[:],
                        rhs=brow[:, fc * 512:(fc + 1) * 512], start=True, stop=True))
                for fc in range(4):
                    S.op("tensor", lambda e, fc=fc: e.matmul(
                        out=ps_c[:, fc, :], lhsT=ones_row[:],
                        rhs=crow[:, fc * 512:(fc + 1) * 512], start=True, stop=True))
                S.op("scalar", lambda e, n=n: e.activation(
                    out=dat[:], in_=dl[:], func=AF.Exp, scale=asb[:, n:n + 1]))
                S.op("vector", lambda e: e.tensor_mul(out=dbut[:], in0=dul[:], in1=psbv))
                S.op("vector", lambda e: e.tensor_tensor_scan(
                    out=ht[:], data0=dat[:], data1=dbut[:], initial=0.0,
                    op0=ALU.mult, op1=ALU.add))
                S.op("vector", lambda e: e.tensor_mul(out=hct[:], in0=ht[:], in1=pscv))
                S.op("vector", lambda e: e.tensor_add(out=yac[:], in0=yac[:], in1=hct[:]))
            S.dma(y[dg * 128:(dg + 1) * 128, :], yac[:], queue="gpsimd")


def build_scan(layer, dg0=0, dg1=16):
    nc, stack, ctx = new_prog()
    S = Seq(ctx)
    with stack:
        io = scan_io(nc, layer, lambda n, s, d, o: (dout if o else din)(nc, n, s, d))
        ctx.psA = ctx.psum("psA", [128, 4, 512], F32)
        ctx.psB = ctx.psum("psB", [128, 4, 512], F32)
        ctx.phase("scan")
        emit_scan2(nc, ctx, S, io, layer)
        S.done()
    return nc



def l0out_io(nc, mk):
    io = {}
    io["y"] = mk("y", (DI, T), F32, 0)
    io["u_f"] = mk("u_f", (DI, T), F32, 0)
    io["z_bf"] = mk("z_bf", (DI, T), BF16, 0)
    io["d0"] = mk("d0", (DI,), F32, 0)
    io["wout_bf"] = mk("wout_bf", (DI, D), BF16, 0)
    io["xn_f"] = mk("xn_f", (D, T), F32, 0)
    io["xn_bf"] = mk("xn_bf", (D, T), BF16, 0)
    io["w1_bf"] = mk("w1_bf", (D, 4096), BF16, 0)
    io["b1"] = mk("b1", (4096,), F32, 0)
    io["w2_bf"] = mk("w2_bf", (4096, D), BF16, 0)
    io["b2"] = mk("b2", (D,), F32, 0)
    io["x"] = mk("x", (D, T), F32, 0)
    io["attn0"] = mk("attn0", (D, T), F32, 1)
    io["h_bf"] = mk("h_bf", (4096, T), BF16, 1)
    io["x1"] = mk("x1", (D, T), F32, 1)
    return io


def emit_l0out(nc, ctx, S, io):
    H = T // 2

    y = io["y"]
    u_f = io["u_f"]
    z_bf = io["z_bf"]
    d0 = io["d0"]
    wout = io["wout_bf"]
    xn_f = io["xn_f"]
    xn_bf = io["xn_bf"]
    w1 = io["w1_bf"]
    b1 = io["b1"]
    w2 = io["w2_bf"]
    b2 = io["b2"]
    x = io["x"]
    attn0 = io["attn0"]
    h_bf = io["h_bf"]
    x1 = io["x1"]

    KD = D // 128
    if True:
        gbuf = ctx.sbuf("gbuf", [128, 16, T], BF16)
        xnb = ctx.sbuf("xnb", [128, KD, T], BF16)
        t0 = ctx.sbuf("t0", [128, T], F32)
        t1 = ctx.sbuf("t1", [128, T], F32)
        dcol = ctx.sbuf("dcol", [128, 16], F32)
        zt = ctx.sbuf("zt", [128, T], BF16)
        wom = ctx.sbuf("wom", [128, 16, 128], BF16)
        w1m = ctx.sbuf("w1m", [128, KD, 128], BF16)
        b1c = ctx.sbuf("b1c", [128, 32], F32)
        b2c = ctx.sbuf("b2c", [128, 8], F32)
        tb = ctx.sbuf("tb", [128, 4, H], F32)
        hb = ctx.sbuf("hb", [128, H], BF16)
        hq = ctx.sbuf("hq", [128, 32, 512], BF16)
        w2m = ctx.sbuf("w2m", [128, 32, 128], BF16)
        knot = ctx.sbuf("knot", [128, 18], F32)
        ps_big = ctx.psA
        psb2 = ps_big[:, 0:2, :].rearrange("p a f -> p (a f)")
        psq = ps_big[:, 0:1, :].rearrange("p a f -> p (a f)")

        sgn = []
        for i, (vk, ck) in enumerate(PWL_KNOTS):
            S.op("vector", lambda e, i=i, vk=vk, ck=ck: e.memset(
                knot[:, i:i + 1], -abs(ck) * vk))
            sgn.append(1.0 if ck >= 0 else -1.0)
        S.op("vector", lambda e: e.memset(knot[:, 16:17], -1024.0))
        S.dma(dcol[:], d0.rearrange("(g p) -> p g", p=128), slow=True)
        S.dma(b1c[:], b1.rearrange("(g p) -> p g", p=128), slow=True)
        S.dma(b2c[:], b2.rearrange("(g p) -> p g", p=128), slow=True)
        S.op("vector", lambda e: e.tensor_single_scalar(
            out=b2c[:], in_=b2c[:], scalar=256.0, op=ALU.mult))

        for g in range(16):
            S.dma(t0[:], y[g * 128:(g + 1) * 128, :])
            S.dma(t1[:], u_f[g * 128:(g + 1) * 128, :])
            S.dma(zt[:], z_bf[g * 128:(g + 1) * 128, :])
            S.op("vector", lambda e, g=g: e.scalar_tensor_tensor(
                out=t0[:], in0=t1[:], scalar=dcol[:, g:g + 1], in1=t0[:],
                op0=ALU.mult, op1=ALU.add))
            S.op("scalar", lambda e: e.activation(out=t1[:], in_=zt[:], func=AF.Silu))
            S.op("vector", lambda e, g=g: e.tensor_mul(
                out=gbuf[:, g, :], in0=t0[:], in1=t1[:]))

        for m in range(KD):
            S.dma(wom[:], wout.rearrange("(k p) m -> p k m", p=128)[:, :, m * 128:(m + 1) * 128])
            for th in range(2):
                for k in range(16):
                    for j in range(2):
                        S.op("tensor", lambda e, k=k, th=th, j=j: e.matmul(
                            out=ps_big[:, j, :], lhsT=wom[:, k, :],
                            rhs=gbuf[:, k, th * H + j * 512:th * H + (j + 1) * 512],
                            start=(k == 0), stop=(k == 15)))
                S.dma(t0[:, 0:H], xn_f[m * 128:(m + 1) * 128, th * H:(th + 1) * H])
                S.op("vector", lambda e: e.tensor_add(
                    out=t1[:, 0:H], in0=t0[:, 0:H], in1=psb2))
                S.dma(attn0[m * 128:(m + 1) * 128, th * H:(th + 1) * H],
                      t1[:, 0:H], queue="gpsimd")

        for k in range(KD):
            S.dma(xnb[:, k, :], xn_bf[k * 128:(k + 1) * 128, :])
        for m in range(32):
            S.dma(w1m[:], w1.rearrange("(k p) m -> p k m", p=128)[:, :, m * 128:(m + 1) * 128])
            for th in range(2):
                for k in range(KD):
                    for j in range(2):
                        S.op("tensor", lambda e, k=k, th=th, j=j: e.matmul(
                            out=ps_big[:, j, :], lhsT=w1m[:, k, :],
                            rhs=xnb[:, k, th * H + j * 512:th * H + (j + 1) * 512],
                            start=(k == 0), stop=(k == KD - 1)))
                S.op("vector", lambda e, m=m: e.tensor_scalar(
                    out=tb[:, 0, :], in0=psb2, scalar1=b1c[:, m:m + 1],
                    scalar2=256.0, op0=ALU.add, op1=ALU.mult))
                S.op("vector", lambda e: e.tensor_scalar(
                    out=tb[:, 1, :], in0=tb[:, 0, :], scalar1=RNE_C,
                    scalar2=RNE_C, op0=ALU.add, op1=ALU.subtract))
                S.op("scalar", lambda e: e.activation(
                    out=tb[:, 2, :], in_=tb[:, 1, :], func=AF.Abs))
                S.op("vector", lambda e: e.tensor_single_scalar(
                    out=tb[:, 3, :], in_=tb[:, 2, :], scalar=PWL_S0, op=ALU.mult))
                for i, (vk, ck) in enumerate(PWL_KNOTS):
                    S.op("scalar", lambda e, i=i, ck=ck: e.activation(
                        out=tb[:, 0, :], in_=tb[:, 2, :], func=AF.Relu,
                        scale=abs(ck), bias=knot[:, i:i + 1]))
                    S.op("vector", lambda e, i=i: e.scalar_tensor_tensor(
                        out=tb[:, 3, :], in0=tb[:, 0, :], scalar=sgn[i],
                        in1=tb[:, 3, :], op0=ALU.mult, op1=ALU.add))
                S.op("scalar", lambda e: e.activation(
                    out=tb[:, 0, :], in_=tb[:, 2, :], func=AF.Relu,
                    bias=knot[:, 16:17]))
                S.op("vector", lambda e: e.tensor_scalar(
                    out=tb[:, 0, :], in0=tb[:, 0, :], scalar1=1.0,
                    scalar2=PWL_J256, op0=ALU.min, op1=ALU.mult))
                S.op("vector", lambda e: e.tensor_add(
                    out=tb[:, 3, :], in0=tb[:, 3, :], in1=tb[:, 0, :]))
                S.op("vector", lambda e: e.scalar_tensor_tensor(
                    out=tb[:, 0, :], in0=tb[:, 1, :], scalar=0.5,
                    in1=tb[:, 3, :], op0=ALU.mult, op1=ALU.add))
                S.op("vector", lambda e: e.tensor_scalar(
                    out=tb[:, 1, :], in0=tb[:, 0, :], scalar1=RNE_C,
                    scalar2=RNE_C, op0=ALU.add, op1=ALU.subtract))
                S.op("vector", lambda e: e.tensor_copy(out=hb[:], in_=tb[:, 1, :]))
                S.dma(h_bf[m * 128:(m + 1) * 128, th * H:(th + 1) * H],
                      hb[:], queue="gpsimd")

        for f4 in range(4):
            for k in range(32):
                S.dma(hq[:, k, :], h_bf[k * 128:(k + 1) * 128, f4 * 512:(f4 + 1) * 512])
            for m in range(KD):
                S.dma(w2m[:], w2.rearrange("(k p) m -> p k m", p=128)[:, :, m * 128:(m + 1) * 128])
                for k in range(32):
                    S.op("tensor", lambda e, k=k: e.matmul(
                        out=psq, lhsT=w2m[:, k, :], rhs=hq[:, k, :],
                        start=(k == 0), stop=(k == 31)))
                S.op("vector", lambda e, m=m: e.tensor_scalar(
                    out=tb[:, 0, 0:512], in0=psq, scalar1=b2c[:, m:m + 1],
                    scalar2=RNE_C, op0=ALU.add, op1=ALU.add))
                S.op("vector", lambda e: e.tensor_scalar(
                    out=tb[:, 1, 0:512], in0=tb[:, 0, 0:512], scalar1=RNE_C,
                    scalar2=1.0 / 256.0, op0=ALU.subtract, op1=ALU.mult))
                S.dma(tb[:, 2, 0:512], x[m * 128:(m + 1) * 128, f4 * 512:(f4 + 1) * 512])
                S.dma(tb[:, 3, 0:512], attn0[m * 128:(m + 1) * 128, f4 * 512:(f4 + 1) * 512])
                S.op("vector", lambda e: e.tensor_add(
                    out=tb[:, 0, 0:512], in0=tb[:, 2, 0:512], in1=tb[:, 3, 0:512]))
                S.op("vector", lambda e: e.tensor_add(
                    out=tb[:, 1, 0:512], in0=tb[:, 0, 0:512], in1=tb[:, 1, 0:512]))
                S.dma(x1[m * 128:(m + 1) * 128, f4 * 512:(f4 + 1) * 512],
                      tb[:, 1, 0:512], queue="gpsimd")


def build_l0out():
    nc, stack, ctx = new_prog()
    S = Seq(ctx)
    with stack:
        io = l0out_io(nc, lambda n, s, d, o: (dout if o else din)(nc, n, s, d))
        ctx.psA = ctx.psum("psA", [128, 4, 512], F32)
        ctx.psB = ctx.psum("psB", [128, 4, 512], F32)
        ctx.phase("l0out")
        emit_l0out2(nc, ctx, S, io)
        S.done()
    return nc



def l1out_io(nc, mk):
    io = {}
    io["y"] = mk("y", (DI, T), F32, 0)
    io["u_f"] = mk("u_f", (DI, T), F32, 0)
    io["z_bf"] = mk("z_bf", (DI, T), BF16, 0)
    io["dskip"] = mk("dskip", (DI,), BF16, 0)
    io["opT"] = mk("opT", (DI, D), BF16, 0)
    io["xn_bf"] = mk("xn_bf", (D, T), BF16, 0)
    io["f1t_bf"] = mk("f1t_bf", (D, 4096), BF16, 0)
    io["w1s"] = mk("w1s", (4096,), F32, 0)
    io["f2t_bf"] = mk("f2t_bf", (D, 4096), BF16, 0)
    io["w2s"] = mk("w2s", (4096,), F32, 0)
    io["h1t_bf"] = mk("h1t_bf", (4096, 4096), BF16, 0)
    io["h1s"] = mk("h1s", (4096,), F32, 0)
    io["h2t_bf"] = mk("h2t_bf", (4096, 4096), BF16, 0)
    io["h2s"] = mk("h2s", (4096,), F32, 0)
    io["w3t_bf"] = mk("w3t_bf", (4096, D), BF16, 0)
    io["w3s"] = mk("w3s", (D,), F32, 0)
    io["x1"] = mk("x1", (D, T), F32, 0)
    io["attn1"] = mk("attn1", (D, T), BF16, 1)
    io["gf"] = mk("gf", (4096, T), BF16, 1)
    io["a1"] = mk("a1", (4096, T), BF16, 1)
    io["a2"] = mk("a2", (4096, T), BF16, 1)
    io["out"] = mk("out", (D, T), F32, 1)
    return io


def emit_l1out(nc, ctx, S, io):
    H = T // 2
    KD = D // 128

    y1 = io["y"]
    u1 = io["u_f"]
    z1 = io["z_bf"]
    dskip = io["dskip"]
    opt = io["opT"]
    xn1 = io["xn_bf"]
    f1t = io["f1t_bf"]
    w1s = io["w1s"]
    f2t = io["f2t_bf"]
    w2s = io["w2s"]
    h1t = io["h1t_bf"]
    h1s = io["h1s"]
    h2t = io["h2t_bf"]
    h2s = io["h2s"]
    w3t = io["w3t_bf"]
    w3s = io["w3s"]
    x1 = io["x1"]
    attn1 = io["attn1"]
    gf = io["gf"]
    a1 = io["a1"]
    a2 = io["a2"]
    out = io["out"]

    if True:
        gbuf = ctx.sbuf("gbuf", [128, 16, T], BF16)
        t0 = ctx.sbuf("t0", [128, T], F32)
        t1 = ctx.sbuf("t1", [128, T], F32)
        zt = ctx.sbuf("zt", [128, T], BF16)
        dcr = ctx.sbuf("dcr", [128, 16], BF16)
        dcol = ctx.sbuf("dcol", [128, 16], F32)
        wm = ctx.sbuf("wm", [128, 16, 128], BF16)
        wm2 = ctx.sbuf("wm2", [128, 16, 128], BF16)
        ab = ctx.sbuf("ab", [128, H], BF16)
        s1c = ctx.sbuf("s1c", [128, 32], F32)
        s2c = ctx.sbuf("s2c", [128, 32], F32)
        sh1c = ctx.sbuf("sh1c", [128, 32], F32)
        sh2c = ctx.sbuf("sh2c", [128, 32], F32)
        s3c = ctx.sbuf("s3c", [128, 8], F32)
        rq = ctx.sbuf("rq", [128, 32, 512], BF16)
        w32 = ctx.sbuf("w32", [128, 32, 128], BF16)
        ps_big = ctx.psA
        psb2 = ps_big[:, 0:2, :].rearrange("p a f -> p (a f)")
        psc2 = ps_big[:, 2:4, :].rearrange("p a f -> p (a f)")
        psq = ps_big[:, 0:1, :].rearrange("p a f -> p (a f)")

        S.dma(dcr[:], dskip.rearrange("(g p) -> p g", p=128), slow=True)
        S.op("vector", lambda e: e.tensor_copy(out=dcol[:], in_=dcr[:]))
        S.dma(s1c[:], w1s.rearrange("(g p) -> p g", p=128), slow=True)
        S.dma(s2c[:], w2s.rearrange("(g p) -> p g", p=128), slow=True)
        S.dma(sh1c[:], h1s.rearrange("(g p) -> p g", p=128), slow=True)
        S.dma(sh2c[:], h2s.rearrange("(g p) -> p g", p=128), slow=True)
        S.dma(s3c[:], w3s.rearrange("(g p) -> p g", p=128), slow=True)

        for g in range(16):
            S.dma(t0[:], y1[g * 128:(g + 1) * 128, :])
            S.dma(t1[:], u1[g * 128:(g + 1) * 128, :])
            S.dma(zt[:], z1[g * 128:(g + 1) * 128, :])
            S.op("vector", lambda e, g=g: e.scalar_tensor_tensor(
                out=t0[:], in0=t1[:], scalar=dcol[:, g:g + 1], in1=t0[:],
                op0=ALU.mult, op1=ALU.add))
            S.op("scalar", lambda e: e.activation(out=t1[:], in_=zt[:], func=AF.Silu))
            S.op("vector", lambda e, g=g: e.tensor_mul(
                out=gbuf[:, g, :], in0=t0[:], in1=t1[:]))

        for m in range(KD):
            S.dma(wm[:], opt.rearrange("(k p) m -> p k m", p=128)[:, :, m * 128:(m + 1) * 128])
            for th in range(2):
                for k in range(16):
                    for j in range(2):
                        S.op("tensor", lambda e, k=k, th=th, j=j: e.matmul(
                            out=ps_big[:, j, :], lhsT=wm[:, k, :],
                            rhs=gbuf[:, k, th * H + j * 512:th * H + (j + 1) * 512],
                            start=(k == 0), stop=(k == 15)))
                S.op("vector", lambda e: e.tensor_copy(out=ab[:], in_=psb2))
                S.dma(attn1[m * 128:(m + 1) * 128, th * H:(th + 1) * H],
                      ab[:], queue="gpsimd")

        for k in range(KD):
            S.dma(gbuf[:, k, :], xn1[k * 128:(k + 1) * 128, :])
        for m in range(32):
            S.dma(wm[:, 0:KD, :], f1t.rearrange("(k p) m -> p k m", p=128)[:, :, m * 128:(m + 1) * 128])
            S.dma(wm2[:, 0:KD, :], f2t.rearrange("(k p) m -> p k m", p=128)[:, :, m * 128:(m + 1) * 128])
            for th in range(2):
                for k in range(KD):
                    for j in range(2):
                        S.op("tensor", lambda e, k=k, th=th, j=j: e.matmul(
                            out=ps_big[:, j, :], lhsT=wm[:, k, :],
                            rhs=gbuf[:, k, th * H + j * 512:th * H + (j + 1) * 512],
                            start=(k == 0), stop=(k == KD - 1)))
                for k in range(KD):
                    for j in range(2):
                        S.op("tensor", lambda e, k=k, th=th, j=j: e.matmul(
                            out=ps_big[:, 2 + j, :], lhsT=wm2[:, k, :],
                            rhs=gbuf[:, k, th * H + j * 512:th * H + (j + 1) * 512],
                            start=(k == 0), stop=(k == KD - 1)))
                S.op("scalar", lambda e, m=m: e.activation(
                    out=t0[:, 0:H], in_=psb2, func=AF.Silu, scale=s1c[:, m:m + 1]))
                S.op("scalar", lambda e, m=m: e.activation(
                    out=t1[:, 0:H], in_=psc2, func=AF.Identity, scale=s2c[:, m:m + 1]))
                S.op("vector", lambda e: e.tensor_mul(
                    out=ab[:], in0=t0[:, 0:H], in1=t1[:, 0:H]))
                S.dma(gf[m * 128:(m + 1) * 128, th * H:(th + 1) * H],
                      ab[:], queue="gpsimd")

        for (src, dst, wmat, scol) in ((gf, a1, h1t, sh1c), (a1, a2, h2t, sh2c)):
            for f4 in range(4):
                for k in range(32):
                    S.dma(rq[:, k, :], src[k * 128:(k + 1) * 128,
                                           f4 * 512:(f4 + 1) * 512])
                for m in range(32):
                    S.dma(w32[:], wmat.rearrange("(k p) m -> p k m", p=128)[:, :, m * 128:(m + 1) * 128])
                    for k in range(32):
                        S.op("tensor", lambda e, k=k: e.matmul(
                            out=psq, lhsT=w32[:, k, :], rhs=rq[:, k, :],
                            start=(k == 0), stop=(k == 31)))
                    S.op("scalar", lambda e, m=m, scol=scol: e.activation(
                        out=ab[:, 0:512], in_=psq, func=AF.Gelu,
                        scale=scol[:, m:m + 1]))
                    S.dma(dst[m * 128:(m + 1) * 128, f4 * 512:(f4 + 1) * 512],
                          ab[:, 0:512], queue="gpsimd")

        for f4 in range(4):
            for k in range(32):
                S.dma(rq[:, k, :], a2[k * 128:(k + 1) * 128, f4 * 512:(f4 + 1) * 512])
            for m in range(KD):
                S.dma(w32[:], w3t.rearrange("(k p) m -> p k m", p=128)[:, :, m * 128:(m + 1) * 128])
                for k in range(32):
                    S.op("tensor", lambda e, k=k: e.matmul(
                        out=psq, lhsT=w32[:, k, :], rhs=rq[:, k, :],
                        start=(k == 0), stop=(k == 31)))
                S.op("scalar", lambda e, m=m: e.activation(
                    out=t0[:, 0:512], in_=psq, func=AF.Identity,
                    scale=s3c[:, m:m + 1]))
                S.dma(t1[:, 0:512], x1[m * 128:(m + 1) * 128, f4 * 512:(f4 + 1) * 512])
                S.dma(ab[:, 0:512], attn1[m * 128:(m + 1) * 128, f4 * 512:(f4 + 1) * 512])
                S.op("vector", lambda e: e.tensor_add(
                    out=t1[:, 0:512], in0=t1[:, 0:512], in1=ab[:, 0:512]))
                S.op("vector", lambda e: e.tensor_add(
                    out=t0[:, 0:512], in0=t0[:, 0:512], in1=t1[:, 0:512]))
                S.dma(out[m * 128:(m + 1) * 128, f4 * 512:(f4 + 1) * 512],
                      t0[:, 0:512], queue="gpsimd")


def build_l1out():
    nc, stack, ctx = new_prog()
    S = Seq(ctx)
    with stack:
        io = l1out_io(nc, lambda n, s, d, o: (dout if o else din)(nc, n, s, d))
        ctx.psA = ctx.psum("psA", [128, 4, 512], F32)
        ctx.psB = ctx.psum("psB", [128, 4, 512], F32)
        ctx.phase("l1out")
        emit_l1out2(nc, ctx, S, io)
        S.done()
    return nc


_PROGS = {}


def _prog(name, builder):
    if name not in _PROGS:
        _PROGS[name] = builder()
    return _PROGS[name]


def _run(name, builder, maps):
    res = run_bass_kernel_spmd(_prog(name, builder), maps, core_ids=list(range(8)))
    return res.results


def kernel(**inp):
    asc = np.ascontiguousarray
    g = {k: np.asarray(v) for k, v in inp.items()}
    x = g["L0_attn_x"]
    xfm = [asc(x[c].T) for c in range(8)]

    f1T = asc(g["L1_ffn_w1_w"].T)
    f2T = asc(g["L1_ffn_w2_w"].T)
    h1T = asc(g["L1_ffn_h1_w"].T)
    h2T = asc(g["L1_ffn_h2_w"].T)
    w3T = asc(g["L1_ffn_w3_w"].T)
    ipT = asc(g["L1_attn_in_proj_w"].T)
    opT = asc(g["L1_attn_out_proj_w"].T)
    xpT = asc(g["L1_attn_x_proj_w"].T)
    dpT = asc(g["L1_attn_dt_proj_w"].T)
    cw0 = asc(g["L0_attn_conv_w"].reshape(DI, 4))
    cw1 = asc(g["L1_attn_conv_w"].reshape(DI, 4))

    if FUSED:
        return kernel_fused(g, xfm, (f1T, f2T, h1T, h2T, w3T, ipT, opT, xpT,
                                     dpT, cw0, cw1))

    pm = {"win": g["L0_attn_W_in"], "wout": g["L0_attn_W_out"],
          "w1": g["L0_ffn_w1"], "w2": g["L0_ffn_w2"],
          "wxp": g["L0_attn_W_xproj"],
          "f1t": f1T, "f2t": f2T, "h1t": h1T, "h2t": h2T, "w3t": w3T}
    W = _run("prep", build_prep, [pm] * 8)[0]

    mA = [{"x": xfm[c], "lnw": g["L0_attn_ln_w"], "lnb": g["L0_attn_ln_b"],
           "win_bf": W["win_bf"], "convw": cw0, "convb": g["L0_attn_conv_b"],
           "wxp_bf": W["wxp_bf"], "wdt": g["L0_attn_W_dt"],
           "bdt": g["L0_attn_b_dt"]} for c in range(8)]
    rA = _run("l0a", lambda: build_lxa(0), mA)

    mS = [{"delta": rA[c]["delta"], "du": rA[c]["du"],
           "du_f16": rA[c]["du_f16"], "Bm": rA[c]["B0"],
           "Cm": rA[c]["C0"], "alog": g["L0_attn_A_log"]} for c in range(8)]
    rS = _run("scan0", lambda: build_scan(0), mS)

    mO = [{"y": rS[c]["y"], "u_f": rA[c]["u_f"], "z_bf": rA[c]["z_bf"],
           "d0": g["L0_attn_D"], "wout_bf": W["wout_bf"], "xn_f": rA[c]["xn_f"],
           "xn_bf": rA[c]["xn_bf"], "w1_bf": W["w1_bf"], "b1": g["L0_ffn_b1"],
           "w2_bf": W["w2_bf"], "b2": g["L0_ffn_b2"], "x": xfm[c]}
          for c in range(8)]
    rO = _run("l0out", build_l0out, mO)

    mA1 = [{"x": rO[c]["x1"], "win_bf": ipT, "convw": cw1,
            "convb": g["L1_attn_conv_b"], "wxp_bf": xpT, "wdt": dpT,
            "bdt": g["L1_attn_dt_proj_b"]} for c in range(8)]
    rA1 = _run("l1a", lambda: build_lxa(1), mA1)

    mS1 = [{"delta": rA1[c]["delta"], "du": rA1[c]["du"],
            "du_f16": rA1[c]["du_f16"], "Bm": rA1[c]["B0"],
            "Cm": rA1[c]["C0"], "alog": g["L1_attn_A_log"]} for c in range(8)]
    rS1 = _run("scan1", lambda: build_scan(1), mS1)

    mO1 = [{"y": rS1[c]["y"], "u_f": rA1[c]["u_f"], "z_bf": rA1[c]["z_bf"],
            "dskip": g["L1_attn_D_skip"], "opT": opT, "xn_bf": rA1[c]["xn_bf"],
            "f1t_bf": W["f1t_bf"], "w1s": g["L1_ffn_w1_s"],
            "f2t_bf": W["f2t_bf"], "w2s": g["L1_ffn_w2_s"],
            "h1t_bf": W["h1t_bf"], "h1s": g["L1_ffn_h1_s"],
            "h2t_bf": W["h2t_bf"], "h2s": g["L1_ffn_h2_s"],
            "w3t_bf": W["w3t_bf"], "w3s": g["L1_ffn_w3_s"],
            "x1": rO[c]["x1"]} for c in range(8)]
    rO1 = _run("l1out", build_l1out, mO1)

    return np.stack([asc(rO1[c]["out"].T) for c in range(8)], axis=0)



def build_fused():
    nc, stack, ctx = new_prog()
    S = Seq(ctx)
    with stack:
        ctx.psA = ctx.psum("psA", [128, 4, 512], F32)
        ctx.psB = ctx.psum("psB", [128, 4, 512], F32)

        def D_(n, s, d):
            return nc.dram_tensor(n, list(s), d, kind="Internal").ap()

        iop = prep_io(nc, lambda n, s, d, o: D_(n, s, d) if o else din(nc, n, s, d))
        ctx.phase("prep")
        emit_prep(nc, ctx, S, iop)

        def mk0(n, s, d, o):
            if n in ("win_bf", "wxp_bf"):
                return iop[n]
            if o:
                return D_("l0_" + n, s, d)
            return din(nc, "l0_" + n, s, d)
        io0 = lxa_io(nc, 0, mk0)
        ctx.phase("l0a")
        emit_lxa(nc, ctx, S, io0, 0)

        y0 = D_("y0", (DI, T), F32)
        ios0 = {"delta": io0["delta"], "du": io0["du"],
                "du_f16": io0["du_f16"], "Bm": io0["B0"],
                "Cm": io0["C0"], "alog": din(nc, "alog0", (DI, 64), F32),
                "y": y0}
        ctx.phase("scan0")
        emit_scan2(nc, ctx, S, ios0, 0)

        def mko0(n, s, d, o):
            m = {"y": y0, "u_f": io0["u_f"], "z_bf": io0["z_bf"],
                 "wout_bf": iop["wout_bf"], "xn_f": io0["xn_f"],
                 "xn_bf": io0["xn_bf"], "w1_bf": iop["w1_bf"],
                 "w2_bf": iop["w2_bf"], "x": io0["x"]}
            if n in m:
                return m[n]
            if o:
                return D_("o0_" + n, s, d)
            return din(nc, "o0_" + n, s, d)
        ioo0 = l0out_io(nc, mko0)
        ctx.phase("l0out")
        emit_l0out2(nc, ctx, S, ioo0)

        def mk1(n, s, d, o):
            if n == "x":
                return ioo0["x1"]
            if o:
                return D_("l1_" + n, s, d)
            return din(nc, "l1_" + n, s, d)
        io1 = lxa_io(nc, 1, mk1)
        ctx.phase("l1a")
        emit_lxa(nc, ctx, S, io1, 1)

        y1 = D_("y1", (DI, T), F32)
        ios1 = {"delta": io1["delta"], "du": io1["du"],
                "du_f16": io1["du_f16"], "Bm": io1["B0"],
                "Cm": io1["C0"], "alog": din(nc, "alog1", (DI, 16), BF16),
                "y": y1}
        ctx.phase("scan1")
        emit_scan2(nc, ctx, S, ios1, 1)

        def mko1(n, s, d, o):
            m = {"y": y1, "u_f": io1["u_f"], "z_bf": io1["z_bf"],
                 "xn_bf": io1["xn_bf"], "f1t_bf": iop["f1t_bf"],
                 "f2t_bf": iop["f2t_bf"], "h1t_bf": iop["h1t_bf"],
                 "h2t_bf": iop["h2t_bf"], "w3t_bf": iop["w3t_bf"],
                 "x1": ioo0["x1"]}
            if n in m:
                return m[n]
            if n == "out":
                return dout(nc, "out", s, d)
            if o:
                return D_("o1_" + n, s, d)
            return din(nc, "o1_" + n, s, d)
        ioo1 = l1out_io(nc, mko1)
        ctx.phase("l1out")
        emit_l1out2(nc, ctx, S, ioo1)
        S.done()
    return nc


FUSED = True


def kernel_fused(g, xfm, hosted):
    (f1T, f2T, h1T, h2T, w3T, ipT, opT, xpT, dpT, cw0, cw1) = hosted
    base = {"win": g["L0_attn_W_in"], "wout": g["L0_attn_W_out"],
            "w1": g["L0_ffn_w1"], "w2": g["L0_ffn_w2"],
            "wxp": g["L0_attn_W_xproj"],
            "f1t": f1T, "f2t": f2T, "h1t": h1T, "h2t": h2T, "w3t": w3T,
            "l0_lnw": g["L0_attn_ln_w"], "l0_lnb": g["L0_attn_ln_b"],
            "l0_convw": cw0, "l0_convb": g["L0_attn_conv_b"],
            "l0_wdt": g["L0_attn_W_dt"], "l0_bdt": g["L0_attn_b_dt"],
            "alog0": g["L0_attn_A_log"],
            "o0_d0": g["L0_attn_D"], "o0_b1": g["L0_ffn_b1"],
            "o0_b2": g["L0_ffn_b2"],
            "l1_win_bf": ipT, "l1_convw": cw1, "l1_convb": g["L1_attn_conv_b"],
            "l1_wxp_bf": xpT, "l1_wdt": dpT, "l1_bdt": g["L1_attn_dt_proj_b"],
            "alog1": g["L1_attn_A_log"],
            "o1_dskip": g["L1_attn_D_skip"], "o1_opT": opT,
            "o1_w1s": g["L1_ffn_w1_s"], "o1_w2s": g["L1_ffn_w2_s"],
            "o1_h1s": g["L1_ffn_h1_s"], "o1_h2s": g["L1_ffn_h2_s"],
            "o1_w3s": g["L1_ffn_w3_s"]}
    maps = [dict(base, l0_x=xfm[c]) for c in range(8)]
    res = _run("fused", build_fused, maps)
    return np.stack([np.ascontiguousarray(res[c]["out"].T) for c in range(8)],
                    axis=0)



def emit_scan2(nc, ctx, S, io, layer):
    NST = 64 if layer == 0 else 16
    a_dt = F32 if layer == 0 else BF16
    tg = f"sc{layer}_"
    W, P, A, V, Y = tg + "rw", tg + "pe", tg + "act", tg + "dve", tg + "yad"
    PD, PA, PV, PP, YST = tg + "pd", tg + "pa", tg + "pv", tg + "pp", tg + "yst"

    delta = io["delta"]
    duf16 = io["du_f16"]
    Bm = io["Bm"]
    Cm = io["Cm"]
    alog = io["alog"]
    y = io["y"]

    q0 = ctx.val("q")
    nb = NST
    NB = 16 * nb

    dl = ctx.sbuf("dl", [128, 2, T], F32)
    duf = ctx.sbuf("duf", [128, 2, T], F16)
    alr = ctx.sbuf("alr", [128, 2, NST], a_dt)
    asb0 = ctx.sbuf("asb0", [128, 2, NST], F32)
    asb = ctx.sbuf("asb", [128, 2, NST], F32)
    brow = ctx.sbuf("brow", [1, 2, T], F32)
    crow = ctx.sbuf("crow", [1, 2, T], F32)
    dat = ctx.sbuf("dat", [128, 2, T], F16)
    besb = ctx.sbuf("besb", [128, 2, T], F16)
    cesb = ctx.sbuf("cesb", [128, 2, T], F16)
    dbut = ctx.sbuf("dbut", [128, 2, T], F16)
    ht = ctx.sbuf("ht", [128, 2, T], F16)
    hct = ctx.sbuf("hct", [128, 2, T], F16)
    yac = ctx.sbuf("yac", [128, 2, T], F32)
    ones_row = ctx.sbuf("ones_row", [1, 128], F32)
    psB, psC = ctx.psA, ctx.psB

    S.op("vector", lambda e: e.memset(ones_row[:], 1.0))
    q0 = ctx.val("q")

    def qw(i):
        return [("q", q0)] if i == 0 else []

    for dg in range(16):
        g2 = dg % 2
        pw = [("q", q0)] if dg < 2 else [(A, 3 * nb * (dg - 1)), (V, 3 * nb * (dg - 1))]
        ctx.emit("sync", (lambda dg=dg, g2=g2: lambda e: e.dma_start(
            out=dl[:, g2, :], in_=delta[dg * 128:(dg + 1) * 128, :]))(),
            waits=tuple(pw), fire=ctx.fire(PD, 16))
        ctx.emit("sync", (lambda dg=dg, g2=g2: lambda e: e.dma_start(
            out=duf[:, g2, :], in_=duf16[dg * 128:(dg + 1) * 128, :]))(),
            fire=ctx.fire(PD, 16))
        ctx.emit("sync", (lambda dg=dg, g2=g2: lambda e: e.dma_start(
            out=alr[:, g2, :], in_=alog[dg * 128:(dg + 1) * 128, :]))(),
            fire=ctx.fire(PD, 16))
        ctx.emit("scalar", (lambda g2=g2: lambda e: e.activation(
            out=asb0[:, g2, :], in_=alr[:, g2, :], func=AF.Exp))(),
            waits=((PD, 48 * (dg + 1)), (PV, max(0, dg - 1))),
            fire=ctx.fire(PA, 1))
        ctx.emit("vector", (lambda g2=g2: lambda e: e.tensor_single_scalar(
            out=asb[:, g2, :], in_=asb0[:, g2, :], scalar=-1.0, op=ALU.mult))(),
            waits=((PA, dg + 1),) + ((A, 3 * nb * (dg - 1)),) if dg >= 2 else ((PA, dg + 1),),
            fire=ctx.fire(PV, 1))
        ctx.emit("gpsimd", (lambda g2=g2: lambda e: e.memset(yac[:, g2, :], 0.0))(),
            waits=((YST, 16 * max(0, dg - 1)),), fire=ctx.fire(PP, 1))

        for n in range(nb):
            i = dg * nb + n
            p = i % 2
            ctx.emit("sync", (lambda n=n, p=p: lambda e: e.dma_start(
                out=brow[:, p, :], in_=Bm[n:n + 1, :]))(),
                waits=tuple(qw(i) + ([(P, i - 1)] if i >= 2 else [])),
                fire=ctx.fire(W, 16))
            ctx.emit("sync", (lambda n=n, p=p: lambda e: e.dma_start(
                out=crow[:, p, :], in_=Cm[n:n + 1, :]))(),
                fire=ctx.fire(W, 16))
            for fc in range(4):
                wts = [(W, 32 * (i + 1)), (A, 3 * i)] if fc == 0 else []
                ctx.emit("tensor", (lambda p=p, fc=fc: lambda e: e.matmul(
                    out=psB[:, fc, :], lhsT=ones_row[:],
                    rhs=brow[:, p, fc * 512:(fc + 1) * 512],
                    start=True, stop=True))(),
                    waits=tuple(qw(i) + wts))
            for fc in range(4):
                ctx.emit("tensor", (lambda p=p, fc=fc: lambda e: e.matmul(
                    out=psC[:, fc, :], lhsT=ones_row[:],
                    rhs=crow[:, p, fc * 512:(fc + 1) * 512],
                    start=True, stop=True))(),
                    fire=ctx.fire(P, 1) if fc == 3 else None)
            ctx.emit("scalar", (lambda g2=g2, n=n, p=p: lambda e: e.activation(
                out=dat[:, p, :], in_=dl[:, g2, :], func=AF.Exp,
                scale=asb[:, g2, n:n + 1]))(),
                waits=tuple(qw(i) + [(PV, dg + 1)] + ([(V, 3 * (i - 1))] if i >= 2 else [])),
                fire=ctx.fire(A, 1))
            ctx.emit("scalar", (lambda p=p: lambda e: e.activation(
                out=besb[:, p, :],
                in_=psB[:].rearrange("p a f -> p (a f)"),
                func=AF.Identity))(),
                waits=((P, i + 1),), fire=ctx.fire(A, 1))
            ctx.emit("scalar", (lambda p=p: lambda e: e.activation(
                out=cesb[:, p, :],
                in_=psC[:].rearrange("p a f -> p (a f)"),
                func=AF.Identity))(),
                fire=ctx.fire(A, 1))
            ctx.emit("vector", (lambda g2=g2, p=p: lambda e: e.tensor_mul(
                out=dbut[:, p, :], in0=duf[:, g2, :], in1=besb[:, p, :]))(),
                waits=tuple(qw(i) + [(A, 3 * i + 2)]),
                fire=ctx.fire(V, 1))
            ctx.emit("vector", (lambda p=p: lambda e: e.tensor_tensor_scan(
                out=ht[:, p, :], data0=dat[:, p, :], data1=dbut[:, p, :],
                initial=0.0, op0=ALU.mult, op1=ALU.add))(),
                waits=((V, 3 * i + 1),), fire=ctx.fire(V, 1))
            ctx.emit("vector", (lambda p=p: lambda e: e.tensor_mul(
                out=hct[:, p, :], in0=ht[:, p, :], in1=cesb[:, p, :]))(),
                waits=((V, 3 * i + 2), (A, 3 * i + 3)) + (((Y, i - 1),) if i >= 2 else ()),
                fire=ctx.fire(V, 1))
            ctx.emit("gpsimd", (lambda g2=g2, p=p: lambda e: e.tensor_add(
                out=yac[:, g2, :], in0=yac[:, g2, :], in1=hct[:, p, :]))(),
                waits=((V, 3 * i + 3), (Y, i), (PP, dg + 1)),
                fire=ctx.fire(Y, 1))
        ctx.emit("gpsimd", (lambda dg=dg, g2=g2: lambda e: e.dma_start(
            out=y[dg * 128:(dg + 1) * 128, :], in_=yac[:, g2, :]))(),
            waits=((Y, nb * (dg + 1)),), fire=ctx.fire(YST, 16))
    ctx.emit("vector", lambda e: e.engine_nop(),
             waits=((YST, 16 * 16),), fire=ctx.fire("q", 1))



def emit_l1out2(nc, ctx, S, io):
    H = T // 2
    KD = D // 128
    tg = "o1_"

    y1, u1, z1 = io["y"], io["u_f"], io["z_bf"]
    dskip, opt, xn1 = io["dskip"], io["opT"], io["xn_bf"]
    f1t, w1s, f2t, w2s = io["f1t_bf"], io["w1s"], io["f2t_bf"], io["w2s"]
    h1t, h1s, h2t, h2s = io["h1t_bf"], io["h1s"], io["h2t_bf"], io["h2s"]
    w3t, w3s, x1 = io["w3t_bf"], io["w3s"], io["x1"]
    attn1, gf, a1, a2, out = io["attn1"], io["gf"], io["a1"], io["a2"], io["out"]

    gbuf = ctx.sbuf("gbuf", [128, 16, T], BF16)
    t0 = ctx.sbuf("t0", [128, T], F32)
    t1 = ctx.sbuf("t1", [128, T], F32)
    zt = ctx.sbuf("zt", [128, T], BF16)
    dcr = ctx.sbuf("dcr", [128, 16], BF16)
    dcol = ctx.sbuf("dcol", [128, 16], F32)
    wm = ctx.sbuf("wm", [128, 2, 16, 128], BF16)
    wm2 = ctx.sbuf("wm2", [128, 2, 16, 128], BF16)
    ab = ctx.sbuf("ab", [128, 2, H], BF16)
    ev0 = ctx.sbuf("ev0", [128, 2, H], BF16)
    ev1 = ctx.sbuf("ev1", [128, 2, H], BF16)
    s1c = ctx.sbuf("s1c", [128, 32], F32)
    s2c = ctx.sbuf("s2c", [128, 32], F32)
    sh1c = ctx.sbuf("sh1c", [128, 32], F32)
    sh2c = ctx.sbuf("sh2c", [128, 32], F32)
    s3c = ctx.sbuf("s3c", [128, 8], F32)
    rq = ctx.sbuf("rq", [128, 2, 32, 512], BF16)
    w32 = ctx.sbuf("w32", [128, 2, 32, 128], BF16)
    fx = ctx.sbuf("fx", [128, 2, 512], F32)
    fa = ctx.sbuf("fa", [128, 2, 512], BF16)
    fo = ctx.sbuf("fo", [128, 2, 512], F32)
    ps = ctx.psA

    q0 = ctx.val("q")
    S.dma(dcr[:], dskip.rearrange("(g p) -> p g", p=128), slow=True)
    S.op("vector", lambda e: e.tensor_copy(out=dcol[:], in_=dcr[:]))
    S.dma(s1c[:], w1s.rearrange("(g p) -> p g", p=128), slow=True)
    S.dma(s2c[:], w2s.rearrange("(g p) -> p g", p=128), slow=True)
    S.dma(sh1c[:], h1s.rearrange("(g p) -> p g", p=128), slow=True)
    S.dma(sh2c[:], h2s.rearrange("(g p) -> p g", p=128), slow=True)
    S.dma(s3c[:], w3s.rearrange("(g p) -> p g", p=128), slow=True)
    q0 = ctx.val("q")

    for g in range(16):
        S.dma(t0[:], y1[g * 128:(g + 1) * 128, :])
        S.dma(t1[:], u1[g * 128:(g + 1) * 128, :])
        S.dma(zt[:], z1[g * 128:(g + 1) * 128, :])
        S.op("vector", (lambda g=g: lambda e: e.scalar_tensor_tensor(
            out=t0[:], in0=t1[:], scalar=dcol[:, g:g + 1],
            in1=t0[:], op0=ALU.mult, op1=ALU.add))())
        S.op("scalar", lambda e: e.activation(out=t1[:], in_=zt[:], func=AF.Silu))
        S.op("vector", (lambda g=g: lambda e: e.tensor_mul(
            out=gbuf[:, g, :], in0=t0[:], in1=t1[:]))())
    q1 = ctx.val("q")

    BL, BP, BE, BS = tg + "bl", tg + "bp", tg + "be", tg + "bs"
    optr = opt.rearrange("(k p) m -> p k m", p=128)
    for m in range(KD):
        pw = m % 2
        ctx.emit("sync", (lambda m=m, pw=pw: lambda e: e.dma_start(
            out=wm[:, pw, :, :], in_=optr[:, :, m * 128:(m + 1) * 128]))(),
            waits=tuple([("q", q1)] + ([(BP, 2 * (m - 1))] if m >= 2 else [])),
            fire=ctx.fire(BL, 16))
        for th in range(2):
            it = m * 2 + th
            sl = (it % 2) * 2
            for k in range(16):
                for j in range(2):
                    wts = []
                    if k == 0 and j == 0:
                        wts = [(BL, 16 * (m + 1)), ("q", q1)]
                        if it >= 2:
                            wts.append((BE, it - 1))
                    ctx.emit("tensor", (lambda k=k, th=th, j=j, sl=sl, pw=pw: lambda e: e.matmul(
                        out=ps[:, sl + j, :], lhsT=wm[:, pw, k, :],
                        rhs=gbuf[:, k, th * H + j * 512:th * H + (j + 1) * 512],
                        start=(k == 0), stop=(k == 15)))(),
                        waits=tuple(wts),
                        fire=ctx.fire(BP, 1) if (k, j) == (15, 1) else None)
            ctx.emit("vector", (lambda it=it, sl=sl: lambda e: e.tensor_copy(
                out=ab[:, it % 2, :],
                in_=ps[:, sl:sl + 2, :].rearrange("p a f -> p (a f)")))(),
                waits=tuple([(BP, it + 1)] + ([(BS, 16 * (it - 1))] if it >= 2 else [])),
                fire=ctx.fire(BE, 1))
            ctx.emit("gpsimd", (lambda m=m, th=th, it=it: lambda e: e.dma_start(
                out=attn1[m * 128:(m + 1) * 128, th * H:(th + 1) * H],
                in_=ab[:, it % 2, :]))(),
                waits=((BE, it + 1),), fire=ctx.fire(BS, 16))

    CL, CP, CE, CS, CX = tg + "cl", tg + "cp", tg + "ce", tg + "cs", tg + "cx"
    f1r = f1t.rearrange("(k p) m -> p k m", p=128)
    f2r = f2t.rearrange("(k p) m -> p k m", p=128)
    for k in range(KD):
        ctx.emit("sync", (lambda k=k: lambda e: e.dma_start(
            out=gbuf[:, k, :], in_=xn1[k * 128:(k + 1) * 128, :]))(),
            waits=((BP, 16),) if k == 0 else (), fire=ctx.fire(CX, 16))
    for m in range(32):
        pw = m % 2
        w = [(CP, 2 * (m - 1))] if m >= 2 else []
        ctx.emit("sync", (lambda m=m, pw=pw: lambda e: e.dma_start(
            out=wm[:, pw, 0:KD, :], in_=f1r[:, :, m * 128:(m + 1) * 128]))(),
            waits=tuple(w), fire=ctx.fire(CL, 16))
        ctx.emit("sync", (lambda m=m, pw=pw: lambda e: e.dma_start(
            out=wm2[:, pw, 0:KD, :], in_=f2r[:, :, m * 128:(m + 1) * 128]))(),
            fire=ctx.fire(CL, 16))
        for th in range(2):
            it = m * 2 + th
            for k in range(KD):
                for j in range(2):
                    wts = []
                    if k == 0 and j == 0:
                        wts = [(CL, 32 * (m + 1)), (CX, 16 * KD)]
                        if it >= 1:
                            wts.append((CE, 3 * it - 1))
                    ctx.emit("tensor", (lambda k=k, th=th, j=j, pw=pw: lambda e: e.matmul(
                        out=ps[:, j, :], lhsT=wm[:, pw, k, :],
                        rhs=gbuf[:, k, th * H + j * 512:th * H + (j + 1) * 512],
                        start=(k == 0), stop=(k == KD - 1)))(),
                        waits=tuple(wts))
            for k in range(KD):
                for j in range(2):
                    ctx.emit("tensor", (lambda k=k, th=th, j=j, pw=pw: lambda e: e.matmul(
                        out=ps[:, 2 + j, :], lhsT=wm2[:, pw, k, :],
                        rhs=gbuf[:, k, th * H + j * 512:th * H + (j + 1) * 512],
                        start=(k == 0), stop=(k == KD - 1)))(),
                        fire=ctx.fire(CP, 1) if (k, j) == (KD - 1, 1) else None)
            pb = it % 2
            ctx.emit("scalar", (lambda m=m, pb=pb: lambda e: e.activation(
                out=ev0[:, pb, :], in_=ps[:, 0:2, :].rearrange("p a f -> p (a f)"),
                func=AF.Silu, scale=s1c[:, m:m + 1]))(),
                waits=tuple([(CP, it + 1)] + ([(CE, 3 * (it - 1))] if it >= 2 else [])),
                fire=ctx.fire(CE, 1))
            ctx.emit("scalar", (lambda m=m, pb=pb: lambda e: e.activation(
                out=ev1[:, pb, :], in_=ps[:, 2:4, :].rearrange("p a f -> p (a f)"),
                func=AF.Identity, scale=s2c[:, m:m + 1]))(),
                fire=ctx.fire(CE, 1))
            ctx.emit("vector", (lambda pb=pb: lambda e: e.tensor_mul(
                out=ab[:, pb, :], in0=ev0[:, pb, :], in1=ev1[:, pb, :]))(),
                waits=tuple([(CE, 3 * it + 2)] + ([(CS, 16 * (it - 1))] if it >= 2 else [])),
                fire=ctx.fire(CE, 1))
            ctx.emit("gpsimd", (lambda m=m, th=th, pb=pb: lambda e: e.dma_start(
                out=gf[m * 128:(m + 1) * 128, th * H:(th + 1) * H],
                in_=ab[:, pb, :]))(),
                waits=((CE, 3 * it + 3),), fire=ctx.fire(CS, 16))

    def hstage(st, src, dst, wmat, scol, prev_done):
        L, R, P, E, X = st + "l", st + "r", st + "p", st + "e", st + "x"
        wr = wmat.rearrange("(k p) m -> p k m", p=128)
        for f4 in range(4):
            pf = f4 % 2
            for k in range(32):
                w = [prev_done] if f4 == 0 and k == 0 else []
                if f4 >= 2:
                    w.append((P, 32 * (f4 - 1)))
                ctx.emit("sync", (lambda k=k, f4=f4, pf=pf: lambda e: e.dma_start(
                    out=rq[:, pf, k, :],
                    in_=src[k * 128:(k + 1) * 128, f4 * 512:(f4 + 1) * 512]))(),
                    waits=tuple(w), fire=ctx.fire(R, 16))
            for m in range(32):
                it = f4 * 32 + m
                pw = it % 2
                w = [prev_done] if it == 0 else []
                if it >= 2:
                    w.append((P, it - 1))
                ctx.emit("scalar", (lambda it=it, m=m, pw=pw: lambda e: e.dma_start(
                    out=w32[:, pw, :, :], in_=wr[:, :, m * 128:(m + 1) * 128]))(),
                    waits=tuple(w), fire=ctx.fire(L, 16))
                for k in range(32):
                    wts = []
                    if k == 0:
                        wts = [(L, 16 * (it + 1)), (R, 512 * (f4 + 1))]
                        if it >= 2:
                            wts.append((E, it - 1))
                    ctx.emit("tensor", (lambda k=k, pw=pw, pf=pf, it=it: lambda e: e.matmul(
                        out=ps[:, it % 4, :], lhsT=w32[:, pw, k, :],
                        rhs=rq[:, pf, k, :], start=(k == 0), stop=(k == 31)))(),
                        waits=tuple(wts),
                        fire=ctx.fire(P, 1) if k == 31 else None)
                ctx.emit("scalar", (lambda it=it, m=m: lambda e: e.activation(
                    out=fa[:, it % 2, :], in_=ps[:, it % 4, :], func=AF.Gelu,
                    scale=scol[:, m:m + 1]))(),
                    waits=tuple([(P, it + 1)] + ([(st + "s", 16 * (it - 1))] if it >= 2 else [])),
                    fire=ctx.fire(E, 1))
                ctx.emit("gpsimd", (lambda it=it, m=m, f4=f4: lambda e: e.dma_start(
                    out=dst[m * 128:(m + 1) * 128, f4 * 512:(f4 + 1) * 512],
                    in_=fa[:, it % 2, :]))(),
                    waits=((E, it + 1),), fire=ctx.fire(st + "s", 16))
        return (st + "s", 16 * 128)

    dD = hstage(tg + "D", gf, a1, h1t, sh1c, (CS, 16 * 64))
    dE = hstage(tg + "E", a1, a2, h2t, sh2c, dD)

    FL, FR, FP, FE, FS = tg + "fl", tg + "fr", tg + "fp", tg + "fe", tg + "fs"
    w3r = w3t.rearrange("(k p) m -> p k m", p=128)
    for f4 in range(4):
        pf = f4 % 2
        for k in range(32):
            w = [dE, (BS, 16 * 16)] if f4 == 0 and k == 0 else []
            if f4 >= 2:
                w.append((FP, 8 * (f4 - 1)))
            ctx.emit("sync", (lambda k=k, f4=f4, pf=pf: lambda e: e.dma_start(
                out=rq[:, pf, k, :],
                in_=a2[k * 128:(k + 1) * 128, f4 * 512:(f4 + 1) * 512]))(),
                waits=tuple(w), fire=ctx.fire(FR, 16))
        for m in range(KD):
            it = f4 * KD + m
            pw = it % 2
            w = [dE] if it == 0 else []
            if it >= 2:
                w.append((FP, it - 1))
            ctx.emit("scalar", (lambda it=it, m=m, pw=pw: lambda e: e.dma_start(
                out=w32[:, pw, :, :], in_=w3r[:, :, m * 128:(m + 1) * 128]))(),
                waits=tuple(w), fire=ctx.fire(FL, 16))
            ctx.emit("sync", (lambda it=it, m=m, f4=f4, pw=pw: lambda e: e.dma_start(
                out=fx[:, pw, :],
                in_=x1[m * 128:(m + 1) * 128, f4 * 512:(f4 + 1) * 512]))(),
                waits=((FE, 3 * (it - 1)),) if it >= 2 else (),
                fire=ctx.fire(FL, 16))
            ctx.emit("sync", (lambda it=it, m=m, f4=f4, pw=pw: lambda e: e.dma_start(
                out=fa[:, pw, :],
                in_=attn1[m * 128:(m + 1) * 128, f4 * 512:(f4 + 1) * 512]))(),
                fire=ctx.fire(FL, 16))
            for k in range(32):
                wts = []
                if k == 0:
                    wts = [(FL, 48 * (it + 1)), (FR, 512 * (f4 + 1))]
                    if it >= 2:
                        wts.append((FE, 3 * (it - 1)))
                ctx.emit("tensor", (lambda k=k, pw=pw, pf=pf, it=it: lambda e: e.matmul(
                    out=ps[:, it % 4, :], lhsT=w32[:, pw, k, :],
                    rhs=rq[:, pf, k, :], start=(k == 0), stop=(k == 31)))(),
                    waits=tuple(wts),
                    fire=ctx.fire(FP, 1) if k == 31 else None)
            ctx.emit("scalar", (lambda it=it, m=m, pw=pw: lambda e: e.activation(
                out=fo[:, pw, :], in_=ps[:, it % 4, :], func=AF.Identity,
                scale=s3c[:, m:m + 1]))(),
                waits=((FP, it + 1),), fire=ctx.fire(FE, 1))
            ctx.emit("vector", (lambda pw=pw: lambda e: e.tensor_add(
                out=fx[:, pw, :], in0=fx[:, pw, :], in1=fa[:, pw, :]))(),
                waits=((FL, 48 * (it + 1)), (FE, 3 * it + 1)),
                fire=ctx.fire(FE, 1))
            ctx.emit("vector", (lambda pw=pw: lambda e: e.tensor_add(
                out=fo[:, pw, :], in0=fo[:, pw, :], in1=fx[:, pw, :]))(),
                waits=tuple([(FE, 3 * it + 2)] + ([(FS, 16 * (it - 1))] if it >= 2 else [])),
                fire=ctx.fire(FE, 1))
            ctx.emit("gpsimd", (lambda it=it, m=m, f4=f4, pw=pw: lambda e: e.dma_start(
                out=out[m * 128:(m + 1) * 128, f4 * 512:(f4 + 1) * 512],
                in_=fo[:, pw, :]))(),
                waits=((FE, 3 * it + 3),), fire=ctx.fire(FS, 16))
    ctx.emit("vector", lambda e: e.engine_nop(),
             waits=((FS, 16 * 32), (BS, 16 * 16)), fire=ctx.fire("q", 1))



def emit_l0out2(nc, ctx, S, io):
    H = T // 2
    KD = D // 128
    tg = "o0_"

    y, u_f, z_bf, d0 = io["y"], io["u_f"], io["z_bf"], io["d0"]
    wout, xn_f, xn_bf = io["wout_bf"], io["xn_f"], io["xn_bf"]
    w1, b1, w2, b2, x = io["w1_bf"], io["b1"], io["w2_bf"], io["b2"], io["x"]
    attn0, h_bf, x1 = io["attn0"], io["h_bf"], io["x1"]

    gbuf = ctx.sbuf("gbuf", [128, 16, T], BF16)
    xnb = ctx.sbuf("xnb", [128, KD, T], BF16)
    t0 = ctx.sbuf("t0", [128, T], F32)
    t1 = ctx.sbuf("t1", [128, T], F32)
    zt = ctx.sbuf("zt", [128, T], BF16)
    dcol = ctx.sbuf("dcol", [128, 16], F32)
    wom = ctx.sbuf("wom", [128, 2, 16, 128], BF16)
    w1m = ctx.sbuf("w1m", [128, 2, KD, 128], BF16)
    b1c = ctx.sbuf("b1c", [128, 32], F32)
    b2c = ctx.sbuf("b2c", [128, 8], F32)
    knot = ctx.sbuf("knot", [128, 18], F32)
    xnt = ctx.sbuf("xnt", [128, 2, H], F32)
    abf = ctx.sbuf("abf", [128, 2, H], F32)
    s0b = ctx.sbuf("s0b", [128, 2, H], F32)
    qb = ctx.sbuf("qb", [128, 2, H], F32)
    vb = ctx.sbuf("vb", [128, 2, H], F32)
    acc = ctx.sbuf("acc", [128, 2, H], F32)
    rr = ctx.sbuf("rr", [128, 2, H], F32)
    hb = ctx.sbuf("hb", [128, 2, H], BF16)
    fx = ctx.sbuf("fx", [128, 2, 512], F32)
    fa = ctx.sbuf("fa", [128, 2, 512], F32)
    fo = ctx.sbuf("fo", [128, 2, 512], F32)
    ps = ctx.psA

    sgn = []
    for i, (vk, ck) in enumerate(PWL_KNOTS):
        S.op("vector", (lambda i=i, vk=vk, ck=ck: lambda e: e.memset(
            knot[:, i:i + 1], -abs(ck) * vk))())
        sgn.append(1.0 if ck >= 0 else -1.0)
    S.op("vector", lambda e: e.memset(knot[:, 16:17], -1024.0))
    S.dma(dcol[:], d0.rearrange("(g p) -> p g", p=128), slow=True)
    S.dma(b1c[:], b1.rearrange("(g p) -> p g", p=128), slow=True)
    S.dma(b2c[:], b2.rearrange("(g p) -> p g", p=128), slow=True)
    S.op("vector", lambda e: e.tensor_single_scalar(
        out=b2c[:], in_=b2c[:], scalar=256.0, op=ALU.mult))

    for g in range(16):
        S.dma(t0[:], y[g * 128:(g + 1) * 128, :])
        S.dma(t1[:], u_f[g * 128:(g + 1) * 128, :])
        S.dma(zt[:], z_bf[g * 128:(g + 1) * 128, :])
        S.op("vector", (lambda g=g: lambda e: e.scalar_tensor_tensor(
            out=t0[:], in0=t1[:], scalar=dcol[:, g:g + 1],
            in1=t0[:], op0=ALU.mult, op1=ALU.add))())
        S.op("scalar", lambda e: e.activation(out=t1[:], in_=zt[:], func=AF.Silu))
        S.op("vector", (lambda g=g: lambda e: e.tensor_mul(
            out=gbuf[:, g, :], in0=t0[:], in1=t1[:]))())
    for k in range(KD):
        S.dma(xnb[:, k, :], xn_bf[k * 128:(k + 1) * 128, :])
    q1 = ctx.val("q")

    BL, BX, BP, BE, BS = tg + "bl", tg + "bx", tg + "bp", tg + "be", tg + "bs"
    womr = wout.rearrange("(k p) m -> p k m", p=128)
    for m in range(KD):
        pw = m % 2
        ctx.emit("sync", (lambda m=m, pw=pw: lambda e: e.dma_start(
            out=wom[:, pw, :, :], in_=womr[:, :, m * 128:(m + 1) * 128]))(),
            waits=tuple([("q", q1)] + ([(BP, 2 * (m - 1))] if m >= 2 else [])),
            fire=ctx.fire(BL, 16))
        for th in range(2):
            it = m * 2 + th
            pb = it % 2
            sl = pb * 2
            ctx.emit("sync", (lambda m=m, th=th, pb=pb: lambda e: e.dma_start(
                out=xnt[:, pb, :],
                in_=xn_f[m * 128:(m + 1) * 128, th * H:(th + 1) * H]))(),
                waits=((BE, 2 * (it - 1)),) if it >= 2 else (),
                fire=ctx.fire(BX, 16))
            for k in range(16):
                for j in range(2):
                    wts = []
                    if k == 0 and j == 0:
                        wts = [(BL, 16 * (m + 1)), ("q", q1)]
                        if it >= 2:
                            wts.append((BE, 2 * (it - 1)))
                    ctx.emit("tensor", (lambda k=k, th=th, j=j, sl=sl, pw=pw: lambda e: e.matmul(
                        out=ps[:, sl + j, :], lhsT=wom[:, pw, k, :],
                        rhs=gbuf[:, k, th * H + j * 512:th * H + (j + 1) * 512],
                        start=(k == 0), stop=(k == 15)))(),
                        waits=tuple(wts),
                        fire=ctx.fire(BP, 1) if (k, j) == (15, 1) else None)
            ctx.emit("vector", (lambda it=it, pb=pb, sl=sl: lambda e: e.tensor_add(
                out=abf[:, pb, :], in0=xnt[:, pb, :],
                in1=ps[:, sl:sl + 2, :].rearrange("p a f -> p (a f)")))(),
                waits=tuple([(BP, it + 1), (BX, 16 * (it + 1))]
                            + ([(BS, 16 * (it - 1))] if it >= 2 else [])),
                fire=ctx.fire(BE, 2))
            ctx.emit("gpsimd", (lambda m=m, th=th, pb=pb: lambda e: e.dma_start(
                out=attn0[m * 128:(m + 1) * 128, th * H:(th + 1) * H],
                in_=abf[:, pb, :]))(),
                waits=((BE, 2 * it + 2),), fire=ctx.fire(BS, 16))

    CL, CP, CA, CV, CS = tg + "cl", tg + "cp", tg + "ca", tg + "cv", tg + "cs"
    w1r = w1.rearrange("(k p) m -> p k m", p=128)
    NA, NV = 18, 24
    for m in range(32):
        pw = m % 2
        ctx.emit("sync", (lambda m=m, pw=pw: lambda e: e.dma_start(
            out=w1m[:, pw, :, :], in_=w1r[:, :, m * 128:(m + 1) * 128]))(),
            waits=tuple([("q", q1)] + ([(CP, 2 * (m - 1))] if m >= 2 else [])),
            fire=ctx.fire(CL, 16))
        for th in range(2):
            it = m * 2 + th
            p = it % 2
            va = NV * it
            aa = NA * it
            for k in range(KD):
                for j in range(2):
                    wts = []
                    if k == 0 and j == 0:
                        wts = [(CL, 16 * (m + 1)), ("q", q1)]
                        if it >= 1:
                            wts.append((CV, NV * (it - 1) + 1))
                    ctx.emit("tensor", (lambda k=k, th=th, j=j, p=p, pw=pw: lambda e: e.matmul(
                        out=ps[:, p * 2 + j, :], lhsT=w1m[:, pw, k, :],
                        rhs=xnb[:, k, th * H + j * 512:th * H + (j + 1) * 512],
                        start=(k == 0), stop=(k == KD - 1)))(),
                        waits=tuple(wts),
                        fire=ctx.fire(CP, 1) if (k, j) == (KD - 1, 1) else None)
            psv = ps[:, p * 2:p * 2 + 2, :].rearrange("p a f -> p (a f)")
            ctx.emit("vector", (lambda m=m, p=p, psv=psv: lambda e: e.tensor_scalar(
                out=s0b[:, p, :], in0=psv, scalar1=b1c[:, m:m + 1],
                scalar2=256.0, op0=ALU.add, op1=ALU.mult))(),
                waits=tuple([(CP, it + 1)] + ([(CV, NV * (it - 1) + 2)] if it >= 1 else [])),
                fire=ctx.fire(CV, 1))
            ctx.emit("vector", (lambda p=p: lambda e: e.tensor_scalar(
                out=qb[:, p, :], in0=s0b[:, p, :], scalar1=RNE_C,
                scalar2=RNE_C, op0=ALU.add, op1=ALU.subtract))(),
                waits=((CV, va + 1),) + (((CV, NV * (it - 1) + NV - 1),) if it >= 1 else ()),
                fire=ctx.fire(CV, 1))
            ctx.emit("scalar", (lambda p=p: lambda e: e.activation(
                out=vb[:, p, :], in_=qb[:, p, :], func=AF.Abs))(),
                waits=((CV, va + 2),) + (((CA, NA * (it - 1) + NA),) if it >= 1 else ()),
                fire=ctx.fire(CA, 1))
            ctx.emit("vector", (lambda p=p: lambda e: e.tensor_single_scalar(
                out=acc[:, p, :], in_=vb[:, p, :], scalar=PWL_S0, op=ALU.mult))(),
                waits=((CA, aa + 1),), fire=ctx.fire(CV, 1))
            for kk, (vk, ck) in enumerate(PWL_KNOTS):
                rp = kk % 2
                w = [(CA, aa + 1)]
                if kk >= 2:
                    w = [(CV, va + 3 + kk - 1)]
                ctx.emit("scalar", (lambda kk=kk, ck=ck, p=p, rp=rp: lambda e: e.activation(
                    out=rr[:, rp, :], in_=vb[:, p, :], func=AF.Relu,
                    scale=abs(ck), bias=knot[:, kk:kk + 1]))(),
                    waits=tuple(w), fire=ctx.fire(CA, 1))
                ctx.emit("vector", (lambda kk=kk, p=p, rp=rp: lambda e: e.scalar_tensor_tensor(
                    out=acc[:, p, :], in0=rr[:, rp, :], scalar=sgn[kk],
                    in1=acc[:, p, :], op0=ALU.mult, op1=ALU.add))(),
                    waits=((CA, aa + 2 + kk), (CV, va + 3 + kk)),
                    fire=ctx.fire(CV, 1))
            ctx.emit("scalar", (lambda p=p: lambda e: e.activation(
                out=rr[:, 0, :], in_=vb[:, p, :], func=AF.Relu,
                bias=knot[:, 16:17]))(),
                waits=((CV, va + 3 + 15),), fire=ctx.fire(CA, 1))
            ctx.emit("vector", (lambda p=p: lambda e: e.tensor_scalar(
                out=rr[:, 0, :], in0=rr[:, 0, :], scalar1=1.0,
                scalar2=PWL_J256, op0=ALU.min, op1=ALU.mult))(),
                waits=((CA, aa + NA),), fire=ctx.fire(CV, 1))
            ctx.emit("vector", (lambda p=p: lambda e: e.tensor_add(
                out=acc[:, p, :], in0=acc[:, p, :], in1=rr[:, 0, :]))(),
                waits=((CV, va + 20),), fire=ctx.fire(CV, 1))
            ctx.emit("vector", (lambda p=p: lambda e: e.scalar_tensor_tensor(
                out=s0b[:, p, :], in0=qb[:, p, :], scalar=0.5,
                in1=acc[:, p, :], op0=ALU.mult, op1=ALU.add))(),
                waits=((CV, va + 21),), fire=ctx.fire(CV, 1))
            ctx.emit("vector", (lambda p=p: lambda e: e.tensor_scalar(
                out=qb[:, p, :], in0=s0b[:, p, :], scalar1=RNE_C,
                scalar2=RNE_C, op0=ALU.add, op1=ALU.subtract))(),
                waits=((CV, va + 22),), fire=ctx.fire(CV, 1))
            ctx.emit("vector", (lambda p=p: lambda e: e.tensor_copy(
                out=hb[:, p, :], in_=qb[:, p, :]))(),
                waits=tuple([(CV, va + 23)] + ([(CS, 16 * (it - 1))] if it >= 2 else [])),
                fire=ctx.fire(CV, 1))
            ctx.emit("gpsimd", (lambda m=m, th=th, p=p: lambda e: e.dma_start(
                out=h_bf[m * 128:(m + 1) * 128, th * H:(th + 1) * H],
                in_=hb[:, p, :]))(),
                waits=((CV, va + 24),), fire=ctx.fire(CS, 16))

    DL, DR, DP, DE, DS = tg + "dl", tg + "dr", tg + "dp", tg + "de", tg + "ds"
    w2r = w2.rearrange("(k p) m -> p k m", p=128)

    def rqv(pf, k):
        fl = pf * 32 + k
        return gbuf[:, fl // 4, (fl % 4) * 512:(fl % 4) * 512 + 512]

    def w2v(pw):
        return xnb[:, pw * 2:pw * 2 + 2, :].rearrange(
            "p a t -> p (a t)").rearrange("p (k m) -> p k m", m=128)

    for f4 in range(4):
        pf = f4 % 2
        for k in range(32):
            w = [(CS, 16 * 64), (BS, 16 * 16)] if f4 == 0 and k == 0 else []
            if f4 >= 2:
                w.append((DP, 8 * (f4 - 1)))
            ctx.emit("sync", (lambda k=k, f4=f4, pf=pf: lambda e: e.dma_start(
                out=rqv(pf, k),
                in_=h_bf[k * 128:(k + 1) * 128, f4 * 512:(f4 + 1) * 512]))(),
                waits=tuple(w), fire=ctx.fire(DR, 16))
        for m in range(KD):
            it = f4 * KD + m
            pw = it % 2
            w = [(CS, 16 * 64), (BS, 16 * 16)] if it == 0 else []
            if it >= 2:
                w.append((DP, it - 1))
            ctx.emit("scalar", (lambda it=it, m=m, pw=pw: lambda e: e.dma_start(
                out=w2v(pw), in_=w2r[:, :, m * 128:(m + 1) * 128]))(),
                waits=tuple(w), fire=ctx.fire(DL, 16))
            ctx.emit("sync", (lambda it=it, m=m, f4=f4, pw=pw: lambda e: e.dma_start(
                out=fx[:, pw, :],
                in_=x[m * 128:(m + 1) * 128, f4 * 512:(f4 + 1) * 512]))(),
                waits=((DE, 3 * (it - 1)),) if it >= 2 else (),
                fire=ctx.fire(DL, 16))
            ctx.emit("sync", (lambda it=it, m=m, f4=f4, pw=pw: lambda e: e.dma_start(
                out=fa[:, pw, :],
                in_=attn0[m * 128:(m + 1) * 128, f4 * 512:(f4 + 1) * 512]))(),
                fire=ctx.fire(DL, 16))
            for k in range(32):
                wts = []
                if k == 0:
                    wts = [(DL, 48 * (it + 1)), (DR, 512 * (f4 + 1))]
                    if it >= 2:
                        wts.append((DE, 3 * (it - 1) + 1))
                ctx.emit("tensor", (lambda k=k, pw=pw, pf=pf, it=it: lambda e: e.matmul(
                    out=ps[:, it % 4, :], lhsT=w2v(pw)[:, k, :],
                    rhs=rqv(pf, k), start=(k == 0), stop=(k == 31)))(),
                    waits=tuple(wts),
                    fire=ctx.fire(DP, 1) if k == 31 else None)
            ctx.emit("vector", (lambda it=it, m=m, pw=pw: lambda e: e.tensor_scalar(
                out=fo[:, pw, :], in0=ps[:, it % 4, :], scalar1=b2c[:, m:m + 1],
                scalar2=RNE_C, op0=ALU.add, op1=ALU.add))(),
                waits=((DP, it + 1), (DL, 48 * (it + 1))), fire=ctx.fire(DE, 1))
            ctx.emit("vector", (lambda pw=pw: lambda e: e.tensor_scalar(
                out=fo[:, pw, :], in0=fo[:, pw, :], scalar1=RNE_C,
                scalar2=1.0 / 256.0, op0=ALU.subtract, op1=ALU.mult))(),
                waits=((DE, 3 * it + 1),), fire=ctx.fire(DE, 1))
            ctx.emit("vector", (lambda pw=pw: lambda e: e.tensor_add(
                out=fx[:, pw, :], in0=fx[:, pw, :], in1=fa[:, pw, :]))(),
                waits=((DE, 3 * it + 1),))
            ctx.emit("vector", (lambda pw=pw: lambda e: e.tensor_add(
                out=fo[:, pw, :], in0=fo[:, pw, :], in1=fx[:, pw, :]))(),
                waits=tuple([(DE, 3 * it + 2)] + ([(DS, 16 * (it - 1))] if it >= 2 else [])),
                fire=ctx.fire(DE, 1))
            ctx.emit("gpsimd", (lambda it=it, m=m, f4=f4, pw=pw: lambda e: e.dma_start(
                out=x1[m * 128:(m + 1) * 128, f4 * 512:(f4 + 1) * 512],
                in_=fo[:, pw, :]))(),
                waits=((DE, 3 * it + 3),), fire=ctx.fire(DS, 16))
    ctx.emit("vector", lambda e: e.engine_nop(),
             waits=((DS, 16 * 32),), fire=ctx.fire("q", 1))
